# version 09
# baseline (speedup 1.0000x reference)
import math
import numpy as np
import ml_dtypes
from contextlib import ExitStack

import concourse.bass as bass
import concourse.mybir as mybir
from concourse.bass_utils import run_bass_kernel_spmd

F32 = mybir.dt.float32
BF16 = mybir.dt.bfloat16
AF = mybir.ActivationFunctionType
ALU = mybir.AluOpType
AX = mybir.AxisListType.X

T = 2048
D = 1024
NCHUNK = 8
H0 = 8
DK0, DV0 = 128, 256
DFF = 4096
PI = math.pi
MAGIC = 12582912.0
ENGINES = ("sync", "gpsimd", "tensor", "scalar", "vector")
SEMS = ("dl", "dg", "dwa", "dwb", "dxa", "dxb", "dsa", "dsb", "st", "ss", "sv")
DMA_SEMS = {"dl", "dg", "dwa", "dwb", "dxa", "dxb", "dsa", "dsb"}

_CONSTS = None


def _host_constants():
    global _CONSTS
    if _CONSTS is not None:
        return _CONSTS
    half = D // 2
    c = np.arange(half, dtype=np.float64)
    scale_vec = (2 * c + 0.4 * D) / (1.4 * D)
    pos = np.arange(T, dtype=np.float64)
    scale = scale_vec[None, :] ** (pos[:, None] / 512.0)
    inv_freq = 1.0 / 10000.0 ** (c / half)
    ang = pos[:, None] * inv_freq[None, :]
    sin_q = np.repeat(np.sin(ang) * scale, 2, axis=-1).T
    cos_q = np.repeat(np.cos(ang) * scale, 2, axis=-1).T
    sin_k = np.repeat(np.sin(ang) / scale, 2, axis=-1).T
    cos_k = np.repeat(np.cos(ang) / scale, 2, axis=-1).T
    gamma = 1.0 - np.exp(np.linspace(np.log(1 / 32.0), np.log(1 / 512.0), H0))
    qfac = np.zeros((H0, T))
    kfac = np.zeros((H0, T))
    for h in range(H0):
        g = gamma[h]
        rnorm = np.sqrt((1.0 - g ** (pos + 1)) / (1.0 - g))
        qfac[h] = g ** pos / rnorm * DK0 ** -0.5
        kfac[h] = g ** (-pos)
    fh = np.arange(D) // DK0
    bf = ml_dtypes.bfloat16
    f3 = np.float32
    tab_aq = np.ascontiguousarray((cos_q * qfac[fh, :]).astype(f3))
    tab_bq = np.ascontiguousarray((sin_q * qfac[fh, :]).astype(f3))
    tab_ak = np.ascontiguousarray((cos_k * kfac[fh, :]).astype(f3))
    tab_bk = np.ascontiguousarray((sin_k * kfac[fh, :]).astype(f3))
    P = np.zeros((128, 128), dtype=np.float64)
    for i in range(64):
        P[2 * i, 2 * i + 1] = -1.0
        P[2 * i + 1, 2 * i] = 1.0
    permT = np.ascontiguousarray(P.T.astype(bf))
    permT_f32 = np.ascontiguousarray(P.T.astype(np.float32))
    tri = (np.arange(128)[None, :] >= np.arange(128)[:, None])
    trimask = np.ascontiguousarray(tri.astype(np.float32))
    selab = np.zeros((2, 128), dtype=np.float32)
    selab[0, :64] = 1.0
    selab[1, 64:] = 1.0
    sel16 = np.zeros((16, 2048), dtype=ml_dtypes.bfloat16)
    for j in range(16):
        sel16[j, j * 128:(j + 1) * 128] = 1.0
    perm = []
    for off in (0, 64, 128):
        for h in range(16):
            perm.extend(range(h * 192 + off, h * 192 + off + 64))
    perm = np.asarray(perm, dtype=np.int64)
    _CONSTS = dict(tab_aq=tab_aq, tab_bq=tab_bq, tab_ak=tab_ak, tab_bk=tab_bk,
                   permT=permT, permT_f32=permT_f32, trimask=trimask,
                   selab=selab, sel16=sel16, l1perm=perm)
    return _CONSTS


class Prog:

    def __init__(self):
        self.nc = bass.Bass("TRN2", target_bir_lowering=False)
        self.stack = ExitStack()
        self.sems = {n: self.stack.enter_context(self.nc.semaphore(n))
                     for n in SEMS}
        self.cnt = {n: 0 for n in SEMS}
        self.ops = None
        self.last_wait = None

    def begin(self):
        assert self.ops is None
        self.ops = {e: [] for e in ENGINES}
        self.last_wait = {e: {} for e in ENGINES}

    def end(self):
        ops, self.ops = self.ops, None
        nonempty = [e for e in ENGINES if ops[e]]
        if not nonempty:
            return
        with self.nc.Block() as b:
            for e in nonempty:
                def runner(eng, _lst=ops[e]):
                    for g in _lst:
                        g(eng)
                getattr(b, e)(runner)

    def op(self, engine, fn=None, wait=(), inc=None, inc_n=1):
        sems = self.sems
        waits = []
        for s, v in wait:
            if not v:
                continue
            assert v <= self.cnt[s], (s, v, self.cnt[s])
            if self.last_wait[engine].get(s, -1) >= v:
                continue
            self.last_wait[engine][s] = v
            waits.append((s, v))

        def g(eng):
            for s, v in waits:
                eng.wait_ge(sems[s], v)
            if fn is not None:
                instr = fn(eng)
                if inc is not None:
                    instr.then_inc(sems[inc], inc_n)
        self.ops[engine].append(g)
        if fn is not None and inc is not None:
            self.cnt[inc] += inc_n
        return self.cnt[inc] if inc is not None else None

    def now(self, sem):
        return self.cnt[sem]

    def vseq(self, fn, wait=()):
        w = list(wait) + [("sv", self.cnt["sv"])]
        return self.op("vector", fn, wait=w, inc="sv")

    def sseq(self, fn, wait=()):
        w = list(wait) + [("ss", self.cnt["ss"])]
        return self.op("scalar", fn, wait=w, inc="ss")

    def at(self, sem):
        return (sem, self.cnt[sem])

    def load(self, dst, src, wait=(), sem="dl", queue=None):
        q = queue or ("gpsimd" if sem in ("dg",) else "sync")
        return self.op(q, lambda e: e.dma_start(dst, src), wait=wait,
                       inc=sem, inc_n=16)

    def loadc(self, dst, src, wait=(), sem="dg"):
        return self.op("gpsimd", lambda e: e.dma_start(dst, src), wait=wait,
                       inc=sem, inc_n=16)

    def mm(self, dst, lhsT, rhs, start, stop, wait=(), inc=None):
        return self.op("tensor",
                       lambda t: t.matmul(dst, lhsT, rhs, start=start, stop=stop),
                       wait=wait, inc=inc)


def col_block_ap(w, c0, c1):
    return w.ap()[:, c0:c1].rearrange("(kc p) m -> p kc m", p=128)


def sb3(t, kchunks):
    return t[:].rearrange("p (kc m) -> p kc m", kc=kchunks)


def vec_pg(v):
    return v.ap().rearrange("(g p) -> p g", p=128)


def vec_pm1(v):
    return v.ap().rearrange("(m p) -> p m one", p=128, one=1)


def row1(v, a, b):
    return v.ap()[a:b].rearrange("(one n) -> one n", one=1)


def colv(v, a, b):
    return v.ap()[a:b].rearrange("(p one) -> p one", one=1)


def build_nc(stage="full"):
    C = _host_constants()
    P = Prog()
    nc = P.nc
    st = P.stack

    def din(name, shape, dtype=F32):
        return nc.dram_tensor(name, shape, dtype, kind="ExternalInput")

    x_in = din("x_fmaj", [D, T])
    wqkv0 = din("L0_attn_w_qkv", [D, 4 * D])
    bqkv0 = din("L0_attn_b_qkv", [4 * D])
    wg0 = din("L0_attn_w_g", [D, 2 * D])
    bg0 = din("L0_attn_b_g", [2 * D])
    wp0 = din("L0_attn_w_p", [2 * D, D])
    bp0 = din("L0_attn_b_p", [D])
    w1 = din("L0_ffn_W1", [D, DFF])
    b1 = din("L0_ffn_b1", [DFF])
    qph = din("L0_ffn_quantum_phases", [DFF])
    qwr = din("L0_ffn_quantum_weights_real", [DFF])
    qwi = din("L0_ffn_quantum_weights_imag", [DFF])
    w2 = din("L0_ffn_W2", [DFF, D])
    b2 = din("L0_ffn_b2", [D])
    wqkv1 = din("l1_wqkv_perm", [D, 3 * D])
    bqkv1 = din("l1_bqkv_perm", [3 * D])
    wout1 = din("L1_attn_Wout", [D, D])
    bout1 = din("L1_attn_bout", [D])
    cnw = [din(f"L1_ffn_cn_w{i}", s) for i, s in
           zip(range(1, 5), ([D, D], [D, D // 2], [D // 2, D // 4], [D // 4, 1]))]
    cnb = [din(f"L1_ffn_cn_b{i}", [s]) for i, s in
           zip(range(1, 5), (D, D // 2, D // 4, 1))]
    gcw = [din(f"L1_ffn_gc_w{i}", s) for i, s in
           zip(range(1, 4), ([D, D // 2], [D // 2, D // 4], [D // 4, 1]))]
    gcb = [din(f"L1_ffn_gc_b{i}", [s]) for i, s in
           zip(range(1, 4), (D // 2, D // 4, 1))]
    arange_in = din("L1_ffn_adaptive_range", [1])
    bqkv0_pg = din("b_qkv0_pg", [128, 32])
    bg0_pg = din("bg0_pg", [128, 16])
    bp0_pg = din("bp0_pg", [128, 8])
    b1_pg = din("b1_pg", [128, 32])
    b2_pg = din("b2_pg", [128, 8])
    qph_pg = din("qph_pg", [128, 32])
    qwr_pg = din("qwr_pg", [128, 32])
    qwi_pg = din("qwi_pg", [128, 32])
    bqkv1_pg = din("b_qkv1_pg", [128, 24])
    bout1_pg = din("bout1_pg", [128, 8])
    cnb_pg = [din(f"cnb{i}_pg", [128, g]) for i, g in ((1, 8), (2, 4), (3, 2))]
    cnw4_pg = din("cnw4_pg", [128, 2])
    gcw3_pg = din("gcw3_pg", [128, 2])
    gcb_pg = [din(f"gcb{i}_pg", [128, g]) for i, g in ((1, 4), (2, 2))]
    tab_d = {n: din(n, [D, T], F32)
             for n in ("tab_aq", "tab_bq", "tab_ak", "tab_bk")}
    permT_d = din("permT", [128, 128], BF16)
    permTf_d = din("permT_f32", [128, 128], F32)
    trimask_d = din("trimask", [128, 128], F32)
    selab_d = din("selab", [2, 128], F32)
    sel16_d = din("sel16", [16, 2048], BF16)

    out_d = nc.dram_tensor("out_fmaj", [D, T], F32, kind="ExternalOutput")
    x1_dram = nc.dram_tensor("x1_scratch", [D, T], F32)

    dbg_d = None
    if stage != "full":
        dbg_d = nc.dram_tensor("dbg", [128, 16 * T], F32, kind="ExternalOutput")

    def dump(buf, cols, wait):
        if P.ops is not None:
            P.end()
        P.begin()
        nparts = buf.shape[0]
        dw = P.op("gpsimd",
                  lambda e: e.dma_start(dbg_d.ap()[0:nparts, 0:cols],
                                        buf[:, 0:cols]),
                  wait=wait, inc="dg", inc_n=16)
        P.op("gpsimd", wait=[("dg", dw)])
        P.end()
        return nc

    xn_bf = st.enter_context(nc.sbuf_tensor("xn_bf", [128, NCHUNK * T], BF16))
    ones_bf = st.enter_context(nc.sbuf_tensor("ones_bf", [128, 1], BF16))
    ones_row_bf = st.enter_context(nc.sbuf_tensor("ones_row_bf", [1, 128], BF16))
    ones_row_f = st.enter_context(nc.sbuf_tensor("ones_row_f", [1, 128], F32))
    permT_sb = st.enter_context(nc.sbuf_tensor("permT_sb", [128, 128], BF16))
    permTf_sb = st.enter_context(nc.sbuf_tensor("permTf_sb", [128, 128], F32))
    xn0f = st.enter_context(nc.sbuf_tensor("xn0f", [128, 8], F32))
    trimask_sb = st.enter_context(nc.sbuf_tensor("trimask_sb", [128, 128], F32))
    selab_sb = st.enter_context(nc.sbuf_tensor("selab_sb", [2, 128], F32))
    bqkv0_sb = st.enter_context(nc.sbuf_tensor("bqkv0_sb", [128, 32], F32))
    neg10_sb = st.enter_context(nc.sbuf_tensor("neg10_sb", [128, 1], F32))
    bqkv1_sb = st.enter_context(nc.sbuf_tensor("bqkv1_sb", [128, 24], F32))

    with ExitStack() as sa:
        x_ph = sa.enter_context(nc.sbuf_tensor("x_ph", [128, NCHUNK * T], F32))
        sq_scr = sa.enter_context(nc.sbuf_tensor("sq_scr", [128, 2 * T], BF16))
        stv = sa.enter_context(nc.sbuf_tensor("stv", [1, 6 * T], F32))
        t_scr = sa.enter_context(nc.sbuf_tensor("t_scrA", [128, T], F32))

        P.begin()
        P.load(permT_sb[:], permT_d.ap())
        P.load(permTf_sb[:], permTf_d.ap())
        P.load(trimask_sb[:], trimask_d.ap())
        P.load(selab_sb[:], selab_d.ap())
        P.load(bqkv0_sb[:], bqkv0_pg.ap())
        P.load(bqkv1_sb[:], bqkv1_pg.ap())
        P.op("vector", lambda v: v.memset(ones_bf[:], 1.0))
        P.op("vector", lambda v: v.memset(ones_row_bf[:], 1.0))
        P.op("vector", lambda v: v.memset(ones_row_f[:], 1.0))
        P.op("vector", lambda v: v.memset(neg10_sb[:], -10.0))
        for c in range(NCHUNK):
            P.load(x_ph[:, c * T:(c + 1) * T], x_in.ap()[c * 128:(c + 1) * 128, :])
        xall = P.now("dl")
        with ExitStack() as sp:
            ps_sum = sp.enter_context(nc.psum_tensor("ps_sum", [1, T], F32))
            ps_sq = sp.enter_context(nc.psum_tensor("ps_sq", [1, T], F32))
            stmm = [0] * NCHUNK
            for c in range(NCHUNK):
                xc = xn_bf[:, c * T:(c + 1) * T]
                sqc = sq_scr[:, (c % 2) * T:(c % 2 + 1) * T]
                cv = P.op("vector",
                          lambda v, d=xc, s=x_ph[:, c * T:(c + 1) * T]:
                          v.tensor_copy(d, s),
                          wait=[("dl", xall)], inc="sv")
                w = [("sv", cv)]
                if c >= 2:
                    w.append(("st", stmm[c - 2]))
                cs = P.op("scalar",
                          lambda s, d=sqc, i=xc: s.activation(d, i, AF.Square),
                          wait=w, inc="ss")
                for qc in range(4):
                    P.mm(ps_sum[0:1, qc * 512:(qc + 1) * 512], ones_bf[:],
                         xc[:, qc * 512:(qc + 1) * 512],
                         start=(c == 0), stop=(c == NCHUNK - 1),
                         wait=[("sv", cv)])
                for qc in range(4):
                    stmm[c] = P.mm(ps_sq[0:1, qc * 512:(qc + 1) * 512], ones_bf[:],
                                   sqc[:, qc * 512:(qc + 1) * 512],
                                   start=(c == 0), stop=(c == NCHUNK - 1),
                                   wait=[("ss", cs)], inc="st")
            P.op("vector", lambda v: v.tensor_copy(stv[0:1, 0:T], ps_sum[:]),
                 wait=[("st", P.now("st"))])
            P.op("vector", lambda v: v.tensor_copy(stv[0:1, T:2 * T], ps_sq[:]),
                 inc="sv")
        mu = stv[0:1, 2 * T:3 * T]
        varp = stv[0:1, 3 * T:4 * T]
        aa = stv[0:1, 4 * T:5 * T]
        bb = stv[0:1, 5 * T:6 * T]
        P.op("vector", lambda v: v.tensor_scalar(mu, stv[0:1, 0:T], 1.0 / D, None,
                                                 ALU.mult))
        P.op("vector", lambda v: v.tensor_scalar(varp, stv[0:1, T:2 * T], 1.0 / D,
                                                 1e-5, ALU.mult, ALU.add))
        P.op("vector", lambda v: v.scalar_tensor_tensor(t_scr[0:1, 0:T], mu, -1.0,
                                                        mu, ALU.mult, ALU.mult))
        vst = P.op("vector", lambda v: v.tensor_add(varp, varp, t_scr[0:1, 0:T]),
                   inc="sv")
        P.op("scalar", lambda s: s.activation(varp, varp, AF.Ln),
             wait=[("sv", vst)])
        sst = P.op("scalar", lambda s: s.activation(aa, varp, AF.Exp, scale=-0.5),
                   inc="ss")
        vb = P.op("vector", lambda v: v.scalar_tensor_tensor(bb, mu, -1.0, aa,
                                                             ALU.mult, ALU.mult),
                  wait=[("ss", sst)], inc="sv")
        with ExitStack() as sp:
            ps_a = sp.enter_context(nc.psum_tensor("ps_a", [128, T], F32))
            ps_b = sp.enter_context(nc.psum_tensor("ps_b", [128, T], F32))
            bcw = 0
            for qc in range(4):
                sl = slice(qc * 512, (qc + 1) * 512)
                P.mm(ps_a[:, sl], ones_row_f[:], aa[:, sl], True, True,
                     wait=[("sv", vb)])
                bcw = P.mm(ps_b[:, sl], ones_row_f[:], bb[:, sl], True, True,
                           inc="st")
            for c in range(NCHUNK):
                P.op("vector",
                     lambda v, a=x_ph[:, c * T:(c + 1) * T]:
                     v.tensor_mul(t_scr[:], a, ps_a[:]),
                     wait=[("st", bcw)])
                P.op("vector",
                     lambda v, d=xn_bf[:, c * T:(c + 1) * T]:
                     v.tensor_add(d, t_scr[:], ps_b[:]),
                     inc="sv")
                P.vseq(lambda v, c_=c: v.tensor_add(xn0f[:, c_:c_ + 1],
                                                    t_scr[:, 0:1],
                                                    ps_b[:, 0:1]))
        P.end()

    if stage == "xn0":
        return dump(xn_bf[:], 8 * T, [("sv", P.now("sv"))])

    retg = None
    with ExitStack() as sl0:
        retg = sl0.enter_context(nc.sbuf_tensor("retg", [128, 16 * T], BF16))
        gnab = sl0.enter_context(nc.sbuf_tensor("gnab", [16, T], BF16))
        sl0h = ExitStack()
        wq_sb = sl0h.enter_context(nc.sbuf_tensor("wq_sb", [128, 8 * 128], BF16))
        wk_sb = sl0h.enter_context(nc.sbuf_tensor("wk_sb", [128, 8 * 128], BF16))
        wv_sb = sl0h.enter_context(nc.sbuf_tensor("wv_sb", [128, 8 * 256], BF16))
        bv_sb = sl0h.enter_context(nc.sbuf_tensor("bv_sb", [1, 256], BF16))
        tabA = sl0h.enter_context(nc.sbuf_tensor("tabA", [128, T], F32))
        tabB = sl0h.enter_context(nc.sbuf_tensor("tabB", [128, T], F32))
        qk_bf = sl0h.enter_context(nc.sbuf_tensor("qk_bf", [128, T], BF16))
        qt_sb = sl0h.enter_context(nc.sbuf_tensor("qt_sb", [128, T], BF16))
        kt_sb = sl0h.enter_context(nc.sbuf_tensor("kt_sb", [128, T], BF16))
        qtf = sl0h.enter_context(nc.sbuf_tensor("qtf", [128, T], F32))
        ktf = sl0h.enter_context(nc.sbuf_tensor("ktf", [128, T], F32))
        wqf_sb = sl0h.enter_context(nc.sbuf_tensor("wqf_sb", [128, 8 * 128], F32))
        q0s = sl0h.enter_context(nc.sbuf_tensor("q0s", [128, 4], F32))
        t1s = sl0h.enter_context(nc.sbuf_tensor("t1s", [128, T], F32))
        v_sb = sl0h.enter_context(nc.sbuf_tensor("v_sb", [128, 16 * 256], BF16))
        pnow = [sl0h.enter_context(nc.sbuf_tensor(f"pnow{i}", [128, 1024], BF16))
                for i in range(2)]
        pstore = sl0h.enter_context(nc.sbuf_tensor("pstore", [128, 8 * 1024], BF16))
        sq2 = sl0h.enter_context(nc.sbuf_tensor("sq2", [128, 1024], BF16))
        gn_scr = sl0h.enter_context(nc.sbuf_tensor("gn_scr", [1, 4 * 512], F32))

        ps_gn = sl0.enter_context(nc.psum_tensor("ps_gn_l0", [1, 512], F32))
        P.begin()
        st_state = {}
        gevp = 0
        gn_free = 0
        dma_free = 0

        stats_done_st = 0
        stats_queue = []

        def flush_stats(n=1):
            for _ in range(min(n, len(stats_queue))):
                stats_queue.pop(0)()

        def queue_gnstats(hh, gevs):
            for qh_ in range(2):
                for qc_ in range(2):
                    stats_queue.append(
                        lambda hh_=hh, g_=tuple(gevs), a_=qh_, b_=qc_:
                        emit_gnstat_unit(hh_, g_, a_, b_))

        def emit_gnstat_unit(hh, gevs, qh, qc):
            nonlocal gn_free, dma_free, stats_done_st
            vcb = 2 * hh
            if True:
                q0h = qh * 1024
                if True:
                    gmu = gn_scr[0:1, 0:512]
                    gva = gn_scr[0:1, 512:1024]
                    gtmp = gn_scr[0:1, 1024:1536]
                    gb_ = gn_scr[0:1, 1536:2048]
                    gmm = 0
                    sqw = 0
                    for dvc in range(2):
                        rsl = retg[:, (vcb + dvc) * T + q0h + qc * 512:
                                   (vcb + dvc) * T + q0h + (qc + 1) * 512]
                        wst = [("sv", gevs[qh])]
                        if gn_free:
                            wst.append(("sv", gn_free))
                        gmm = P.mm(ps_gn[0:1, 0:512], ones_bf[:], rsl,
                                   start=(dvc == 0), stop=(dvc == 1), wait=wst,
                                   inc="st" if dvc == 1 else None)
                        wsq = [("sv", gevs[qh])]
                        if stats_done_st:
                            wsq.append(("st", stats_done_st))
                        sqw = P.op("scalar",
                                   lambda s,
                                   d=sq2[:, dvc * 512:(dvc + 1) * 512],
                                   i=rsl: s.activation(d, i, AF.Square),
                                   wait=wsq, inc="ss")
                    dw = [("st", gmm)]
                    if dma_free:
                        dw.append(("dg", dma_free))
                    gmur = P.vseq(lambda v: v.tensor_scalar(gmu,
                                                            ps_gn[0:1, 0:512],
                                                            1.0 / DV0, None,
                                                            ALU.mult), wait=dw)
                    for dvc in range(2):
                        gmm = P.mm(ps_gn[0:1, 0:512], ones_bf[:],
                                   sq2[:, dvc * 512:(dvc + 1) * 512],
                                   start=(dvc == 0), stop=(dvc == 1),
                                   wait=[("ss", sqw), ("sv", gmur)],
                                   inc="st" if dvc == 1 else None)
                    stats_done_st = gmm
                    P.vseq(lambda v: v.tensor_scalar(gva, ps_gn[0:1, 0:512],
                                                     1.0 / DV0, 1e-5, ALU.mult,
                                                     ALU.add),
                           wait=[("st", gmm)])
                    P.vseq(lambda v: v.scalar_tensor_tensor(gtmp, gmu, -1.0,
                                                            gmu, ALU.mult,
                                                            ALU.mult))
                    gn_free = P.vseq(lambda v: v.tensor_add(gva, gva, gtmp))
                    P.sseq(lambda s: s.activation(gva, gva, AF.Ln),
                           wait=[("sv", gn_free)])
                    gns = P.sseq(lambda s: s.activation(gva, gva, AF.Exp,
                                                        scale=-0.5))
                    gnb = P.vseq(lambda v: v.scalar_tensor_tensor(
                        gb_, gmu, -1.0, gva, ALU.mult, ALU.mult),
                        wait=[("ss", gns)])
                    col = q0h + qc * 512
                    P.loadc(gnab[hh:hh + 1, col:col + 512], gva,
                            wait=[("sv", gnb)])
                    dma_free = P.loadc(gnab[8 + hh:9 + hh, col:col + 512], gb_)

        for h in range(H0):
            S = st_state
            P.load(tabA[:], tab_d["tab_aq"].ap()[h * 128:(h + 1) * 128, :],
                   wait=[("sv", S.get("vk0", 0))])
            P.load(tabB[:], tab_d["tab_bq"].ap()[h * 128:(h + 1) * 128, :])
            P.load(sb3(wqf_sb, 8), col_block_ap(wqkv0, h * 128, (h + 1) * 128),
                   wait=[("st", S.get("t0k_mm", 0))])
            ldl = P.now("dl")
            P.loadc(sb3(wq_sb, 8), col_block_ap(wqkv0, h * 128, (h + 1) * 128),
                    wait=[("st", S.get("qkv_end", 0))])
            P.loadc(sb3(wk_sb, 8),
                    col_block_ap(wqkv0, D + h * 128, D + (h + 1) * 128))
            P.loadc(sb3(wv_sb, 8),
                    col_block_ap(wqkv0, 2 * D + h * 256, 2 * D + (h + 1) * 256))
            P.loadc(bv_sb[:], row1(bqkv0, 2 * D + h * 256, 2 * D + (h + 1) * 256))
            ldg = P.now("dg")

            with ExitStack() as sp:
                ps_qk = sp.enter_context(nc.psum_tensor(f"ps_qk{h}", [128, T],
                                                        F32))
                ps_v = [sp.enter_context(nc.psum_tensor(f"ps_v{h}_{i}",
                                                        [128, 256], F32))
                        for i in range(2)]
                ps_t0 = sp.enter_context(nc.psum_tensor(f"ps_t0{h}", [128, 4],
                                                        F32))
                vev = [0, 0]

                def vtile(tt):
                    par = tt % 2
                    w = [("dg", ldg)]
                    if vev[par]:
                        w.append(("sv", vev[par]))
                    for kc in range(8):
                        P.mm(ps_v[par][:],
                             xn_bf[:, kc * T + tt * 128:kc * T + (tt + 1) * 128],
                             wv_sb[:, kc * 256:(kc + 1) * 256],
                             start=(kc == 0), stop=False, wait=w)
                    vmm = P.mm(ps_v[par][:], ones_row_bf[:], bv_sb[:],
                               start=False, stop=True, inc="st")
                    vev[par] = P.op(
                        "vector",
                        lambda v, d=v_sb[:, tt * 256:(tt + 1) * 256],
                        s=ps_v[par]: v.tensor_copy(d, s[:]),
                        wait=[("st", vmm)], inc="sv")

                qmm = 0
                for qn in range(4):
                    for kc in range(8):
                        qmm = P.mm(ps_qk[:, qn * 512:(qn + 1) * 512],
                                   wq_sb[:, kc * 128:(kc + 1) * 128],
                                   xn_bf[:, kc * T + qn * 512:kc * T + (qn + 1) * 512],
                                   start=(kc == 0), stop=(kc == 7),
                                   wait=[("dg", ldg), ("sv", gevp)],
                                   inc="st" if kc == 7 else None)
                flush_stats()
                eq = P.op("scalar",
                          lambda s, b=bqkv0_sb[:, h:h + 1], p=ps_qk:
                          s.activation(t1s[:], p[:], AF.Identity, bias=b),
                          wait=[("st", qmm), ("dl", ldl)], inc="ss")
                eqb = P.op("vector", lambda v: v.tensor_copy(qk_bf[:], t1s[:]),
                           wait=[("ss", eq)], inc="sv")
                vtile(0)
                vtile(1)
                rmm = 0
                for qn in range(4):
                    rmm = P.mm(ps_qk[:, qn * 512:(qn + 1) * 512], permT_sb[:],
                               qk_bf[:, qn * 512:(qn + 1) * 512], True, True,
                               wait=[("sv", eqb), ("ss", eq)],
                               inc="st" if qn == 3 else None)
                P.op("vector", lambda v: v.tensor_mul(t1s[:], t1s[:], tabA[:]),
                     wait=[("ss", eq), ("dl", ldl), ("sv", eqb)])
                P.op("vector",
                     lambda v, p=ps_qk: v.scalar_tensor_tensor(
                         qtf[:], p[:], 1.0, tabB[:], ALU.mult, ALU.mult),
                     wait=[("st", rmm)])
                P.op("vector", lambda v: v.tensor_add(qtf[:], t1s[:], qtf[:]))
                vq = P.op("vector", lambda v: v.tensor_copy(qt_sb[:], qtf[:]),
                          inc="sv")
                vtile(2)
                vtile(3)
                flush_stats()
                t0m = 0
                for kc in range(8):
                    t0m = P.mm(ps_t0[:, 0:1], wqf_sb[:, kc * 128:(kc + 1) * 128],
                               xn0f[:, kc:kc + 1], start=(kc == 0),
                               stop=(kc == 7), wait=[("dl", ldl)],
                               inc="st" if kc == 7 else None)
                t0e = P.sseq(lambda s, b=bqkv0_sb[:, h:h + 1], p=ps_t0:
                             s.activation(q0s[:, 0:1], p[:, 0:1],
                                          AF.Identity, bias=b),
                             wait=[("st", t0m)])
                t0r = P.mm(ps_t0[:, 1:2], permTf_sb[:], q0s[:, 0:1], True,
                           True, wait=[("ss", t0e)], inc="st")
                P.vseq(lambda v: v.tensor_mul(q0s[:, 1:2], q0s[:, 0:1],
                                              tabA[:, 0:1]),
                       wait=[("sv", vq), ("ss", t0e)])
                P.vseq(lambda v, p=ps_t0: v.scalar_tensor_tensor(
                    qtf[:, 0:1], p[:, 1:2], 1.0, tabB[:, 0:1], ALU.mult,
                    ALU.mult), wait=[("st", t0r)])
                vq0 = P.vseq(lambda v: v.tensor_add(qtf[:, 0:1], qtf[:, 0:1],
                                                    q0s[:, 1:2]))
                kmm = 0
                for qn in range(4):
                    for kc in range(8):
                        kmm = P.mm(ps_qk[:, qn * 512:(qn + 1) * 512],
                                   wk_sb[:, kc * 128:(kc + 1) * 128],
                                   xn_bf[:, kc * T + qn * 512:kc * T + (qn + 1) * 512],
                                   start=(kc == 0), stop=(kc == 7),
                                   wait=[("sv", vq)],
                                   inc="st" if kc == 7 else None)
                P.load(tabA[:], tab_d["tab_ak"].ap()[h * 128:(h + 1) * 128, :],
                       wait=[("sv", vq0)])
                P.load(tabB[:], tab_d["tab_bk"].ap()[h * 128:(h + 1) * 128, :])
                P.load(sb3(wqf_sb, 8),
                       col_block_ap(wqkv0, D + h * 128, D + (h + 1) * 128),
                       wait=[("ss", t0e)])
                ldk = P.now("dl")
                flush_stats()
                ek = P.op("scalar",
                          lambda s, b=bqkv0_sb[:, 8 + h:9 + h], p=ps_qk:
                          s.activation(t1s[:], p[:], AF.Identity, bias=b),
                          wait=[("st", kmm)], inc="ss")
                ekb = P.op("vector", lambda v: v.tensor_copy(qk_bf[:], t1s[:]),
                           wait=[("ss", ek)], inc="sv")
                vtile(4)
                vtile(5)
                rkm = 0
                for qn in range(4):
                    rkm = P.mm(ps_qk[:, qn * 512:(qn + 1) * 512], permT_sb[:],
                               qk_bf[:, qn * 512:(qn + 1) * 512], True, True,
                               wait=[("sv", ekb), ("ss", ek)],
                               inc="st" if qn == 3 else None)
                P.op("vector", lambda v: v.tensor_mul(t1s[:], t1s[:], tabA[:]),
                     wait=[("ss", ek), ("dl", ldk), ("sv", ekb)])
                P.op("vector",
                     lambda v, p=ps_qk: v.scalar_tensor_tensor(
                         ktf[:], p[:], 1.0, tabB[:], ALU.mult, ALU.mult),
                     wait=[("st", rkm)])
                P.op("vector", lambda v: v.tensor_add(ktf[:], t1s[:], ktf[:]))
                vk = P.op("vector", lambda v: v.tensor_copy(kt_sb[:], ktf[:]),
                          inc="sv")
                for tt in range(6, 12):
                    vtile(tt)
                flush_stats()
                t0m = 0
                for kc in range(8):
                    t0m = P.mm(ps_t0[:, 2:3], wqf_sb[:, kc * 128:(kc + 1) * 128],
                               xn0f[:, kc:kc + 1], start=(kc == 0),
                               stop=(kc == 7), wait=[("dl", ldk)],
                               inc="st" if kc == 7 else None)
                st_state = {"t0k_mm": t0m}
                t0e = P.sseq(lambda s, b=bqkv0_sb[:, 8 + h:9 + h], p=ps_t0:
                             s.activation(q0s[:, 2:3], p[:, 2:3],
                                          AF.Identity, bias=b),
                             wait=[("st", t0m)])
                t0r = P.mm(ps_t0[:, 3:4], permTf_sb[:], q0s[:, 2:3], True,
                           True, wait=[("ss", t0e)], inc="st")
                P.vseq(lambda v: v.tensor_mul(q0s[:, 3:4], q0s[:, 2:3],
                                              tabA[:, 0:1]),
                       wait=[("sv", vk), ("ss", t0e)])
                P.vseq(lambda v, p=ps_t0: v.scalar_tensor_tensor(
                    ktf[:, 0:1], p[:, 3:4], 1.0, tabB[:, 0:1], ALU.mult,
                    ALU.mult), wait=[("st", t0r)])
                vk0 = P.vseq(lambda v: v.tensor_add(ktf[:, 0:1], ktf[:, 0:1],
                                                    q0s[:, 3:4]))
                for tt in range(12, 16):
                    vtile(tt)
                st_state["vk0"] = vk0
                st_state["qkv_end"] = P.now("st")

            with ExitStack() as sp:
                ps_s = [sp.enter_context(nc.psum_tensor(f"ps_s{h}_{i}",
                                                        [128, 512], F32))
                        for i in range(2)]
                ps_av = sp.enter_context(nc.psum_tensor(f"ps_av{h}",
                                                        [128, 2048], F32))
                vready = P.now("sv")
                evv = [0, 0]
                evs = [0, 0]
                avdone = [0, 0]
                avprev = 0
                gevs = []
                for qh in range(2):
                    q0h = qh * 1024
                    lastk = 7 if qh == 0 else 15
                    for tk in range(lastk + 1):
                        par = tk % 2
                        k0 = tk * 128
                        if qh == 0 or tk >= 8:
                            spans = [(k0, k0 + 128, True)]
                            q = k0 + 128
                            while q < T:
                                qe = min(q + 512, T)
                                spans.append((q, qe, False))
                                q = qe
                            for (qa, qb2, isdiag) in spans:
                                w = [("sv", vready)]
                                if evv[par]:
                                    w.append(("sv", evv[par]))
                                if evs[par]:
                                    w.append(("ss", evs[par]))
                                if isdiag:
                                    if tk == 0:
                                        w.append(("sv", vk0))
                                    smm = P.mm(ps_s[par][:, 0:qb2 - qa],
                                               ktf[:, k0:k0 + 128],
                                               qtf[:, qa:qb2], True, True,
                                               wait=w, inc="st")
                                else:
                                    smm = P.mm(ps_s[par][:, 0:qb2 - qa],
                                               kt_sb[:, k0:k0 + 128],
                                               qt_sb[:, qa:qb2], True, True,
                                               wait=w, inc="st")
                                dsts = []
                                if qh == 0:
                                    if qa < 1024:
                                        bb2 = min(qb2, 1024)
                                        dsts.append((pnow[par][:, qa:bb2], qa,
                                                     bb2, True))
                                    if qb2 > 1024:
                                        aa2 = max(qa, 1024)
                                        dsts.append(
                                            (pstore[:, tk * 1024 + aa2 - 1024:
                                                     tk * 1024 + qb2 - 1024],
                                             aa2, qb2, False))
                                else:
                                    dsts.append(
                                        (pnow[par][:, qa - 1024:qb2 - 1024],
                                         qa, qb2, True))
                                for dst, a2, b2_, ispn in dsts:
                                    wv2 = [("st", smm)]
                                    if ispn and avdone[par]:
                                        wv2.append(("st", avdone[par]))
                                    if isdiag:
                                        evv[par] = P.op(
                                            "vector",
                                            lambda v, d=dst,
                                            s=ps_s[par][:, a2 - qa:b2_ - qa]:
                                            v.tensor_mul(d, s, trimask_sb[:]),
                                            wait=wv2, inc="sv")
                                    else:
                                        evs[par] = P.op(
                                            "scalar",
                                            lambda s_, d=dst,
                                            s=ps_s[par][:, a2 - qa:b2_ - qa]:
                                            s_.activation(d, s, AF.Identity),
                                            wait=wv2, inc="ss")
                        qa = max(k0, q0h)
                        qb2 = q0h + 1024
                        first = (tk == 0)
                        if qh == 0 or tk >= 8:
                            srcbuf, off = pnow[par], q0h
                        else:
                            srcbuf, off = pstore, 1024 - tk * 1024
                        w = [("sv", evv[par]), ("ss", evs[par])]
                        if first and gevp:
                            w.append(("sv", gevp))
                        q = qa
                        while q < qb2:
                            if q % 512:
                                qe = min(qb2, (q // 512 + 1) * 512)
                            else:
                                qe = min(q + 512, qb2)
                            for dvc in range(2):
                                avprev = P.mm(
                                    ps_av[:, dvc * 1024 + q - q0h:
                                          dvc * 1024 + qe - q0h],
                                    v_sb[:, tk * 256 + dvc * 128:
                                         tk * 256 + (dvc + 1) * 128],
                                    srcbuf[:, q - off:qe - off],
                                    start=first, stop=(tk == lastk), wait=w,
                                    inc="st")
                            q = qe
                        avdone[par] = avprev
                    vcb = 2 * h
                    gev = 0
                    for dvc in range(2):
                        gev = P.op(
                            "vector",
                            lambda v,
                            d=retg[:, (vcb + dvc) * T + q0h:
                                   (vcb + dvc) * T + q0h + 1024],
                            s=ps_av[:, dvc * 1024:(dvc + 1) * 1024]:
                            v.tensor_copy(d, s),
                            wait=[("st", avprev)], inc="sv")
                    gevp = gev
                    gevs.append(gev)
                queue_gnstats(h, gevs)
        flush_stats(99)
        P.end()
        sl0h.close()

        if stage == "ret0":
            return dump(retg[:], 16 * T, [("sv", P.now("sv"))])

        with ExitStack() as s2:
            wgf_sb = s2.enter_context(nc.sbuf_tensor("wgf_sb", [128, 8 * 2048],
                                                     BF16))
            bg_sb = s2.enter_context(nc.sbuf_tensor("bg_sb", [128, 16], F32))
            sel16_sb = s2.enter_context(nc.sbuf_tensor("sel16_sb", [16, 2048],
                                                       BF16))
            gate_scr = [s2.enter_context(nc.sbuf_tensor(f"gate{i}", [128, 512],
                                                        F32)) for i in range(2)]
            tp_scr = [s2.enter_context(nc.sbuf_tensor(f"tp{i}", [128, 512],
                                                      F32)) for i in range(2)]
            P.begin()
            P.loadc(sb3(wgf_sb, 8),
                    wg0.ap().rearrange("(kc p) m -> p kc m", p=128))
            gw = P.now("dg")
            P.load(bg_sb[:], bg0_pg.ap())
            P.load(sel16_sb[:], sel16_d.ap())
            bgw = P.now("dl")
            gnabw = P.now("dg")
            with ExitStack() as sp:
                ps_g = [sp.enter_context(nc.psum_tensor(f"ps_g{i}", [128, 512],
                                                        F32)) for i in range(2)]
                ps_bca = [sp.enter_context(nc.psum_tensor(f"ps_bca{i}",
                                                          [128, 512], F32))
                          for i in range(2)]
                ps_bcb = [sp.enter_context(nc.psum_tensor(f"ps_bcb{i}",
                                                          [128, 512], F32))
                          for i in range(2)]
                gev = [0, 0]
                i = 0
                for h in range(H0):
                    for dvc in range(2):
                        vc = 2 * h + dvc
                        for qn in range(4):
                            par = i % 2
                            i += 1
                            w = [("dg", gw), ("dg", gnabw), ("dl", bgw)]
                            if gev[par]:
                                w.append(("sv", gev[par]))
                            bca = P.mm(ps_bca[par][:],
                                       sel16_sb[:, h * 128:(h + 1) * 128],
                                       gnab[:, qn * 512:(qn + 1) * 512],
                                       True, True, wait=w, inc="st")
                            bcb = P.mm(ps_bcb[par][:],
                                       sel16_sb[:, (8 + h) * 128:(9 + h) * 128],
                                       gnab[:, qn * 512:(qn + 1) * 512],
                                       True, True, inc="st")
                            gm = 0
                            for kc in range(8):
                                gm = P.mm(ps_g[par][:],
                                          wgf_sb[:, kc * 2048 + vc * 128:
                                                 kc * 2048 + (vc + 1) * 128],
                                          xn_bf[:, kc * T + qn * 512:
                                                kc * T + (qn + 1) * 512],
                                          start=(kc == 0), stop=(kc == 7),
                                          inc="st" if kc == 7 else None)
                            sev = P.op("scalar",
                                       lambda s, d=gate_scr[par], p=ps_g[par],
                                       v_=vc: s.activation(
                                           d[:], p[:], AF.Silu,
                                           bias=bg_sb[:, v_:v_ + 1]),
                                       wait=[("st", gm)], inc="ss")
                            rsl = retg[:, vc * T + qn * 512:vc * T + (qn + 1) * 512]
                            P.op("vector",
                                 lambda v, r=rsl, p=par: v.tensor_mul(
                                     tp_scr[p][:], r, ps_bca[p][:]),
                                 wait=[("st", bcb)])
                            P.op("vector",
                                 lambda v, p=par: v.tensor_add(
                                     tp_scr[p][:], tp_scr[p][:], ps_bcb[p][:]))
                            gev[par] = P.op("vector",
                                            lambda v, r=rsl, p=par,
                                            g=gate_scr[par]:
                                            v.tensor_mul(r, tp_scr[p][:], g[:]),
                                            wait=[("ss", sev)], inc="sv")
            P.end()

        if stage == "gated":
            return dump(retg[:], 16 * T, [("sv", P.now("sv"))])

        sacc = st.enter_context(nc.sbuf_tensor("sacc", [128, NCHUNK * T], F32,
                                               side="right"))
        with ExitStack() as s3:
            wp_sb = [s3.enter_context(nc.sbuf_tensor(f"wp_sb{i}", [128, 16 * 128],
                                                     BF16)) for i in range(2)]
            bp_sb = s3.enter_context(nc.sbuf_tensor("bp_sb", [128, 8], F32))
            P.begin()
            P.load(bp_sb[:], bp0_pg.ap())
            bpw = P.now("dl")
            gatedone = P.now("sv")
            with ExitStack() as sp:
                ps_wp = [sp.enter_context(nc.psum_tensor(f"ps_wp{i}", [128, 512],
                                                         F32)) for i in range(2)]
                wsem = ["dwa", "dwb"]
                wl = [P.loadc(sb3(wp_sb[i], 16),
                              col_block_ap(wp0, i * 128, (i + 1) * 128),
                              sem=wsem[i]) for i in range(2)]
                wpev = [0, 0]
                for oc in range(8):
                    bpar = oc % 2
                    for qc in range(4):
                        par = (oc * 4 + qc) % 2
                        w = [(wsem[bpar], wl[bpar]), ("sv", gatedone)]
                        if wpev[par]:
                            w.append(("ss", wpev[par]))
                        pm = 0
                        for vc in range(16):
                            pm = P.mm(ps_wp[par][:],
                                      wp_sb[bpar][:, vc * 128:(vc + 1) * 128],
                                      retg[:, vc * T + qc * 512:
                                           vc * T + (qc + 1) * 512],
                                      start=(vc == 0), stop=(vc == 15), wait=w,
                                      inc="st" if vc == 15 else None)
                        wpev[par] = P.op(
                            "scalar",
                            lambda s, d=sacc[:, oc * T + qc * 512:
                                             oc * T + (qc + 1) * 512],
                            p=ps_wp[par], o=oc: s.activation(
                                d, p[:], AF.Identity, bias=bp_sb[:, o:o + 1]),
                            wait=[("st", pm), ("dl", bpw)], inc="ss")
                    if oc + 2 < 8:
                        wl[bpar] = P.loadc(sb3(wp_sb[bpar], 16),
                                           col_block_ap(wp0, (oc + 2) * 128,
                                                        (oc + 3) * 128),
                                           wait=[("st", P.now("st"))], sem=wsem[bpar])
            P.end()

    if stage == "s0":
        return dump(sacc[:], 8 * T, [("ss", P.now("ss"))])

    sln = ExitStack()
    lnst = sln.enter_context(nc.sbuf_tensor("lnst", [1, 2 * T], F32))
    with ExitStack() as s4:
        act = s4.enter_context(nc.sbuf_tensor("act", [128, 32 * 512], BF16))
        w1_sb = [s4.enter_context(nc.sbuf_tensor(f"w1_sb{i}", [128, 8 * 128],
                                                 BF16)) for i in range(2)]
        w2_sb = [s4.enter_context(nc.sbuf_tensor(f"w2_sb{i}", [128, 32 * 128],
                                                 BF16)) for i in range(2)]
        qv = s4.enter_context(nc.sbuf_tensor("qv", [128, 8 * 32], F32))
        b2_sb = s4.enter_context(nc.sbuf_tensor("b2_sb", [128, 8], F32))
        s01 = s4.enter_context(nc.sbuf_tensor("s01", [128, 512], F32))
        c01 = s4.enter_context(nc.sbuf_tensor("c01", [128, 512], F32))
        tqs = s4.enter_context(nc.sbuf_tensor("tqs", [128, 512], F32))
        g_scr = s4.enter_context(nc.sbuf_tensor("g_scr", [128, 512], F32))
        xi_scr = [s4.enter_context(nc.sbuf_tensor(f"xi{i}", [128, 512], F32))
                  for i in range(2)]
        x2_scr = [s4.enter_context(nc.sbuf_tensor(f"x2s{i}", [128, 512], F32))
                  for i in range(2)]
        x2b_scr = [s4.enter_context(nc.sbuf_tensor(f"x2b{i}", [128, 512], BF16))
                   for i in range(2)]
        sq3 = s4.enter_context(nc.sbuf_tensor("sq3", [128, 512], BF16))

        P.begin()
        P.load(qv[:, 5 * 32:6 * 32], qph_pg.ap())
        P.load(qv[:, 6 * 32:7 * 32], qwr_pg.ap())
        P.load(qv[:, 7 * 32:8 * 32], qwi_pg.ap())
        P.load(qv[:, 0:32], b1_pg.ap())
        P.load(b2_sb[:], b2_pg.ap())
        qload = P.now("dl")

        def qrow(g):
            return qv[:, g * 32:(g + 1) * 32]
        b1v, av_, bv_, b1s, b1c = qrow(0), qrow(1), qrow(2), qrow(3), qrow(4)
        th, wr, wi = qrow(5), qrow(6), qrow(7)
        scr1 = s01[:, 0:32]
        scr2 = s01[:, 32:64]
        sinth = s01[:, 64:96]
        costh = s01[:, 96:128]

        def rngred(dst, src, extra, wait=()):
            P.vseq(lambda v: v.tensor_scalar(scr1, src, 1.0 / (2 * PI),
                                             extra / (2 * PI), ALU.mult,
                                             ALU.add), wait=wait)
            P.vseq(lambda v: v.tensor_scalar(scr2, scr1, MAGIC, None, ALU.add))
            P.vseq(lambda v: v.tensor_scalar(scr2, scr2, -MAGIC, None, ALU.add))
            P.vseq(lambda v: v.tensor_sub(scr1, scr1, scr2))
            return P.vseq(lambda v: v.tensor_scalar(dst, scr1, 2 * PI, None,
                                                    ALU.mult))
        r1 = rngred(scr2, th, 0.0, wait=[("dl", qload)])
        s1_ = P.op("scalar", lambda s: s.activation(sinth, scr2, AF.Sin),
                   wait=[("sv", r1)], inc="ss")
        r2 = rngred(scr2, th, PI / 2, wait=[("ss", s1_)])
        s2_ = P.op("scalar", lambda s: s.activation(costh, scr2, AF.Sin),
                   wait=[("sv", r2)], inc="ss")
        P.vseq(lambda v: v.tensor_mul(scr1, costh, wr), wait=[("ss", s2_)])
        P.vseq(lambda v: v.tensor_mul(scr2, sinth, wi))
        P.vseq(lambda v: v.tensor_add(scr1, scr1, scr2))
        P.vseq(lambda v: v.tensor_mul(scr2, scr1, wi))
        P.vseq(lambda v: v.tensor_scalar(scr2, scr2, 0.1, None, ALU.mult))
        P.vseq(lambda v: v.tensor_mul(av_, scr2, sinth))
        P.vseq(lambda v: v.tensor_mul(bv_, scr2, costh))
        P.vseq(lambda v: v.tensor_scalar(b1s, b1v, 0.1, None, ALU.mult))
        qprep = P.vseq(lambda v: v.tensor_scalar(b1c, b1v, 0.1, PI / 2,
                                                 ALU.mult, ALU.add))

        if stage == "qprep":
            return dump(qv[:], 8 * 32, [("sv", qprep)])
        with ExitStack() as sp:
            ps_h = [sp.enter_context(nc.psum_tensor(f"ps_h{i}", [128, 512], F32))
                    for i in range(4)]
            ps_o = [sp.enter_context(nc.psum_tensor(f"ps_o{i}", [128, 512], F32))
                    for i in range(2)]
            ps_st = sp.enter_context(nc.psum_tensor("ps_st", [1, 1024], F32))
            wsem = ["dwa", "dwb"]
            xsem = ["dxa", "dxb"]
            ssem_ = ["dsa", "dsb"]
            w1l = [P.loadc(sb3(w1_sb[i], 8), col_block_ap(w1, i * 128,
                                                          (i + 1) * 128),
                           sem=wsem[i]) for i in range(2)]
            w2ld = [0, 0]
            w2done = 0
            psfree = [0, 0]
            x2free = [0, 0]
            xifree = [0, 0]
            stm = 0
            sqe = 0
            actev = [0, 0, 0, 0]
            for fq in range(4):
                hoff = fq * 512
                for mc in range(32):
                    par = mc % 2
                    par4 = mc % 4
                    w = [(wsem[par], w1l[par]), ("sv", qprep)]
                    if actev[par4]:
                        w.append(("sv", actev[par4]))
                    if w2done:
                        w.append(("st", w2done))
                    hm = 0
                    for kc in range(8):
                        hm = P.mm(ps_h[par4][:],
                                  w1_sb[par][:, kc * 128:(kc + 1) * 128],
                                  xn_bf[:, kc * T + hoff:kc * T + hoff + 512],
                                  start=(kc == 0), stop=(kc == 7), wait=w,
                                  inc="st" if kc == 7 else None)
                    idx = fq * 32 + mc
                    if idx + 2 < 128:
                        nb = (idx + 2) % 32
                        w1l[par] = P.loadc(sb3(w1_sb[par], 8),
                                           col_block_ap(w1, nb * 128,
                                                        (nb + 1) * 128),
                                           wait=[("st", P.now("st"))],
                                           sem=wsem[par])
                    P.op("scalar",
                         lambda s, p=ps_h[par4], m=mc: s.activation(
                             s01[:], p[:], AF.Sin, bias=b1s[:, m:m + 1],
                             scale=0.1),
                         wait=[("st", hm)])
                    se = P.op("scalar",
                              lambda s, p=ps_h[par4], m=mc: s.activation(
                                  c01[:], p[:], AF.Sin, bias=b1c[:, m:m + 1],
                                  scale=0.1),
                              inc="ss")
                    P.op("vector",
                         lambda v, p=ps_h[par4], m=mc: v.scalar_tensor_tensor(
                             tqs[:], c01[:], av_[:, m:m + 1], p[:], ALU.mult,
                             ALU.add),
                         wait=[("ss", se)])
                    P.op("vector",
                         lambda v, m=mc: v.scalar_tensor_tensor(
                             tqs[:], s01[:], bv_[:, m:m + 1], tqs[:], ALU.mult,
                             ALU.add))
                    actev[par4] = P.op(
                        "vector",
                        lambda v, m=mc, d=act[:, mc * 512:(mc + 1) * 512]:
                        v.tensor_scalar(d, tqs[:], b1v[:, m:m + 1], 0.0, ALU.add,
                                        ALU.max),
                        inc="sv")
                if stage == "act0" and fq == 0:
                    return dump(act[:], 32 * 512, [("sv", P.now("sv"))])
                actall = P.now("sv")
                w2ld = [P.loadc(sb3(w2_sb[i], 32),
                                col_block_ap(w2, i * 128, (i + 1) * 128),
                                wait=[("st", P.now("st"))], sem=wsem[i])
                        for i in range(2)]
                toks = slice(hoff, hoff + 512)
                pend_lnstat = None
                stprev = [(0, 0), (0, 0)]
                for oc in range(8):
                    par = oc % 2
                    w = [(wsem[par], w2ld[par]), ("sv", actall)]
                    if psfree[par]:
                        w.append(("ss", psfree[par]))
                    om = 0
                    for vc in range(32):
                        om = P.mm(ps_o[par][:],
                                  w2_sb[par][:, vc * 128:(vc + 1) * 128],
                                  act[:, vc * 512:(vc + 1) * 512],
                                  start=(vc == 0), stop=(vc == 31), wait=w,
                                  inc="st" if vc == 31 else None)
                    if pend_lnstat is not None:
                        pend_lnstat()
                        pend_lnstat = None
                    if oc + 2 < 8:
                        w2ld[par] = P.loadc(
                            sb3(w2_sb[par], 32),
                            col_block_ap(w2, (oc + 2) * 128, (oc + 3) * 128),
                            wait=[("st", P.now("st"))], sem=wsem[par])
                    xw = [("sv", xifree[par])] if xifree[par] else []
                    xl = P.load(xi_scr[par][:],
                                x_in.ap()[oc * 128:(oc + 1) * 128, toks],
                                wait=xw, sem=xsem[par])
                    ge = psfree[par] = P.op(
                        "scalar",
                        lambda s, p=ps_o[par], o=oc: s.activation(
                            g_scr[:], p[:], AF.Silu, bias=b2_sb[:, o:o + 1]),
                        wait=[("st", om), ("dl", qload)], inc="ss")
                    w2_ = [("ss", ge)]
                    if x2free[par]:
                        w2_.append((ssem_[par], x2free[par]))
                    P.op("vector",
                         lambda v, p=par, o=oc, ts=toks: v.tensor_mul(
                             x2_scr[p][:], g_scr[:],
                             sacc[:, o * T + ts.start:o * T + ts.stop]),
                         wait=w2_)
                    xifree[par] = P.op(
                        "vector",
                        lambda v, p=par: v.tensor_add(
                            x2_scr[p][:], x2_scr[p][:], xi_scr[p][:]),
                        wait=[(xsem[par], xl)], inc="sv")
                    x2vw = []
                    if stprev[par][0]:
                        x2vw.append(("st", stprev[par][0]))
                    if stprev[par][1]:
                        x2vw.append(("ss", stprev[par][1]))
                    x2v = P.op("vector",
                               lambda v, p=par: v.tensor_copy(x2b_scr[p][:],
                                                              x2_scr[p][:]),
                               wait=x2vw, inc="sv")
                    x2free[par] = P.load(
                        x1_dram.ap()[oc * 128:(oc + 1) * 128, toks],
                        x2_scr[par][:], wait=[("sv", xifree[par])],
                        sem=ssem_[par])

                    def mk_lnstat(oc_=oc, par_=par, x2v_=x2v):
                        def go():
                            nonlocal stm, sqe
                            P.mm(ps_st[0:1, 0:512], ones_bf[:],
                                 x2b_scr[par_][:], start=(oc_ == 0),
                                 stop=(oc_ == 7), wait=[("sv", x2v_)])
                            sqe = P.op("scalar",
                                       lambda s, p=par_: s.activation(
                                           sq3[:], x2b_scr[p][:], AF.Square),
                                       wait=[("sv", x2v_),
                                             ("st", stprev[1 - par_][0])],
                                       inc="ss")
                            stm = P.mm(ps_st[0:1, 512:1024], ones_bf[:],
                                       sq3[:], start=(oc_ == 0),
                                       stop=(oc_ == 7), wait=[("ss", sqe)],
                                       inc="st" if oc_ == 7 else None)
                            stprev[par_] = (stm, sqe)
                        return go
                    pend_lnstat = mk_lnstat()
                pend_lnstat()
                pend_lnstat = None
                P.op("vector",
                     lambda v, d=lnst[0:1, hoff:hoff + 512],
                     s=ps_st[0:1, 0:512]: v.tensor_copy(d, s),
                     wait=[("st", stm)])
                lnev = P.op("vector",
                            lambda v, d=lnst[0:1, T + hoff:T + hoff + 512],
                            s=ps_st[0:1, 512:1024]: v.tensor_copy(d, s),
                            inc="sv")
                w2done = P.now("st")
        P.end()

    if stage == "x1":
        with ExitStack() as sd:
            bb_ = sd.enter_context(nc.sbuf_tensor("bb_", [128, 8 * T], F32))
            P.begin()
            dv = 0
            for c in range(NCHUNK):
                dv = P.load(bb_[:, c * T:(c + 1) * T],
                            x1_dram.ap()[c * 128:(c + 1) * 128, :],
                            wait=[("dsa", P.cnt["dsa"]), ("dsb", P.cnt["dsb"])])
            P.end()
            return dump(bb_[:], 8 * T, [("dl", dv)])

    with ExitStack() as s5:
        x1_scr = [s5.enter_context(nc.sbuf_tensor(f"x1l{i}", [128, T], F32))
                  for i in range(2)]
        t_scr2 = s5.enter_context(nc.sbuf_tensor("t_scr2", [128, T], F32))
        lns2 = s5.enter_context(nc.sbuf_tensor("lns2", [1, 4 * T], F32))
        P.begin()
        mu = lns2[0:1, 0:T]
        va = lns2[0:1, T:2 * T]
        bb2 = lns2[0:1, 2 * T:3 * T]
        tmp = lns2[0:1, 3 * T:4 * T]
        P.op("vector", lambda v: v.tensor_scalar(mu, lnst[0:1, 0:T], 1.0 / D,
                                                 None, ALU.mult),
             wait=[("sv", P.now("sv"))])
        P.op("vector", lambda v: v.tensor_scalar(va, lnst[0:1, T:2 * T], 1.0 / D,
                                                 1e-5, ALU.mult, ALU.add))
        P.op("vector", lambda v: v.scalar_tensor_tensor(tmp, mu, -1.0, mu,
                                                        ALU.mult, ALU.mult))
        d1 = P.op("vector", lambda v: v.tensor_add(va, va, tmp), inc="sv")
        P.op("scalar", lambda s: s.activation(va, va, AF.Ln), wait=[("sv", d1)])
        d2 = P.op("scalar", lambda s: s.activation(va, va, AF.Exp, scale=-0.5),
                  inc="ss")
        d3 = P.op("vector", lambda v: v.scalar_tensor_tensor(bb2, mu, -1.0, va,
                                                             ALU.mult, ALU.mult),
                  wait=[("ss", d2)], inc="sv")
        with ExitStack() as sp:
            ps_a = sp.enter_context(nc.psum_tensor("ps_a2", [128, T], F32))
            ps_b = sp.enter_context(nc.psum_tensor("ps_b2", [128, T], F32))
            bmm = 0
            for qc in range(4):
                sl = slice(qc * 512, (qc + 1) * 512)
                P.mm(ps_a[:, sl], ones_row_f[:], va[:, sl], True, True,
                     wait=[("sv", d3)])
                bmm = P.mm(ps_b[:, sl], ones_row_f[:], bb2[:, sl], True, True,
                           inc="st")
            apl = [0, 0]
            xs = ["dxa", "dxb"]
            for c in range(NCHUNK):
                par = c % 2
                w = [("dsa", P.cnt["dsa"]), ("dsb", P.cnt["dsb"])]
                if apl[par]:
                    w.append(("sv", apl[par]))
                lw = P.load(x1_scr[par][:],
                            x1_dram.ap()[c * 128:(c + 1) * 128, :], wait=w,
                            sem=xs[par])
                P.op("vector",
                     lambda v, p=par: v.tensor_mul(t_scr2[:], x1_scr[p][:],
                                                   ps_a[:]),
                     wait=[(xs[par], lw), ("st", bmm)])
                apl[par] = P.op("vector",
                                lambda v, d=xn_bf[:, c * T:(c + 1) * T]:
                                v.tensor_add(d, t_scr2[:], ps_b[:]), inc="sv")
        P.end()

    if stage == "xn1":
        return dump(xn_bf[:], 8 * T, [("sv", P.now("sv"))])
    sln.close()

    sat = ExitStack()
    attn_bf = sat.enter_context(nc.sbuf_tensor("attn_bf", [128, NCHUNK * T], BF16))
    with ExitStack() as s6:
        wq_sb = s6.enter_context(nc.sbuf_tensor("l1wq", [128, 8 * 128], BF16))
        wk_sb = s6.enter_context(nc.sbuf_tensor("l1wk", [128, 8 * 128], BF16))
        wv_sb = s6.enter_context(nc.sbuf_tensor("l1wv", [128, 8 * 128], BF16))
        bv_sb = s6.enter_context(nc.sbuf_tensor("l1bv", [1, 128], BF16))
        qp_sb = s6.enter_context(nc.sbuf_tensor("qp_sb", [128, T], BF16))
        kp_sb = s6.enter_context(nc.sbuf_tensor("kp_sb", [128, T], BF16))
        v_aug = s6.enter_context(nc.sbuf_tensor("v_aug", [128, 16 * 130], BF16))
        pno = [s6.enter_context(nc.sbuf_tensor(f"pno{i}", [128, 1024], BF16))
               for i in range(2)]
        att_scr = s6.enter_context(nc.sbuf_tensor("att_scr", [128, T], F32))
        rsa = s6.enter_context(nc.sbuf_tensor("rsa", [1, T], F32))
        rsb = s6.enter_context(nc.sbuf_tensor("rsb", [1, T], F32))
        rs_inv = s6.enter_context(nc.sbuf_tensor("rs_inv", [2, T], F32))

        P.begin()
        vv = v_aug[:].rearrange("p (t g) -> p t g", g=130)
        P.op("vector", lambda v: v.memset(vv[:, :, 64:65], 1.0))
        vinit = P.op("vector", lambda v: v.memset(vv[:, :, 129:130], 1.0),
                     inc="sv")
        P.end()

        rsmove = 0
        bmm_prev = 0
        norm_sv = 0
        ps_bc = s6.enter_context(nc.psum_tensor("l1ps_bc", [128, 1024], F32))
        P.begin()
        prev_sv = 0
        prev_ss = 0
        qkv_end_prev = 0
        pend_norm = None
        for pr in range(8):
            P.loadc(sb3(wq_sb, 8), col_block_ap(wqkv1, pr * 128, (pr + 1) * 128),
                    wait=[("st", qkv_end_prev)])
            P.loadc(sb3(wk_sb, 8),
                    col_block_ap(wqkv1, D + pr * 128, D + (pr + 1) * 128))
            P.loadc(sb3(wv_sb, 8),
                    col_block_ap(wqkv1, 2 * D + pr * 128, 2 * D + (pr + 1) * 128))
            P.loadc(bv_sb[:], row1(bqkv1, 2 * D + pr * 128, 2 * D + (pr + 1) * 128))
            ldg = P.now("dg")
            with ExitStack() as sp:
                ps_qk = sp.enter_context(nc.psum_tensor(f"l1ps_qk{pr}", [128, T],
                                                        F32))
                ps_v = [sp.enter_context(nc.psum_tensor(f"l1ps_v{pr}_{i}",
                                                        [128, 128], F32))
                        for i in range(2)]
                vev = [vinit, vinit]

                def vtile(tt):
                    par = tt % 2
                    w = [("dg", ldg), ("sv", vev[par])]
                    for kc in range(8):
                        P.mm(ps_v[par][:],
                             xn_bf[:, kc * T + tt * 128:kc * T + (tt + 1) * 128],
                             wv_sb[:, kc * 128:(kc + 1) * 128],
                             start=(kc == 0), stop=False, wait=w)
                    vmm = P.mm(ps_v[par][:], ones_row_bf[:], bv_sb[:],
                               start=False, stop=True, inc="st")
                    vev[par] = P.op(
                        "vector",
                        lambda v, t_=tt, p=ps_v[par]: v.tensor_copy(
                            v_aug[:, t_ * 130:(t_ + 1) * 130]
                            .rearrange("p (g c) -> p g c", c=65)[:, :, 0:64],
                            p[:].rearrange("p (g c) -> p g c", c=64)),
                        wait=[("st", vmm)], inc="sv")

                qmm = 0
                for qn in range(4):
                    for kc in range(8):
                        qmm = P.mm(ps_qk[:, qn * 512:(qn + 1) * 512],
                                   wq_sb[:, kc * 128:(kc + 1) * 128],
                                   xn_bf[:, kc * T + qn * 512:kc * T + (qn + 1) * 512],
                                   start=(kc == 0), stop=(kc == 7),
                                   wait=[("dg", ldg), ("sv", prev_sv),
                                         ("ss", prev_ss)],
                                   inc="st" if kc == 7 else None)
                if pend_norm is not None:
                    pend_norm()
                    pend_norm = None
                eq = P.op("scalar",
                          lambda s, b=bqkv1_sb[:, pr:pr + 1], p=ps_qk:
                          s.activation(qp_sb[:], p[:], AF.Identity, bias=b),
                          wait=[("st", qmm)], inc="ss")
                for tt in range(0, 4):
                    vtile(tt)
                kmm = 0
                for qn in range(4):
                    for kc in range(8):
                        kmm = P.mm(ps_qk[:, qn * 512:(qn + 1) * 512],
                                   wk_sb[:, kc * 128:(kc + 1) * 128],
                                   xn_bf[:, kc * T + qn * 512:kc * T + (qn + 1) * 512],
                                   start=(kc == 0), stop=(kc == 7),
                                   wait=[("ss", eq)],
                                   inc="st" if kc == 7 else None)
                ek = P.op("scalar",
                          lambda s, b=bqkv1_sb[:, 8 + pr:9 + pr], p=ps_qk:
                          s.activation(kp_sb[:], p[:], AF.Identity, bias=b),
                          wait=[("st", kmm)], inc="ss")
                for tt in range(4, 16):
                    vtile(tt)
                qkv_end_prev = P.now("st")
            with ExitStack() as sp:
                ps_s = [sp.enter_context(nc.psum_tensor(f"l1ps_s{pr}_{i}",
                                                        [128, 1024], F32))
                        for i in range(2)]
                ps_av = sp.enter_context(nc.psum_tensor(f"l1ps_av{pr}",
                                                        [65, 1024], F32))
                vall = P.now("sv")
                ekk = P.now("ss")
                exd = [0, 0]
                avm = [0, 0]
                evdone = 0
                for hb in range(2):
                    hb0 = hb * 64
                    for qh in range(2):
                        q0h = qh * 1024
                        pend_av = None
                        for tk in range(16):
                            par = tk % 2
                            w = [("ss", ekk), ("sv", vall)]
                            if exd[par]:
                                w.append(("ss", exd[par]))
                            smm = 0
                            for qn2 in range(2):
                                smm = P.mm(ps_s[par][:, qn2 * 512:(qn2 + 1) * 512],
                                           kp_sb[hb0:hb0 + 64,
                                                 tk * 128:(tk + 1) * 128],
                                           qp_sb[hb0:hb0 + 64,
                                                 q0h + qn2 * 512:
                                                 q0h + (qn2 + 1) * 512],
                                           True, True, wait=w,
                                           inc="st" if qn2 == 1 else None)
                            if pend_av is not None:
                                emit_av, pend_av = pend_av, None
                                emit_av()
                            we = [("st", smm)]
                            if avm[par]:
                                we.append(("st", avm[par]))
                            exd[par] = P.op(
                                "scalar",
                                lambda s, d=pno[par], p=ps_s[par]: s.activation(
                                    d[:], p[:], AF.Exp, scale=0.125,
                                    bias=neg10_sb[:, 0:1]),
                                wait=we, inc="ss")

                            def mk_av(tk_=tk, par_=par, exd_=None):
                                def go():
                                    wa = [("ss", exd[par_])]
                                    if tk_ == 0 and evdone:
                                        wa.append(("sv", evdone))
                                    for qn2 in range(2):
                                        avm[par_] = P.mm(
                                            ps_av[:, qn2 * 512:(qn2 + 1) * 512],
                                            v_aug[:, tk_ * 130 + hb * 65:
                                                  tk_ * 130 + (hb + 1) * 65]
                                            if hb == 0 else
                                            v_aug[:, tk_ * 130 + 65:
                                                  tk_ * 130 + 130],
                                            pno[par_][:, qn2 * 512:
                                                      (qn2 + 1) * 512],
                                            start=(tk_ == 0), stop=(tk_ == 15),
                                            wait=wa,
                                            inc="st" if (tk_ == 15 and qn2 == 1)
                                            else None)
                                return go
                            pend_av = mk_av()
                        pend_av()
                        aw = [("st", avm[1]), ("dg", rsmove)]
                        if norm_sv:
                            aw.append(("sv", norm_sv))
                        ae = P.op("vector",
                                  lambda v, d=att_scr[hb0:hb0 + 64,
                                                      q0h:q0h + 1024],
                                  s=ps_av[0:64, :]: v.tensor_copy(d, s),
                                  wait=aw)
                        dst_rs = (rsa if hb == 0 else rsb)[0:1, q0h:q0h + 1024]
                        evdone = P.op("vector",
                                      lambda v, d=dst_rs, s=ps_av[64:65, :]:
                                      v.reciprocal(d, s), inc="sv")
                m1w = [("sv", evdone)]
                if bmm_prev:
                    m1w.append(("st", bmm_prev))
                P.op("gpsimd", lambda e: e.dma_start(rs_inv[0:1, :], rsa[0:1, :]),
                     wait=m1w, inc="dg", inc_n=16)
                rsmove = P.op("gpsimd",
                              lambda e: e.dma_start(rs_inv[1:2, :], rsb[0:1, :]),
                              inc="dg", inc_n=16)
                prev_sv = evdone
                prev_ss = P.now("ss")

            def mk_norm(pr_=pr, rsm=rsmove):
                def go():
                    nonlocal bmm_prev, norm_sv
                    bmm = 0
                    for half2 in range(2):
                        w = [("dg", rsm)]
                        if norm_sv:
                            w.append(("sv", norm_sv))
                        for qn in range(2):
                            c0 = half2 * 1024 + qn * 512
                            bmm = P.mm(ps_bc[:, qn * 512:(qn + 1) * 512],
                                       selab_sb[:], rs_inv[:, c0:c0 + 512],
                                       True, True, wait=w, inc="st")
                        norm_sv = P.op(
                            "vector",
                            lambda v,
                            d=attn_bf[:, pr_ * T + half2 * 1024:
                                      pr_ * T + (half2 + 1) * 1024],
                            s=att_scr[:, half2 * 1024:(half2 + 1) * 1024],
                            p=ps_bc: v.tensor_mul(d, s, p[:, 0:1024]),
                            wait=[("st", bmm)], inc="sv")
                    bmm_prev = bmm
                return go
            pend_norm = mk_norm()
        flush_norm = pend_norm
        flush_norm()
        P.end()

    if stage == "attn1":
        return dump(attn_bf[:], 8 * T, [("sv", P.now("sv"))])

    with ExitStack() as s7:
        wo_sb = [s7.enter_context(nc.sbuf_tensor(f"wo{i}", [128, 8 * 128], BF16))
                 for i in range(2)]
        bo_sb = s7.enter_context(nc.sbuf_tensor("bo_sb", [128, 8], F32))
        P.begin()
        P.load(bo_sb[:], bout1_pg.ap())
        bow = P.now("dl")
        attdone = P.now("sv")
        with ExitStack() as sp:
            ps_wo = [sp.enter_context(nc.psum_tensor(f"ps_wo{i}", [128, 512],
                                                     F32)) for i in range(2)]
            wsem = ["dwa", "dwb"]
            wl = [P.loadc(sb3(wo_sb[i], 8), col_block_ap(wout1, i * 128,
                                                         (i + 1) * 128),
                          wait=[("st", P.now("st"))], sem=wsem[i]) for i in range(2)]
            oev = [0, 0]
            for oc in range(8):
                bpar = oc % 2
                for qc in range(4):
                    par = (oc * 4 + qc) % 2
                    w = [(wsem[bpar], wl[bpar]), ("sv", attdone)]
                    if oev[par]:
                        w.append(("ss", oev[par]))
                    pm = 0
                    for kc in range(8):
                        pm = P.mm(ps_wo[par][:],
                                  wo_sb[bpar][:, kc * 128:(kc + 1) * 128],
                                  attn_bf[:, kc * T + qc * 512:
                                          kc * T + (qc + 1) * 512],
                                  start=(kc == 0), stop=(kc == 7), wait=w,
                                  inc="st" if kc == 7 else None)
                    oev[par] = P.op(
                        "scalar",
                        lambda s, d=sacc[:, oc * T + qc * 512:
                                         oc * T + (qc + 1) * 512],
                        p=ps_wo[par], o=oc: s.activation(
                            d, p[:], AF.Identity, bias=bo_sb[:, o:o + 1]),
                        wait=[("st", pm), ("dl", bow)], inc="ss")
                if oc + 2 < 8:
                    wl[bpar] = P.loadc(sb3(wo_sb[bpar], 8),
                                       col_block_ap(wout1, (oc + 2) * 128,
                                                    (oc + 3) * 128),
                                       wait=[("st", P.now("st"))], sem=wsem[bpar])
        P.end()

    if stage == "s1":
        return dump(sacc[:], 8 * T, [("ss", P.now("ss"))])
    sat.close()

    with ExitStack() as s8:
        cw1 = s8.enter_context(nc.sbuf_tensor("cw1", [128, 8 * 1024], BF16))
        cw2 = s8.enter_context(nc.sbuf_tensor("cw2", [128, 8 * 512], BF16))
        cw3 = s8.enter_context(nc.sbuf_tensor("cw3", [128, 4 * 256], BF16))
        cw4 = s8.enter_context(nc.sbuf_tensor("cw4", [128, 2], BF16))
        cb1 = s8.enter_context(nc.sbuf_tensor("cb1", [128, 8], F32))
        cb2 = s8.enter_context(nc.sbuf_tensor("cb2", [128, 4], F32))
        cb3 = s8.enter_context(nc.sbuf_tensor("cb3", [128, 2], F32))
        sc_sb = s8.enter_context(nc.sbuf_tensor("sc_sb", [1, 16], F32))
        gcv = s8.enter_context(nc.sbuf_tensor("gcv", [128, 16], BF16))
        gcs = s8.enter_context(nc.sbuf_tensor("gcs", [128, 8], F32))
        out_q = s8.enter_context(nc.sbuf_tensor("out_q", [128, 8 * 512], F32))
        outb_q = s8.enter_context(nc.sbuf_tensor("outb_q", [128, 8 * 512], BF16))
        h1b = s8.enter_context(nc.sbuf_tensor("h1b", [128, 8 * 512], BF16))
        h2b = s8.enter_context(nc.sbuf_tensor("h2b", [128, 4 * 512], BF16))
        h3b = s8.enter_context(nc.sbuf_tensor("h3b", [128, 2 * 512], BF16))
        th4 = s8.enter_context(nc.sbuf_tensor("th4", [1, 512], F32))
        cpv = s8.enter_context(nc.sbuf_tensor("cpv", [1, 512], F32))
        tmpc = s8.enter_context(nc.sbuf_tensor("tmpc", [128, 512], F32))
        tmc2 = s8.enter_context(nc.sbuf_tensor("tmc2", [128, 512], F32))
        gf_scr = [s8.enter_context(nc.sbuf_tensor(f"gfs{i}", [128, 512], F32))
                  for i in range(2)]
        fm_scr = s8.enter_context(nc.sbuf_tensor("fm_scr", [128, 512], F32))
        xf_all = s8.enter_context(nc.sbuf_tensor("xf_all", [128, 8 * 512], F32))
        s8g = ExitStack()
        gw1 = s8g.enter_context(nc.sbuf_tensor("gw1", [128, 8 * 512], BF16))
        gw2 = s8g.enter_context(nc.sbuf_tensor("gw2", [128, 4 * 256], BF16))
        gw3 = s8g.enter_context(nc.sbuf_tensor("gw3", [128, 2], BF16))
        gb1 = s8g.enter_context(nc.sbuf_tensor("gb1", [128, 4], F32))
        gb2 = s8g.enter_context(nc.sbuf_tensor("gb2", [128, 2], F32))

        P.begin()
        P.loadc(sb3(cw1, 8), cnw[0].ap().rearrange("(kc p) m -> p kc m", p=128))
        P.loadc(sb3(cw2, 8), cnw[1].ap().rearrange("(kc p) m -> p kc m", p=128))
        P.loadc(sb3(cw3, 4), cnw[2].ap().rearrange("(kc p) m -> p kc m", p=128))
        P.loadc(cw4[:], cnw4_pg.ap())
        P.loadc(sb3(gw1, 8), gcw[0].ap().rearrange("(kc p) m -> p kc m", p=128))
        P.loadc(sb3(gw2, 4), gcw[1].ap().rearrange("(kc p) m -> p kc m", p=128))
        P.loadc(gw3[:], gcw3_pg.ap())
        wload = P.now("dg")
        P.load(cb1[:], cnb_pg[0].ap())
        P.load(cb2[:], cnb_pg[1].ap())
        P.load(cb3[:], cnb_pg[2].ap())
        P.load(sc_sb[0:1, 6:7], row1(cnb[3], 0, 1))
        P.load(gb1[:], gcb_pg[0].ap())
        P.load(gb2[:], gcb_pg[1].ap())
        P.load(sc_sb[0:1, 7:8], row1(gcb[2], 0, 1))
        P.load(sc_sb[0:1, 2:3], row1(arange_in, 0, 1))
        bload = P.now("dl")
        P.vseq(lambda v: v.tensor_scalar(sc_sb[0:1, 0:1], sc_sb[0:1, 6:7],
                                         0.5, None, ALU.mult),
               wait=[("dl", bload)])
        sc1 = P.vseq(lambda v: v.tensor_scalar(sc_sb[0:1, 1:2],
                                               sc_sb[0:1, 7:8], 0.5, None,
                                               ALU.mult))
        for c in range(NCHUNK):
            P.op("vector",
                 lambda v, d=gcs[:, c:c + 1], s=xn_bf[:, c * T:(c + 1) * T]:
                 v.reduce_sum(d, s, axis=AX), inc="sv")
        gsum = P.vseq(lambda v: v.tensor_copy(gcv[:, 0:8], gcs[:]))
        with ExitStack() as sp:
            ps_gn = sp.enter_context(nc.psum_tensor("ps_gfn", [128, 8], F32))
            ps_gf = sp.enter_context(nc.psum_tensor("ps_gf", [1, 2], F32))
            gmm = 0
            for m in range(4):
                for kc in range(8):
                    gmm = P.mm(ps_gn[:, m:m + 1],
                               gw1[:, kc * 512 + m * 128:kc * 512 + (m + 1) * 128],
                               gcv[:, kc:kc + 1], start=(kc == 0), stop=(kc == 7),
                               wait=[("dg", wload), ("sv", gsum)],
                               inc="st" if kc == 7 else None)
            g1e = 0
            for m in range(4):
                g1e = P.op("scalar",
                           lambda s, m_=m: s.activation(
                               gcv[:, 8 + m_:9 + m_], ps_gn[:, m_:m_ + 1],
                               AF.Gelu, bias=gb1[:, m_:m_ + 1], scale=1.0 / T),
                           wait=[("st", gmm), ("dl", bload)], inc="ss")
            gm2 = 0
            for m in range(2):
                for kc in range(4):
                    gm2 = P.mm(ps_gn[:, 4 + m:5 + m],
                               gw2[:, kc * 256 + m * 128:kc * 256 + (m + 1) * 128],
                               gcv[:, 8 + kc:9 + kc], start=(kc == 0),
                               stop=(kc == 3),
                               wait=[("ss", g1e)],
                               inc="st" if kc == 3 else None)
            g2e = 0
            for m in range(2):
                g2e = P.op("scalar",
                           lambda s, m_=m: s.activation(
                               gcv[:, 12 + m_:13 + m_], ps_gn[:, 4 + m_:5 + m_],
                               AF.Gelu, bias=gb2[:, m_:m_ + 1]),
                           wait=[("st", gm2)], inc="ss")
            gm3 = 0
            for kc in range(2):
                gm3 = P.mm(ps_gf[0:1, 0:1], gw3[:, kc:kc + 1],
                           gcv[:, 12 + kc:13 + kc], start=(kc == 0),
                           stop=(kc == 1),
                           wait=[("ss", g2e)], inc="st" if kc == 1 else None)
            sgf = P.op("vector", lambda v: v.tensor_copy(sc_sb[0:1, 8:9],
                                                         ps_gf[0:1, 0:1]),
                       wait=[("st", gm3)], inc="sv")
            tgf = P.op("scalar", lambda s: s.activation(sc_sb[0:1, 3:4],
                                                        sc_sb[0:1, 8:9],
                                                        AF.Tanh, scale=0.5,
                                                        bias=sc_sb[0:1, 1:2]),
                       wait=[("sv", sgf), ("sv", sc1)], inc="ss")
        P.vseq(lambda v: v.tensor_scalar(sc_sb[0:1, 3:4], sc_sb[0:1, 3:4],
                                         0.5, 0.5, ALU.mult, ALU.add),
               wait=[("ss", tgf)])
        P.vseq(lambda v: v.tensor_scalar(sc_sb[0:1, 8:9], sc_sb[0:1, 3:4],
                                         1.4, None, ALU.mult))
        P.vseq(lambda v: v.tensor_mul(sc_sb[0:1, 8:9], sc_sb[0:1, 8:9],
                                      sc_sb[0:1, 2:3]))
        P.vseq(lambda v: v.tensor_scalar(sc_sb[0:1, 9:10], sc_sb[0:1, 2:3],
                                         -0.7, 0.1, ALU.mult, ALU.add))
        P.vseq(lambda v: v.tensor_add(sc_sb[0:1, 4:5], sc_sb[0:1, 8:9],
                                      sc_sb[0:1, 9:10]))
        cprep = P.vseq(lambda v: v.tensor_scalar(sc_sb[0:1, 5:6],
                                                 sc_sb[0:1, 2:3], 0.3, None,
                                                 ALU.mult))
        P.end()

        if stage == "gfs":
            return dump(sc_sb[0:1, :], 16, [("sv", cprep)])

        s8g.close()
        tw_sgn = [1, 1, -1, -1, -1, -1, 1, 1]
        prev_q_done = 0
        osem = ["dsa", "dsb"]
        ostore = [P.cnt["dsa"], P.cnt["dsb"]]
        xf_w = [0, 0, 0, 0]
        prev_fin = 0
        P.begin()
        for Q in range(4):
            qoff = Q * 512
            for c in range(NCHUNK):
                xf_w[Q] = P.load(xf_all[:, c * 512:(c + 1) * 512],
                                 x1_dram.ap()[c * 128:(c + 1) * 128,
                                              qoff:qoff + 512],
                                 wait=[("ss", prev_q_done), ("sv", prev_fin)],
                                 sem="dxa" if Q % 2 == 0 else "dxb")
            inits = 0
            for c in range(NCHUNK):
                w = [("sv", cprep)]
                if prev_q_done:
                    w.append(("ss", prev_q_done))
                inits = P.op("vector",
                             lambda v, d=out_q[:, c * 512:(c + 1) * 512],
                             s=xn_bf[:, c * T + qoff:c * T + qoff + 512]:
                             v.tensor_copy(d, s), wait=w, inc="sv")
            with ExitStack() as sp:
                ps_c = [sp.enter_context(nc.psum_tensor(f"ps_c{Q}{i}", [128, 512],
                                                        F32)) for i in range(2)]
                ps_c4 = sp.enter_context(nc.psum_tensor(f"ps_c4{Q}", [1, 512],
                                                        F32))
                ps_cp = sp.enter_context(nc.psum_tensor(f"ps_cp{Q}", [128, 512],
                                                        F32))
                tws = inits
                psc_free = [0, 0]
                for cyc in range(3):
                    def rhs1(kc):
                        if cyc == 0:
                            return xn_bf[:, kc * T + qoff:kc * T + qoff + 512]
                        return outb_q[:, kc * 512:(kc + 1) * 512]
                    h1e = 0
                    for mc in range(8):
                        par = mc % 2
                        w = [("dg", wload), ("dl", bload), ("sv", tws),
                             ("ss", psc_free[par])]
                        m1 = 0
                        for kc in range(8):
                            m1 = P.mm(ps_c[par][:],
                                      cw1[:, kc * 1024 + mc * 128:
                                          kc * 1024 + (mc + 1) * 128],
                                      rhs1(kc), start=(kc == 0), stop=(kc == 7),
                                      wait=w, inc="st" if kc == 7 else None)
                        h1e = psc_free[par] = P.op(
                            "scalar",
                            lambda s, d=h1b[:, mc * 512:(mc + 1) * 512],
                            p=ps_c[par], m=mc: s.activation(
                                d, p[:], AF.Gelu, bias=cb1[:, m:m + 1]),
                            wait=[("st", m1)], inc="ss")
                    h2e = 0
                    for mc in range(4):
                        par = mc % 2
                        m2 = 0
                        for kc in range(8):
                            m2 = P.mm(ps_c[par][:],
                                      cw2[:, kc * 512 + mc * 128:
                                          kc * 512 + (mc + 1) * 128],
                                      h1b[:, kc * 512:(kc + 1) * 512],
                                      start=(kc == 0), stop=(kc == 7),
                                      wait=[("ss", h1e),
                                            ("ss", psc_free[par])],
                                      inc="st" if kc == 7 else None)
                        h2e = psc_free[par] = P.op(
                            "scalar",
                            lambda s, d=h2b[:, mc * 512:(mc + 1) * 512],
                            p=ps_c[par], m=mc: s.activation(
                                d, p[:], AF.Gelu, bias=cb2[:, m:m + 1]),
                            wait=[("st", m2)], inc="ss")
                    h3e = 0
                    for mc in range(2):
                        par = mc % 2
                        m3 = 0
                        for kc in range(4):
                            m3 = P.mm(ps_c[par][:],
                                      cw3[:, kc * 256 + mc * 128:
                                          kc * 256 + (mc + 1) * 128],
                                      h2b[:, kc * 512:(kc + 1) * 512],
                                      start=(kc == 0), stop=(kc == 3),
                                      wait=[("ss", h2e),
                                            ("ss", psc_free[par])],
                                      inc="st" if kc == 3 else None)
                        h3e = psc_free[par] = P.op(
                            "scalar",
                            lambda s, d=h3b[:, mc * 512:(mc + 1) * 512],
                            p=ps_c[par], m=mc: s.activation(
                                d, p[:], AF.Gelu, bias=cb3[:, m:m + 1]),
                            wait=[("st", m3)], inc="ss")
                    m4 = 0
                    for kc in range(2):
                        m4 = P.mm(ps_c4[0:1, :], cw4[:, kc:kc + 1],
                                  h3b[:, kc * 512:(kc + 1) * 512],
                                  start=(kc == 0), stop=(kc == 1),
                                  wait=[("ss", h3e)],
                                  inc="st" if kc == 1 else None)
                    t4e = P.op("scalar",
                               lambda s: s.activation(th4[:], ps_c4[:], AF.Tanh,
                                                      scale=0.5,
                                                      bias=sc_sb[0:1, 0:1]),
                               wait=[("st", m4)], inc="ss")
                    cpe = P.op("vector",
                               lambda v: v.tensor_scalar(cpv[:], th4[:],
                                                         sc_sb[0:1, 5:6],
                                                         sc_sb[0:1, 4:5],
                                                         ALU.mult, ALU.add),
                               wait=[("ss", t4e)], inc="sv")
                    cbm = P.mm(ps_cp[:], ones_row_f[:], cpv[:], True, True,
                               wait=[("sv", cpe)], inc="st")
                    for c in range(4):
                        c2 = c + 4
                        oc1 = out_q[:, c * 512:(c + 1) * 512]
                        oc2 = out_q[:, c2 * 512:(c2 + 1) * 512]
                        P.op("vector", lambda v, a=oc1: v.tensor_copy(tmpc[:], a),
                             wait=[("st", cbm)])
                        P.op("vector", lambda v, a=oc2: v.tensor_mul(tmc2[:],
                                                                     ps_cp[:],
                                                                     a))
                        if tw_sgn[c] > 0:
                            P.op("vector",
                                 lambda v, a=oc1: v.tensor_add(a, a, tmc2[:]))
                        else:
                            P.op("vector",
                                 lambda v, a=oc1: v.tensor_sub(a, a, tmc2[:]))
                        P.op("vector", lambda v: v.tensor_mul(tmc2[:], ps_cp[:],
                                                              tmpc[:]))
                        if tw_sgn[c2] > 0:
                            tws = P.op("vector",
                                       lambda v, a=oc2: v.tensor_add(a, a,
                                                                     tmc2[:]),
                                       inc="sv")
                        else:
                            tws = P.op("vector",
                                       lambda v, a=oc2: v.tensor_sub(a, a,
                                                                     tmc2[:]),
                                       inc="sv")
                    if cyc < 2:
                        for c in range(NCHUNK):
                            tws = P.op(
                                "vector",
                                lambda v, d=outb_q[:, c * 512:(c + 1) * 512],
                                s=out_q[:, c * 512:(c + 1) * 512]:
                                v.tensor_copy(d, s),
                                wait=[("sv", tws)], inc="sv")
                if stage == "cyc0" and Q == 0:
                    return dump(out_q[:], 8 * 512, [("sv", tws)])
                xsem = "dxa" if Q % 2 == 0 else "dxb"
                for c in range(NCHUNK):
                    par = c % 2
                    w = [("sv", tws), (xsem, xf_w[Q])]
                    if ostore[par]:
                        w.append((osem[par], ostore[par]))
                    sge = prev_q_done = P.op(
                        "scalar",
                        lambda s, p=par, s_=out_q[:, c * 512:(c + 1) * 512]:
                        s.activation(gf_scr[p][:], s_, AF.Silu),
                        wait=w, inc="ss")
                    P.op("vector",
                         lambda v, p=par, c_=c, qo=qoff: v.tensor_mul(
                             fm_scr[:], gf_scr[p][:],
                             sacc[:, c_ * T + qo:c_ * T + qo + 512]),
                         wait=[("ss", sge)])
                    fin = P.op("vector",
                               lambda v, p=par, c_=c: v.tensor_add(
                                   gf_scr[p][:], fm_scr[:],
                                   xf_all[:, c_ * 512:(c_ + 1) * 512]),
                               inc="sv")
                    prev_fin = fin
                    ostore[par] = P.load(
                        out_d.ap()[c * 128:(c + 1) * 128, qoff:qoff + 512],
                        gf_scr[par][:], wait=[("sv", fin)], sem=osem[par])
        P.end()

    P.begin()
    P.op("sync", wait=[("dsa", P.cnt["dsa"]), ("dsb", P.cnt["dsb"]),
                       ("dl", P.now("dl")), ("dg", P.now("dg"))])
    P.end()
    P.stack.close()
    return nc


def _in_maps(inputs):
    C = _host_constants()
    x = np.ascontiguousarray(np.asarray(inputs["L0_attn_hidden_states"],
                                        dtype=np.float32))
    wqkv1p = np.ascontiguousarray(
        np.asarray(inputs["L1_attn_Wqkv"], dtype=np.float32)[:, C["l1perm"]])
    bqkv1p = np.ascontiguousarray(
        np.asarray(inputs["L1_attn_bqkv"], dtype=np.float32)[C["l1perm"]])
    common = {}
    for k in ("L0_attn_w_qkv", "L0_attn_b_qkv", "L0_attn_w_g", "L0_attn_b_g",
              "L0_attn_w_p", "L0_attn_b_p", "L0_ffn_W1", "L0_ffn_b1",
              "L0_ffn_quantum_phases", "L0_ffn_quantum_weights_real",
              "L0_ffn_quantum_weights_imag", "L0_ffn_W2", "L0_ffn_b2",
              "L1_attn_Wout", "L1_attn_bout",
              "L1_ffn_cn_w1", "L1_ffn_cn_b1", "L1_ffn_cn_w2", "L1_ffn_cn_b2",
              "L1_ffn_cn_w3", "L1_ffn_cn_b3", "L1_ffn_cn_w4", "L1_ffn_cn_b4",
              "L1_ffn_gc_w1", "L1_ffn_gc_b1", "L1_ffn_gc_w2", "L1_ffn_gc_b2",
              "L1_ffn_gc_w3", "L1_ffn_gc_b3", "L1_ffn_adaptive_range"):
        v = np.asarray(inputs[k], dtype=np.float32)
        common[k] = np.ascontiguousarray(v.reshape(v.shape if v.ndim > 0 else (1,)))
    common["L1_ffn_cn_w4"] = common["L1_ffn_cn_w4"].reshape(256, 1)
    common["L1_ffn_gc_w3"] = common["L1_ffn_gc_w3"].reshape(256, 1)
    common["L1_ffn_cn_b4"] = common["L1_ffn_cn_b4"].reshape(1)
    common["L1_ffn_gc_b3"] = common["L1_ffn_gc_b3"].reshape(1)
    common["L1_ffn_adaptive_range"] = common["L1_ffn_adaptive_range"].reshape(1)
    common["l1_wqkv_perm"] = wqkv1p
    common["l1_bqkv_perm"] = bqkv1p

    def pg(v, g):
        return np.ascontiguousarray(np.asarray(v, dtype=np.float32)
                                    .reshape(g, 128).T)
    common["b_qkv0_pg"] = pg(inputs["L0_attn_b_qkv"], 32)
    common["bg0_pg"] = pg(inputs["L0_attn_b_g"], 16)
    common["bp0_pg"] = pg(inputs["L0_attn_b_p"], 8)
    common["b1_pg"] = pg(inputs["L0_ffn_b1"], 32)
    common["b2_pg"] = pg(inputs["L0_ffn_b2"], 8)
    common["qph_pg"] = pg(inputs["L0_ffn_quantum_phases"], 32)
    common["qwr_pg"] = pg(inputs["L0_ffn_quantum_weights_real"], 32)
    common["qwi_pg"] = pg(inputs["L0_ffn_quantum_weights_imag"], 32)
    common["b_qkv1_pg"] = pg(bqkv1p, 24)
    common["bout1_pg"] = pg(inputs["L1_attn_bout"], 8)
    common["cnb1_pg"] = pg(inputs["L1_ffn_cn_b1"], 8)
    common["cnb2_pg"] = pg(inputs["L1_ffn_cn_b2"], 4)
    common["cnb3_pg"] = pg(inputs["L1_ffn_cn_b3"], 2)
    common["gcb1_pg"] = pg(inputs["L1_ffn_gc_b1"], 4)
    common["gcb2_pg"] = pg(inputs["L1_ffn_gc_b2"], 2)
    common["cnw4_pg"] = pg(inputs["L1_ffn_cn_w4"], 2)
    common["gcw3_pg"] = pg(inputs["L1_ffn_gc_w3"], 2)
    for k in ("tab_aq", "tab_bq", "tab_ak", "tab_bk", "permT", "permT_f32",
              "trimask", "selab", "sel16"):
        common[k] = C[k]
    maps = []
    for i in range(8):
        m = dict(common)
        m["x_fmaj"] = np.ascontiguousarray(x[i].T)
        maps.append(m)
    return maps


def kernel(**inputs):
    nc = build_nc("full")
    maps = _in_maps(inputs)
    res = run_bass_kernel_spmd(nc, maps, core_ids=list(range(8)))
    return np.stack([np.asarray(r["out_fmaj"], dtype=np.float32).T
                     for r in res.results], axis=0)


# version 10
# speedup vs baseline: 1.0697x; 1.0697x over previous
import math
import numpy as np
import ml_dtypes
from contextlib import ExitStack

import concourse.bass as bass
import concourse.mybir as mybir
from concourse.bass_utils import run_bass_kernel_spmd

F32 = mybir.dt.float32
BF16 = mybir.dt.bfloat16
AF = mybir.ActivationFunctionType
ALU = mybir.AluOpType
AX = mybir.AxisListType.X

T = 2048
D = 1024
NCHUNK = 8
H0 = 8
DK0, DV0 = 128, 256
DFF = 4096
PI = math.pi
MAGIC = 12582912.0
ENGINES = ("sync", "gpsimd", "tensor", "scalar", "vector")
SEMS = ("dl", "dg", "dwa", "dwb", "dxa", "dxb", "dsa", "dsb", "st", "ss", "sv")
DMA_SEMS = {"dl", "dg", "dwa", "dwb", "dxa", "dxb", "dsa", "dsb"}

_CONSTS = None


def _host_constants():
    global _CONSTS
    if _CONSTS is not None:
        return _CONSTS
    half = D // 2
    c = np.arange(half, dtype=np.float64)
    scale_vec = (2 * c + 0.4 * D) / (1.4 * D)
    pos = np.arange(T, dtype=np.float64)
    scale = scale_vec[None, :] ** (pos[:, None] / 512.0)
    inv_freq = 1.0 / 10000.0 ** (c / half)
    ang = pos[:, None] * inv_freq[None, :]
    sin_q = np.repeat(np.sin(ang) * scale, 2, axis=-1).T
    cos_q = np.repeat(np.cos(ang) * scale, 2, axis=-1).T
    sin_k = np.repeat(np.sin(ang) / scale, 2, axis=-1).T
    cos_k = np.repeat(np.cos(ang) / scale, 2, axis=-1).T
    gamma = 1.0 - np.exp(np.linspace(np.log(1 / 32.0), np.log(1 / 512.0), H0))
    qfac = np.zeros((H0, T))
    kfac = np.zeros((H0, T))
    for h in range(H0):
        g = gamma[h]
        rnorm = np.sqrt((1.0 - g ** (pos + 1)) / (1.0 - g))
        qfac[h] = g ** pos / rnorm * DK0 ** -0.5
        kfac[h] = g ** (-pos)
    fh = np.arange(D) // DK0
    bf = ml_dtypes.bfloat16
    f3 = np.float32
    tab_aq = np.ascontiguousarray((cos_q * qfac[fh, :]).astype(f3))
    tab_bq = np.ascontiguousarray((sin_q * qfac[fh, :]).astype(f3))
    tab_ak = np.ascontiguousarray((cos_k * kfac[fh, :]).astype(f3))
    tab_bk = np.ascontiguousarray((sin_k * kfac[fh, :]).astype(f3))
    P = np.zeros((128, 128), dtype=np.float64)
    for i in range(64):
        P[2 * i, 2 * i + 1] = -1.0
        P[2 * i + 1, 2 * i] = 1.0
    permT = np.ascontiguousarray(P.T.astype(bf))
    permT_f32 = np.ascontiguousarray(P.T.astype(np.float32))
    tri = (np.arange(128)[None, :] >= np.arange(128)[:, None])
    trimask = np.ascontiguousarray(tri.astype(np.float32))
    selab = np.zeros((2, 128), dtype=np.float32)
    selab[0, :64] = 1.0
    selab[1, 64:] = 1.0
    sel16 = np.zeros((16, 2048), dtype=ml_dtypes.bfloat16)
    for j in range(16):
        sel16[j, j * 128:(j + 1) * 128] = 1.0
    perm = []
    for off in (0, 64, 128):
        for h in range(16):
            perm.extend(range(h * 192 + off, h * 192 + off + 64))
    perm = np.asarray(perm, dtype=np.int64)
    _CONSTS = dict(tab_aq=tab_aq, tab_bq=tab_bq, tab_ak=tab_ak, tab_bk=tab_bk,
                   permT=permT, permT_f32=permT_f32, trimask=trimask,
                   selab=selab, sel16=sel16, l1perm=perm)
    return _CONSTS


class Prog:

    def __init__(self):
        self.nc = bass.Bass("TRN2", target_bir_lowering=False)
        self.stack = ExitStack()
        self.sems = {n: self.stack.enter_context(self.nc.semaphore(n))
                     for n in SEMS}
        self.cnt = {n: 0 for n in SEMS}
        self.ops = None
        self.last_wait = None

    def begin(self):
        assert self.ops is None
        self.ops = {e: [] for e in ENGINES}
        self.last_wait = {e: {} for e in ENGINES}

    def end(self):
        ops, self.ops = self.ops, None
        nonempty = [e for e in ENGINES if ops[e]]
        if not nonempty:
            return
        with self.nc.Block() as b:
            for e in nonempty:
                def runner(eng, _lst=ops[e]):
                    for g in _lst:
                        g(eng)
                getattr(b, e)(runner)

    def op(self, engine, fn=None, wait=(), inc=None, inc_n=1):
        sems = self.sems
        waits = []
        for s, v in wait:
            if not v:
                continue
            assert v <= self.cnt[s], (s, v, self.cnt[s])
            if self.last_wait[engine].get(s, -1) >= v:
                continue
            self.last_wait[engine][s] = v
            waits.append((s, v))

        def g(eng):
            for s, v in waits:
                eng.wait_ge(sems[s], v)
            if fn is not None:
                instr = fn(eng)
                if inc is not None:
                    instr.then_inc(sems[inc], inc_n)
        self.ops[engine].append(g)
        if fn is not None and inc is not None:
            self.cnt[inc] += inc_n
        return self.cnt[inc] if inc is not None else None

    def now(self, sem):
        return self.cnt[sem]

    def vseq(self, fn, wait=()):
        w = list(wait) + [("sv", self.cnt["sv"])]
        return self.op("vector", fn, wait=w, inc="sv")

    def sseq(self, fn, wait=()):
        w = list(wait) + [("ss", self.cnt["ss"])]
        return self.op("scalar", fn, wait=w, inc="ss")

    def at(self, sem):
        return (sem, self.cnt[sem])

    def load(self, dst, src, wait=(), sem="dl", queue=None):
        q = queue or ("gpsimd" if sem in ("dg",) else "sync")
        return self.op(q, lambda e: e.dma_start(dst, src), wait=wait,
                       inc=sem, inc_n=16)

    def loadc(self, dst, src, wait=(), sem="dg"):
        return self.op("gpsimd", lambda e: e.dma_start(dst, src), wait=wait,
                       inc=sem, inc_n=16)

    def mm(self, dst, lhsT, rhs, start, stop, wait=(), inc=None):
        return self.op("tensor",
                       lambda t: t.matmul(dst, lhsT, rhs, start=start, stop=stop),
                       wait=wait, inc=inc)


def col_block_ap(w, c0, c1):
    return w.ap()[:, c0:c1].rearrange("(kc p) m -> p kc m", p=128)


def sb3(t, kchunks):
    return t[:].rearrange("p (kc m) -> p kc m", kc=kchunks)


def vec_pg(v):
    return v.ap().rearrange("(g p) -> p g", p=128)


def vec_pm1(v):
    return v.ap().rearrange("(m p) -> p m one", p=128, one=1)


def row1(v, a, b):
    return v.ap()[a:b].rearrange("(one n) -> one n", one=1)


def colv(v, a, b):
    return v.ap()[a:b].rearrange("(p one) -> p one", one=1)


def build_nc(stage="full"):
    C = _host_constants()
    P = Prog()
    nc = P.nc
    st = P.stack

    def din(name, shape, dtype=F32):
        return nc.dram_tensor(name, shape, dtype, kind="ExternalInput")

    x_in = din("x_fmaj", [D, T])
    wqkv0 = din("L0_attn_w_qkv", [D, 4 * D])
    bqkv0 = din("L0_attn_b_qkv", [4 * D])
    wg0 = din("L0_attn_w_g", [D, 2 * D])
    bg0 = din("L0_attn_b_g", [2 * D])
    wp0 = din("L0_attn_w_p", [2 * D, D])
    bp0 = din("L0_attn_b_p", [D])
    w1 = din("L0_ffn_W1", [D, DFF])
    b1 = din("L0_ffn_b1", [DFF])
    qph = din("L0_ffn_quantum_phases", [DFF])
    qwr = din("L0_ffn_quantum_weights_real", [DFF])
    qwi = din("L0_ffn_quantum_weights_imag", [DFF])
    w2 = din("L0_ffn_W2", [DFF, D])
    b2 = din("L0_ffn_b2", [D])
    wqkv1 = din("l1_wqkv_perm", [D, 3 * D])
    bqkv1 = din("l1_bqkv_perm", [3 * D])
    wout1 = din("L1_attn_Wout", [D, D])
    bout1 = din("L1_attn_bout", [D])
    cnw = [din(f"L1_ffn_cn_w{i}", s) for i, s in
           zip(range(1, 5), ([D, D], [D, D // 2], [D // 2, D // 4], [D // 4, 1]))]
    cnb = [din(f"L1_ffn_cn_b{i}", [s]) for i, s in
           zip(range(1, 5), (D, D // 2, D // 4, 1))]
    gcw = [din(f"L1_ffn_gc_w{i}", s) for i, s in
           zip(range(1, 4), ([D, D // 2], [D // 2, D // 4], [D // 4, 1]))]
    gcb = [din(f"L1_ffn_gc_b{i}", [s]) for i, s in
           zip(range(1, 4), (D // 2, D // 4, 1))]
    arange_in = din("L1_ffn_adaptive_range", [1])
    bqkv0_pg = din("b_qkv0_pg", [128, 32])
    bg0_pg = din("bg0_pg", [128, 16])
    bp0_pg = din("bp0_pg", [128, 8])
    b1_pg = din("b1_pg", [128, 32])
    b2_pg = din("b2_pg", [128, 8])
    qph_pg = din("qph_pg", [128, 32])
    qwr_pg = din("qwr_pg", [128, 32])
    qwi_pg = din("qwi_pg", [128, 32])
    bqkv1_pg = din("b_qkv1_pg", [128, 24])
    bout1_pg = din("bout1_pg", [128, 8])
    cnb_pg = [din(f"cnb{i}_pg", [128, g]) for i, g in ((1, 8), (2, 4), (3, 2))]
    cnw4_pg = din("cnw4_pg", [128, 2])
    gcw3_pg = din("gcw3_pg", [128, 2])
    gcb_pg = [din(f"gcb{i}_pg", [128, g]) for i, g in ((1, 4), (2, 2))]
    tab_d = {n: din(n, [D, T], F32)
             for n in ("tab_aq", "tab_bq", "tab_ak", "tab_bk")}
    permT_d = din("permT", [128, 128], BF16)
    permTf_d = din("permT_f32", [128, 128], F32)
    trimask_d = din("trimask", [128, 128], F32)
    selab_d = din("selab", [2, 128], F32)
    sel16_d = din("sel16", [16, 2048], BF16)

    out_d = nc.dram_tensor("out_fmaj", [D, T], F32, kind="ExternalOutput")
    x1_dram = nc.dram_tensor("x1_scratch", [D, T], F32)

    dbg_d = None
    if stage != "full":
        dbg_d = nc.dram_tensor("dbg", [128, 16 * T], F32, kind="ExternalOutput")

    def dump(buf, cols, wait):
        if P.ops is not None:
            P.end()
        P.begin()
        nparts = buf.shape[0]
        dw = P.op("gpsimd",
                  lambda e: e.dma_start(dbg_d.ap()[0:nparts, 0:cols],
                                        buf[:, 0:cols]),
                  wait=wait, inc="dg", inc_n=16)
        P.op("gpsimd", wait=[("dg", dw)])
        P.end()
        return nc

    xn_bf = st.enter_context(nc.sbuf_tensor("xn_bf", [128, NCHUNK * T], BF16))
    ones_bf = st.enter_context(nc.sbuf_tensor("ones_bf", [128, 1], BF16))
    ones_row_bf = st.enter_context(nc.sbuf_tensor("ones_row_bf", [1, 128], BF16))
    ones_row_f = st.enter_context(nc.sbuf_tensor("ones_row_f", [1, 128], F32))
    permT_sb = st.enter_context(nc.sbuf_tensor("permT_sb", [128, 128], BF16))
    permTf_sb = st.enter_context(nc.sbuf_tensor("permTf_sb", [128, 128], F32))
    xn0f = st.enter_context(nc.sbuf_tensor("xn0f", [128, 8], F32))
    trimask_sb = st.enter_context(nc.sbuf_tensor("trimask_sb", [128, 128], F32))
    selab_sb = st.enter_context(nc.sbuf_tensor("selab_sb", [2, 128], F32))
    bqkv0_sb = st.enter_context(nc.sbuf_tensor("bqkv0_sb", [128, 32], F32))
    neg10_sb = st.enter_context(nc.sbuf_tensor("neg10_sb", [128, 1], F32))
    bqkv1_sb = st.enter_context(nc.sbuf_tensor("bqkv1_sb", [128, 24], F32))

    with ExitStack() as sa:
        x_ph = sa.enter_context(nc.sbuf_tensor("x_ph", [128, NCHUNK * T], F32))
        sq_scr = sa.enter_context(nc.sbuf_tensor("sq_scr", [128, 2 * T], BF16))
        stv = sa.enter_context(nc.sbuf_tensor("stv", [1, 6 * T], F32))
        t_scr = sa.enter_context(nc.sbuf_tensor("t_scrA", [128, T], F32))

        P.begin()
        P.load(permT_sb[:], permT_d.ap())
        P.load(permTf_sb[:], permTf_d.ap())
        P.load(trimask_sb[:], trimask_d.ap())
        P.load(selab_sb[:], selab_d.ap())
        P.load(bqkv0_sb[:], bqkv0_pg.ap())
        P.load(bqkv1_sb[:], bqkv1_pg.ap())
        P.op("vector", lambda v: v.memset(ones_bf[:], 1.0))
        P.op("vector", lambda v: v.memset(ones_row_bf[:], 1.0))
        P.op("vector", lambda v: v.memset(ones_row_f[:], 1.0))
        P.op("vector", lambda v: v.memset(neg10_sb[:], -10.0))
        for c in range(NCHUNK):
            P.load(x_ph[:, c * T:(c + 1) * T], x_in.ap()[c * 128:(c + 1) * 128, :])
        xall = P.now("dl")
        with ExitStack() as sp:
            ps_sum = sp.enter_context(nc.psum_tensor("ps_sum", [1, T], F32))
            ps_sq = sp.enter_context(nc.psum_tensor("ps_sq", [1, T], F32))
            stmm = [0] * NCHUNK
            for c in range(NCHUNK):
                xc = xn_bf[:, c * T:(c + 1) * T]
                sqc = sq_scr[:, (c % 2) * T:(c % 2 + 1) * T]
                cv = P.op("vector",
                          lambda v, d=xc, s=x_ph[:, c * T:(c + 1) * T]:
                          v.tensor_copy(d, s),
                          wait=[("dl", xall)], inc="sv")
                w = [("sv", cv)]
                if c >= 2:
                    w.append(("st", stmm[c - 2]))
                cs = P.op("scalar",
                          lambda s, d=sqc, i=xc: s.activation(d, i, AF.Square),
                          wait=w, inc="ss")
                for qc in range(4):
                    P.mm(ps_sum[0:1, qc * 512:(qc + 1) * 512], ones_bf[:],
                         xc[:, qc * 512:(qc + 1) * 512],
                         start=(c == 0), stop=(c == NCHUNK - 1),
                         wait=[("sv", cv)])
                for qc in range(4):
                    stmm[c] = P.mm(ps_sq[0:1, qc * 512:(qc + 1) * 512], ones_bf[:],
                                   sqc[:, qc * 512:(qc + 1) * 512],
                                   start=(c == 0), stop=(c == NCHUNK - 1),
                                   wait=[("ss", cs)], inc="st")
            P.op("vector", lambda v: v.tensor_copy(stv[0:1, 0:T], ps_sum[:]),
                 wait=[("st", P.now("st"))])
            P.op("vector", lambda v: v.tensor_copy(stv[0:1, T:2 * T], ps_sq[:]),
                 inc="sv")
        mu = stv[0:1, 2 * T:3 * T]
        varp = stv[0:1, 3 * T:4 * T]
        aa = stv[0:1, 4 * T:5 * T]
        bb = stv[0:1, 5 * T:6 * T]
        P.op("vector", lambda v: v.tensor_scalar(mu, stv[0:1, 0:T], 1.0 / D, None,
                                                 ALU.mult))
        P.op("vector", lambda v: v.tensor_scalar(varp, stv[0:1, T:2 * T], 1.0 / D,
                                                 1e-5, ALU.mult, ALU.add))
        P.op("vector", lambda v: v.scalar_tensor_tensor(t_scr[0:1, 0:T], mu, -1.0,
                                                        mu, ALU.mult, ALU.mult))
        vst = P.op("vector", lambda v: v.tensor_add(varp, varp, t_scr[0:1, 0:T]),
                   inc="sv")
        P.op("scalar", lambda s: s.activation(varp, varp, AF.Ln),
             wait=[("sv", vst)])
        sst = P.op("scalar", lambda s: s.activation(aa, varp, AF.Exp, scale=-0.5),
                   inc="ss")
        vb = P.op("vector", lambda v: v.scalar_tensor_tensor(bb, mu, -1.0, aa,
                                                             ALU.mult, ALU.mult),
                  wait=[("ss", sst)], inc="sv")
        with ExitStack() as sp:
            ps_a = sp.enter_context(nc.psum_tensor("ps_a", [128, T], F32))
            ps_b = sp.enter_context(nc.psum_tensor("ps_b", [128, T], F32))
            bcw = 0
            for qc in range(4):
                sl = slice(qc * 512, (qc + 1) * 512)
                P.mm(ps_a[:, sl], ones_row_f[:], aa[:, sl], True, True,
                     wait=[("sv", vb)])
                bcw = P.mm(ps_b[:, sl], ones_row_f[:], bb[:, sl], True, True,
                           inc="st")
            for c in range(NCHUNK):
                P.op("vector",
                     lambda v, a=x_ph[:, c * T:(c + 1) * T]:
                     v.tensor_mul(t_scr[:], a, ps_a[:]),
                     wait=[("st", bcw)])
                P.op("vector",
                     lambda v, d=xn_bf[:, c * T:(c + 1) * T]:
                     v.tensor_add(d, t_scr[:], ps_b[:]),
                     inc="sv")
                P.vseq(lambda v, c_=c: v.tensor_add(xn0f[:, c_:c_ + 1],
                                                    t_scr[:, 0:1],
                                                    ps_b[:, 0:1]))
        P.end()

    if stage == "xn0":
        return dump(xn_bf[:], 8 * T, [("sv", P.now("sv"))])

    retg = None
    with ExitStack() as sl0:
        retg = sl0.enter_context(nc.sbuf_tensor("retg", [128, 16 * T], BF16))
        gnab = sl0.enter_context(nc.sbuf_tensor("gnab", [16, T], BF16))
        sl0h = ExitStack()
        wq_sb = sl0h.enter_context(nc.sbuf_tensor("wq_sb", [128, 8 * 128], BF16))
        wk_sb = sl0h.enter_context(nc.sbuf_tensor("wk_sb", [128, 8 * 128], BF16))
        wv_sb = sl0h.enter_context(nc.sbuf_tensor("wv_sb", [128, 8 * 256], BF16))
        bv_sb = sl0h.enter_context(nc.sbuf_tensor("bv_sb", [1, 256], BF16))
        tabA = sl0h.enter_context(nc.sbuf_tensor("tabA", [128, T], F32))
        tabB = sl0h.enter_context(nc.sbuf_tensor("tabB", [128, T], F32))
        qk_bf = sl0h.enter_context(nc.sbuf_tensor("qk_bf", [128, T], BF16))
        qt_sb = sl0h.enter_context(nc.sbuf_tensor("qt_sb", [128, T], BF16))
        kt_sb = sl0h.enter_context(nc.sbuf_tensor("kt_sb", [128, T], BF16))
        qtf = sl0h.enter_context(nc.sbuf_tensor("qtf", [128, T], F32))
        ktf = sl0h.enter_context(nc.sbuf_tensor("ktf", [128, T], F32))
        wqf_sb = sl0h.enter_context(nc.sbuf_tensor("wqf_sb", [128, 8 * 128], F32))
        q0s = sl0h.enter_context(nc.sbuf_tensor("q0s", [128, 4], F32))
        t1s = sl0h.enter_context(nc.sbuf_tensor("t1s", [128, T], F32))
        v_sb = sl0h.enter_context(nc.sbuf_tensor("v_sb", [128, 16 * 256], BF16))
        pnow = [sl0h.enter_context(nc.sbuf_tensor(f"pnow{i}", [128, 1024], BF16))
                for i in range(2)]
        pstore = sl0h.enter_context(nc.sbuf_tensor("pstore", [128, 8 * 1024], BF16))
        sq2 = sl0h.enter_context(nc.sbuf_tensor("sq2", [128, 1024], BF16))
        gn_scr = sl0h.enter_context(nc.sbuf_tensor("gn_scr", [1, 4 * 512], F32))

        ps_gn = sl0.enter_context(nc.psum_tensor("ps_gn_l0", [1, 512], F32))
        P.begin()
        st_state = {}
        gevp = 0
        gn_free = 0
        dma_free = 0

        stats_done_st = 0
        stats_queue = []

        def flush_stats(n=1):
            for _ in range(min(n, len(stats_queue))):
                stats_queue.pop(0)()

        def queue_gnstats(hh, gevs):
            for qh_ in range(2):
                for qc_ in range(2):
                    stats_queue.append(
                        lambda hh_=hh, g_=tuple(gevs), a_=qh_, b_=qc_:
                        emit_gnstat_unit(hh_, g_, a_, b_))

        def emit_gnstat_unit(hh, gevs, qh, qc):
            nonlocal gn_free, dma_free, stats_done_st
            vcb = 2 * hh
            if True:
                q0h = qh * 1024
                if True:
                    gmu = gn_scr[0:1, 0:512]
                    gva = gn_scr[0:1, 512:1024]
                    gtmp = gn_scr[0:1, 1024:1536]
                    gb_ = gn_scr[0:1, 1536:2048]
                    gmm = 0
                    sqw = 0
                    for dvc in range(2):
                        rsl = retg[:, (vcb + dvc) * T + q0h + qc * 512:
                                   (vcb + dvc) * T + q0h + (qc + 1) * 512]
                        wst = [("sv", gevs[qh])]
                        if gn_free:
                            wst.append(("sv", gn_free))
                        gmm = P.mm(ps_gn[0:1, 0:512], ones_bf[:], rsl,
                                   start=(dvc == 0), stop=(dvc == 1), wait=wst,
                                   inc="st" if dvc == 1 else None)
                        wsq = [("sv", gevs[qh])]
                        if stats_done_st:
                            wsq.append(("st", stats_done_st))
                        sqw = P.op("scalar",
                                   lambda s,
                                   d=sq2[:, dvc * 512:(dvc + 1) * 512],
                                   i=rsl: s.activation(d, i, AF.Square),
                                   wait=wsq, inc="ss")
                    dw = [("st", gmm)]
                    if dma_free:
                        dw.append(("dg", dma_free))
                    gmur = P.vseq(lambda v: v.tensor_scalar(gmu,
                                                            ps_gn[0:1, 0:512],
                                                            1.0 / DV0, None,
                                                            ALU.mult), wait=dw)
                    for dvc in range(2):
                        gmm = P.mm(ps_gn[0:1, 0:512], ones_bf[:],
                                   sq2[:, dvc * 512:(dvc + 1) * 512],
                                   start=(dvc == 0), stop=(dvc == 1),
                                   wait=[("ss", sqw), ("sv", gmur)],
                                   inc="st" if dvc == 1 else None)
                    stats_done_st = gmm
                    P.vseq(lambda v: v.tensor_scalar(gva, ps_gn[0:1, 0:512],
                                                     1.0 / DV0, 1e-5, ALU.mult,
                                                     ALU.add),
                           wait=[("st", gmm)])
                    P.vseq(lambda v: v.scalar_tensor_tensor(gtmp, gmu, -1.0,
                                                            gmu, ALU.mult,
                                                            ALU.mult))
                    gn_free = P.vseq(lambda v: v.tensor_add(gva, gva, gtmp))
                    P.sseq(lambda s: s.activation(gva, gva, AF.Ln),
                           wait=[("sv", gn_free)])
                    gns = P.sseq(lambda s: s.activation(gva, gva, AF.Exp,
                                                        scale=-0.5))
                    gnb = P.vseq(lambda v: v.scalar_tensor_tensor(
                        gb_, gmu, -1.0, gva, ALU.mult, ALU.mult),
                        wait=[("ss", gns)])
                    col = q0h + qc * 512
                    P.loadc(gnab[hh:hh + 1, col:col + 512], gva,
                            wait=[("sv", gnb)])
                    dma_free = P.loadc(gnab[8 + hh:9 + hh, col:col + 512], gb_)

        for h in range(H0):
            S = st_state
            P.load(tabA[:], tab_d["tab_aq"].ap()[h * 128:(h + 1) * 128, :],
                   wait=[("sv", S.get("vk0", 0))])
            P.load(tabB[:], tab_d["tab_bq"].ap()[h * 128:(h + 1) * 128, :])
            P.load(sb3(wqf_sb, 8), col_block_ap(wqkv0, h * 128, (h + 1) * 128),
                   wait=[("st", S.get("t0k_mm", 0))])
            ldl = P.now("dl")
            P.loadc(sb3(wq_sb, 8), col_block_ap(wqkv0, h * 128, (h + 1) * 128),
                    wait=[("st", S.get("qkv_end", 0))])
            P.loadc(sb3(wk_sb, 8),
                    col_block_ap(wqkv0, D + h * 128, D + (h + 1) * 128))
            P.loadc(sb3(wv_sb, 8),
                    col_block_ap(wqkv0, 2 * D + h * 256, 2 * D + (h + 1) * 256))
            P.loadc(bv_sb[:], row1(bqkv0, 2 * D + h * 256, 2 * D + (h + 1) * 256))
            ldg = P.now("dg")

            with ExitStack() as sp:
                ps_qk = sp.enter_context(nc.psum_tensor(f"ps_qk{h}", [128, T],
                                                        F32))
                ps_v = [sp.enter_context(nc.psum_tensor(f"ps_v{h}_{i}",
                                                        [128, 256], F32))
                        for i in range(2)]
                ps_t0 = sp.enter_context(nc.psum_tensor(f"ps_t0{h}", [128, 4],
                                                        F32))
                vev = [0, 0]

                def vtile(tt):
                    par = tt % 2
                    w = [("dg", ldg)]
                    if vev[par]:
                        w.append(("sv", vev[par]))
                    for kc in range(8):
                        P.mm(ps_v[par][:],
                             xn_bf[:, kc * T + tt * 128:kc * T + (tt + 1) * 128],
                             wv_sb[:, kc * 256:(kc + 1) * 256],
                             start=(kc == 0), stop=False, wait=w)
                    vmm = P.mm(ps_v[par][:], ones_row_bf[:], bv_sb[:],
                               start=False, stop=True, inc="st")
                    vev[par] = P.op(
                        "vector",
                        lambda v, d=v_sb[:, tt * 256:(tt + 1) * 256],
                        s=ps_v[par]: v.tensor_copy(d, s[:]),
                        wait=[("st", vmm)], inc="sv")

                qmm = 0
                for qn in range(4):
                    for kc in range(8):
                        qmm = P.mm(ps_qk[:, qn * 512:(qn + 1) * 512],
                                   wq_sb[:, kc * 128:(kc + 1) * 128],
                                   xn_bf[:, kc * T + qn * 512:kc * T + (qn + 1) * 512],
                                   start=(kc == 0), stop=(kc == 7),
                                   wait=[("dg", ldg), ("sv", gevp)],
                                   inc="st" if kc == 7 else None)
                flush_stats()
                eq = P.op("scalar",
                          lambda s, b=bqkv0_sb[:, h:h + 1], p=ps_qk:
                          s.activation(t1s[:], p[:], AF.Identity, bias=b),
                          wait=[("st", qmm), ("dl", ldl)], inc="ss")
                eqb = P.op("vector", lambda v: v.tensor_copy(qk_bf[:], t1s[:]),
                           wait=[("ss", eq)], inc="sv")
                vtile(0)
                vtile(1)
                rmm = 0
                for qn in range(4):
                    rmm = P.mm(ps_qk[:, qn * 512:(qn + 1) * 512], permT_sb[:],
                               qk_bf[:, qn * 512:(qn + 1) * 512], True, True,
                               wait=[("sv", eqb), ("ss", eq)],
                               inc="st" if qn == 3 else None)
                P.op("vector", lambda v: v.tensor_mul(t1s[:], t1s[:], tabA[:]),
                     wait=[("ss", eq), ("dl", ldl), ("sv", eqb)])
                P.op("vector",
                     lambda v, p=ps_qk: v.scalar_tensor_tensor(
                         qtf[:], p[:], 1.0, tabB[:], ALU.mult, ALU.mult),
                     wait=[("st", rmm)])
                P.op("vector", lambda v: v.tensor_add(qtf[:], t1s[:], qtf[:]))
                vq = P.op("vector", lambda v: v.tensor_copy(qt_sb[:], qtf[:]),
                          inc="sv")
                vtile(2)
                vtile(3)
                flush_stats()
                t0m = 0
                for kc in range(8):
                    t0m = P.mm(ps_t0[:, 0:1], wqf_sb[:, kc * 128:(kc + 1) * 128],
                               xn0f[:, kc:kc + 1], start=(kc == 0),
                               stop=(kc == 7), wait=[("dl", ldl)],
                               inc="st" if kc == 7 else None)
                t0e = P.sseq(lambda s, b=bqkv0_sb[:, h:h + 1], p=ps_t0:
                             s.activation(q0s[:, 0:1], p[:, 0:1],
                                          AF.Identity, bias=b),
                             wait=[("st", t0m)])
                t0r = P.mm(ps_t0[:, 1:2], permTf_sb[:], q0s[:, 0:1], True,
                           True, wait=[("ss", t0e)], inc="st")
                P.vseq(lambda v: v.tensor_mul(q0s[:, 1:2], q0s[:, 0:1],
                                              tabA[:, 0:1]),
                       wait=[("sv", vq), ("ss", t0e)])
                P.vseq(lambda v, p=ps_t0: v.scalar_tensor_tensor(
                    qtf[:, 0:1], p[:, 1:2], 1.0, tabB[:, 0:1], ALU.mult,
                    ALU.mult), wait=[("st", t0r)])
                vq0 = P.vseq(lambda v: v.tensor_add(qtf[:, 0:1], qtf[:, 0:1],
                                                    q0s[:, 1:2]))
                kmm = 0
                for qn in range(4):
                    for kc in range(8):
                        kmm = P.mm(ps_qk[:, qn * 512:(qn + 1) * 512],
                                   wk_sb[:, kc * 128:(kc + 1) * 128],
                                   xn_bf[:, kc * T + qn * 512:kc * T + (qn + 1) * 512],
                                   start=(kc == 0), stop=(kc == 7),
                                   wait=[("sv", vq)],
                                   inc="st" if kc == 7 else None)
                P.load(tabA[:], tab_d["tab_ak"].ap()[h * 128:(h + 1) * 128, :],
                       wait=[("sv", vq0)])
                P.load(tabB[:], tab_d["tab_bk"].ap()[h * 128:(h + 1) * 128, :])
                P.load(sb3(wqf_sb, 8),
                       col_block_ap(wqkv0, D + h * 128, D + (h + 1) * 128),
                       wait=[("ss", t0e)])
                ldk = P.now("dl")
                flush_stats()
                ek = P.op("scalar",
                          lambda s, b=bqkv0_sb[:, 8 + h:9 + h], p=ps_qk:
                          s.activation(t1s[:], p[:], AF.Identity, bias=b),
                          wait=[("st", kmm)], inc="ss")
                ekb = P.op("vector", lambda v: v.tensor_copy(qk_bf[:], t1s[:]),
                           wait=[("ss", ek)], inc="sv")
                vtile(4)
                vtile(5)
                rkm = 0
                for qn in range(4):
                    rkm = P.mm(ps_qk[:, qn * 512:(qn + 1) * 512], permT_sb[:],
                               qk_bf[:, qn * 512:(qn + 1) * 512], True, True,
                               wait=[("sv", ekb), ("ss", ek)],
                               inc="st" if qn == 3 else None)
                P.op("vector", lambda v: v.tensor_mul(t1s[:], t1s[:], tabA[:]),
                     wait=[("ss", ek), ("dl", ldk), ("sv", ekb)])
                P.op("vector",
                     lambda v, p=ps_qk: v.scalar_tensor_tensor(
                         ktf[:], p[:], 1.0, tabB[:], ALU.mult, ALU.mult),
                     wait=[("st", rkm)])
                P.op("vector", lambda v: v.tensor_add(ktf[:], t1s[:], ktf[:]))
                vk = P.op("vector", lambda v: v.tensor_copy(kt_sb[:], ktf[:]),
                          inc="sv")
                for tt in range(6, 12):
                    vtile(tt)
                flush_stats()
                t0m = 0
                for kc in range(8):
                    t0m = P.mm(ps_t0[:, 2:3], wqf_sb[:, kc * 128:(kc + 1) * 128],
                               xn0f[:, kc:kc + 1], start=(kc == 0),
                               stop=(kc == 7), wait=[("dl", ldk)],
                               inc="st" if kc == 7 else None)
                st_state = {"t0k_mm": t0m}
                t0e = P.sseq(lambda s, b=bqkv0_sb[:, 8 + h:9 + h], p=ps_t0:
                             s.activation(q0s[:, 2:3], p[:, 2:3],
                                          AF.Identity, bias=b),
                             wait=[("st", t0m)])
                t0r = P.mm(ps_t0[:, 3:4], permTf_sb[:], q0s[:, 2:3], True,
                           True, wait=[("ss", t0e)], inc="st")
                P.vseq(lambda v: v.tensor_mul(q0s[:, 3:4], q0s[:, 2:3],
                                              tabA[:, 0:1]),
                       wait=[("sv", vk), ("ss", t0e)])
                P.vseq(lambda v, p=ps_t0: v.scalar_tensor_tensor(
                    ktf[:, 0:1], p[:, 3:4], 1.0, tabB[:, 0:1], ALU.mult,
                    ALU.mult), wait=[("st", t0r)])
                vk0 = P.vseq(lambda v: v.tensor_add(ktf[:, 0:1], ktf[:, 0:1],
                                                    q0s[:, 3:4]))
                for tt in range(12, 16):
                    vtile(tt)
                st_state["vk0"] = vk0
                st_state["qkv_end"] = P.now("st")

            with ExitStack() as sp:
                ps_s = [sp.enter_context(nc.psum_tensor(f"ps_s{h}_{i}",
                                                        [128, 512], F32))
                        for i in range(2)]
                ps_av = sp.enter_context(nc.psum_tensor(f"ps_av{h}",
                                                        [128, 2048], F32))
                vready = P.now("sv")
                evv = [0, 0]
                evs = [0, 0]
                avdone = [0, 0]
                avprev = 0
                gevs = []
                for qh in range(2):
                    q0h = qh * 1024
                    lastk = 7 if qh == 0 else 15
                    for tk in range(lastk + 1):
                        par = tk % 2
                        k0 = tk * 128
                        if qh == 0 or tk >= 8:
                            spans = [(k0, k0 + 128, True)]
                            q = k0 + 128
                            while q < T:
                                qe = min(q + 512, T)
                                spans.append((q, qe, False))
                                q = qe
                            for (qa, qb2, isdiag) in spans:
                                w = [("sv", vready)]
                                if evv[par]:
                                    w.append(("sv", evv[par]))
                                if evs[par]:
                                    w.append(("ss", evs[par]))
                                if isdiag:
                                    if tk == 0:
                                        w.append(("sv", vk0))
                                    smm = P.mm(ps_s[par][:, 0:qb2 - qa],
                                               ktf[:, k0:k0 + 128],
                                               qtf[:, qa:qb2], True, True,
                                               wait=w, inc="st")
                                else:
                                    smm = P.mm(ps_s[par][:, 0:qb2 - qa],
                                               kt_sb[:, k0:k0 + 128],
                                               qt_sb[:, qa:qb2], True, True,
                                               wait=w, inc="st")
                                dsts = []
                                if qh == 0:
                                    if qa < 1024:
                                        bb2 = min(qb2, 1024)
                                        dsts.append((pnow[par][:, qa:bb2], qa,
                                                     bb2, True))
                                    if qb2 > 1024:
                                        aa2 = max(qa, 1024)
                                        dsts.append(
                                            (pstore[:, tk * 1024 + aa2 - 1024:
                                                     tk * 1024 + qb2 - 1024],
                                             aa2, qb2, False))
                                else:
                                    dsts.append(
                                        (pnow[par][:, qa - 1024:qb2 - 1024],
                                         qa, qb2, True))
                                for dst, a2, b2_, ispn in dsts:
                                    wv2 = [("st", smm)]
                                    if ispn and avdone[par]:
                                        wv2.append(("st", avdone[par]))
                                    if isdiag:
                                        evv[par] = P.op(
                                            "vector",
                                            lambda v, d=dst,
                                            s=ps_s[par][:, a2 - qa:b2_ - qa]:
                                            v.tensor_mul(d, s, trimask_sb[:]),
                                            wait=wv2, inc="sv")
                                    else:
                                        evs[par] = P.op(
                                            "scalar",
                                            lambda s_, d=dst,
                                            s=ps_s[par][:, a2 - qa:b2_ - qa]:
                                            s_.activation(d, s, AF.Identity),
                                            wait=wv2, inc="ss")
                        qa = max(k0, q0h)
                        qb2 = q0h + 1024
                        first = (tk == 0)
                        if qh == 0 or tk >= 8:
                            srcbuf, off = pnow[par], q0h
                        else:
                            srcbuf, off = pstore, 1024 - tk * 1024
                        w = [("sv", evv[par]), ("ss", evs[par])]
                        if first and gevp:
                            w.append(("sv", gevp))
                        q = qa
                        while q < qb2:
                            if q % 512:
                                qe = min(qb2, (q // 512 + 1) * 512)
                            else:
                                qe = min(q + 512, qb2)
                            for dvc in range(2):
                                avprev = P.mm(
                                    ps_av[:, dvc * 1024 + q - q0h:
                                          dvc * 1024 + qe - q0h],
                                    v_sb[:, tk * 256 + dvc * 128:
                                         tk * 256 + (dvc + 1) * 128],
                                    srcbuf[:, q - off:qe - off],
                                    start=first, stop=(tk == lastk), wait=w,
                                    inc="st")
                            q = qe
                        avdone[par] = avprev
                    vcb = 2 * h
                    gev = 0
                    for dvc in range(2):
                        gev = P.op(
                            "vector",
                            lambda v,
                            d=retg[:, (vcb + dvc) * T + q0h:
                                   (vcb + dvc) * T + q0h + 1024],
                            s=ps_av[:, dvc * 1024:(dvc + 1) * 1024]:
                            v.tensor_copy(d, s),
                            wait=[("st", avprev)], inc="sv")
                    gevp = gev
                    gevs.append(gev)
                queue_gnstats(h, gevs)
        flush_stats(99)
        P.end()
        sl0h.close()

        if stage == "ret0":
            return dump(retg[:], 16 * T, [("sv", P.now("sv"))])

        with ExitStack() as s2:
            wgf_sb = s2.enter_context(nc.sbuf_tensor("wgf_sb", [128, 8 * 2048],
                                                     BF16))
            bg_sb = s2.enter_context(nc.sbuf_tensor("bg_sb", [128, 16], F32))
            sel16_sb = s2.enter_context(nc.sbuf_tensor("sel16_sb", [16, 2048],
                                                       BF16))
            gate_scr = [s2.enter_context(nc.sbuf_tensor(f"gate{i}", [128, 512],
                                                        F32)) for i in range(2)]
            tp_scr = [s2.enter_context(nc.sbuf_tensor(f"tp{i}", [128, 512],
                                                      F32)) for i in range(2)]
            P.begin()
            P.loadc(sb3(wgf_sb, 8),
                    wg0.ap().rearrange("(kc p) m -> p kc m", p=128))
            gw = P.now("dg")
            P.load(bg_sb[:], bg0_pg.ap())
            P.load(sel16_sb[:], sel16_d.ap())
            bgw = P.now("dl")
            gnabw = P.now("dg")
            with ExitStack() as sp:
                ps_g = [sp.enter_context(nc.psum_tensor(f"ps_g{i}", [128, 512],
                                                        F32)) for i in range(2)]
                ps_bca = [sp.enter_context(nc.psum_tensor(f"ps_bca{i}",
                                                          [128, 512], F32))
                          for i in range(2)]
                ps_bcb = [sp.enter_context(nc.psum_tensor(f"ps_bcb{i}",
                                                          [128, 512], F32))
                          for i in range(2)]
                gev = [0, 0]
                i = 0
                for h in range(H0):
                    for dvc in range(2):
                        vc = 2 * h + dvc
                        for qn in range(4):
                            par = i % 2
                            i += 1
                            w = [("dg", gw), ("dg", gnabw), ("dl", bgw)]
                            if gev[par]:
                                w.append(("sv", gev[par]))
                            bca = P.mm(ps_bca[par][:],
                                       sel16_sb[:, h * 128:(h + 1) * 128],
                                       gnab[:, qn * 512:(qn + 1) * 512],
                                       True, True, wait=w, inc="st")
                            bcb = P.mm(ps_bcb[par][:],
                                       sel16_sb[:, (8 + h) * 128:(9 + h) * 128],
                                       gnab[:, qn * 512:(qn + 1) * 512],
                                       True, True, inc="st")
                            gm = 0
                            for kc in range(8):
                                gm = P.mm(ps_g[par][:],
                                          wgf_sb[:, kc * 2048 + vc * 128:
                                                 kc * 2048 + (vc + 1) * 128],
                                          xn_bf[:, kc * T + qn * 512:
                                                kc * T + (qn + 1) * 512],
                                          start=(kc == 0), stop=(kc == 7),
                                          inc="st" if kc == 7 else None)
                            sev = P.op("scalar",
                                       lambda s, d=gate_scr[par], p=ps_g[par],
                                       v_=vc: s.activation(
                                           d[:], p[:], AF.Silu,
                                           bias=bg_sb[:, v_:v_ + 1]),
                                       wait=[("st", gm)], inc="ss")
                            rsl = retg[:, vc * T + qn * 512:vc * T + (qn + 1) * 512]
                            P.op("vector",
                                 lambda v, r=rsl, p=par: v.tensor_mul(
                                     tp_scr[p][:], r, ps_bca[p][:]),
                                 wait=[("st", bcb)])
                            P.op("vector",
                                 lambda v, p=par: v.tensor_add(
                                     tp_scr[p][:], tp_scr[p][:], ps_bcb[p][:]))
                            gev[par] = P.op("vector",
                                            lambda v, r=rsl, p=par,
                                            g=gate_scr[par]:
                                            v.tensor_mul(r, tp_scr[p][:], g[:]),
                                            wait=[("ss", sev)], inc="sv")
            P.end()

        if stage == "gated":
            return dump(retg[:], 16 * T, [("sv", P.now("sv"))])

        sacc = st.enter_context(nc.sbuf_tensor("sacc", [128, NCHUNK * T], F32,
                                               side="right"))
        with ExitStack() as s3:
            wp_sb = [s3.enter_context(nc.sbuf_tensor(f"wp_sb{i}", [128, 16 * 128],
                                                     BF16)) for i in range(2)]
            bp_sb = s3.enter_context(nc.sbuf_tensor("bp_sb", [128, 8], F32))
            P.begin()
            P.load(bp_sb[:], bp0_pg.ap())
            bpw = P.now("dl")
            gatedone = P.now("sv")
            with ExitStack() as sp:
                ps_wp = [sp.enter_context(nc.psum_tensor(f"ps_wp{i}", [128, 512],
                                                         F32)) for i in range(2)]
                wsem = ["dwa", "dwb"]
                wl = [P.loadc(sb3(wp_sb[i], 16),
                              col_block_ap(wp0, i * 128, (i + 1) * 128),
                              sem=wsem[i]) for i in range(2)]
                wpev = [0, 0]
                for oc in range(8):
                    bpar = oc % 2
                    for qc in range(4):
                        par = (oc * 4 + qc) % 2
                        w = [(wsem[bpar], wl[bpar]), ("sv", gatedone)]
                        if wpev[par]:
                            w.append(("ss", wpev[par]))
                        pm = 0
                        for vc in range(16):
                            pm = P.mm(ps_wp[par][:],
                                      wp_sb[bpar][:, vc * 128:(vc + 1) * 128],
                                      retg[:, vc * T + qc * 512:
                                           vc * T + (qc + 1) * 512],
                                      start=(vc == 0), stop=(vc == 15), wait=w,
                                      inc="st" if vc == 15 else None)
                        wpev[par] = P.op(
                            "scalar",
                            lambda s, d=sacc[:, oc * T + qc * 512:
                                             oc * T + (qc + 1) * 512],
                            p=ps_wp[par], o=oc: s.activation(
                                d, p[:], AF.Identity, bias=bp_sb[:, o:o + 1]),
                            wait=[("st", pm), ("dl", bpw)], inc="ss")
                    if oc + 2 < 8:
                        wl[bpar] = P.loadc(sb3(wp_sb[bpar], 16),
                                           col_block_ap(wp0, (oc + 2) * 128,
                                                        (oc + 3) * 128),
                                           wait=[("st", P.now("st"))], sem=wsem[bpar])
            P.end()

    if stage == "s0":
        return dump(sacc[:], 8 * T, [("ss", P.now("ss"))])

    sln = ExitStack()
    lnst = sln.enter_context(nc.sbuf_tensor("lnst", [1, 2 * T], F32))
    with ExitStack() as s4:
        act = s4.enter_context(nc.sbuf_tensor("act", [128, 32 * 512], BF16))
        w1_sb = [s4.enter_context(nc.sbuf_tensor(f"w1_sb{i}", [128, 8 * 128],
                                                 BF16)) for i in range(2)]
        w2_sb = [s4.enter_context(nc.sbuf_tensor(f"w2_sb{i}", [128, 32 * 128],
                                                 BF16)) for i in range(2)]
        qv = s4.enter_context(nc.sbuf_tensor("qv", [128, 8 * 32], F32))
        b2_sb = s4.enter_context(nc.sbuf_tensor("b2_sb", [128, 8], F32))
        s01 = s4.enter_context(nc.sbuf_tensor("s01", [128, 512], F32))
        c01 = s4.enter_context(nc.sbuf_tensor("c01", [128, 512], F32))
        tqs = s4.enter_context(nc.sbuf_tensor("tqs", [128, 512], F32))
        g_scr = s4.enter_context(nc.sbuf_tensor("g_scr", [128, 512], F32))
        xi_scr = [s4.enter_context(nc.sbuf_tensor(f"xi{i}", [128, 512], F32))
                  for i in range(2)]
        x2_scr = [s4.enter_context(nc.sbuf_tensor(f"x2s{i}", [128, 512], F32))
                  for i in range(2)]
        x2b_scr = [s4.enter_context(nc.sbuf_tensor(f"x2b{i}", [128, 512], BF16))
                   for i in range(2)]
        sq3 = s4.enter_context(nc.sbuf_tensor("sq3", [128, 512], BF16))

        P.begin()
        P.load(qv[:, 5 * 32:6 * 32], qph_pg.ap())
        P.load(qv[:, 6 * 32:7 * 32], qwr_pg.ap())
        P.load(qv[:, 7 * 32:8 * 32], qwi_pg.ap())
        P.load(qv[:, 0:32], b1_pg.ap())
        P.load(b2_sb[:], b2_pg.ap())
        qload = P.now("dl")

        def qrow(g):
            return qv[:, g * 32:(g + 1) * 32]
        b1v, av_, bv_, b1s, b1c = qrow(0), qrow(1), qrow(2), qrow(3), qrow(4)
        th, wr, wi = qrow(5), qrow(6), qrow(7)
        scr1 = s01[:, 0:32]
        scr2 = s01[:, 32:64]
        sinth = s01[:, 64:96]
        costh = s01[:, 96:128]

        def rngred(dst, src, extra, wait=()):
            P.vseq(lambda v: v.tensor_scalar(scr1, src, 1.0 / (2 * PI),
                                             extra / (2 * PI), ALU.mult,
                                             ALU.add), wait=wait)
            P.vseq(lambda v: v.tensor_scalar(scr2, scr1, MAGIC, None, ALU.add))
            P.vseq(lambda v: v.tensor_scalar(scr2, scr2, -MAGIC, None, ALU.add))
            P.vseq(lambda v: v.tensor_sub(scr1, scr1, scr2))
            return P.vseq(lambda v: v.tensor_scalar(dst, scr1, 2 * PI, None,
                                                    ALU.mult))
        r1 = rngred(scr2, th, 0.0, wait=[("dl", qload)])
        s1_ = P.op("scalar", lambda s: s.activation(sinth, scr2, AF.Sin),
                   wait=[("sv", r1)], inc="ss")
        r2 = rngred(scr2, th, PI / 2, wait=[("ss", s1_)])
        s2_ = P.op("scalar", lambda s: s.activation(costh, scr2, AF.Sin),
                   wait=[("sv", r2)], inc="ss")
        P.vseq(lambda v: v.tensor_mul(scr1, costh, wr), wait=[("ss", s2_)])
        P.vseq(lambda v: v.tensor_mul(scr2, sinth, wi))
        P.vseq(lambda v: v.tensor_add(scr1, scr1, scr2))
        P.vseq(lambda v: v.tensor_mul(scr2, scr1, wi))
        P.vseq(lambda v: v.tensor_scalar(scr2, scr2, 0.1, None, ALU.mult))
        P.vseq(lambda v: v.tensor_mul(av_, scr2, sinth))
        P.vseq(lambda v: v.tensor_mul(bv_, scr2, costh))
        P.vseq(lambda v: v.tensor_scalar(b1s, b1v, 0.1, None, ALU.mult))
        qprep = P.vseq(lambda v: v.tensor_scalar(b1c, b1v, 0.1, PI / 2,
                                                 ALU.mult, ALU.add))

        if stage == "qprep":
            return dump(qv[:], 8 * 32, [("sv", qprep)])
        with ExitStack() as sp:
            ps_h = [sp.enter_context(nc.psum_tensor(f"ps_h{i}", [128, 512], F32))
                    for i in range(4)]
            ps_o = [sp.enter_context(nc.psum_tensor(f"ps_o{i}", [128, 512], F32))
                    for i in range(2)]
            ps_st = sp.enter_context(nc.psum_tensor("ps_st", [1, 1024], F32))
            wsem = ["dwa", "dwb"]
            xsem = ["dxa", "dxb"]
            ssem_ = ["dsa", "dsb"]
            w1l = [P.loadc(sb3(w1_sb[i], 8), col_block_ap(w1, i * 128,
                                                          (i + 1) * 128),
                           sem=wsem[i]) for i in range(2)]
            w2ld = [0, 0]
            w2done = 0
            psfree = [0, 0]
            x2free = [0, 0]
            xifree = [0, 0]
            stm = 0
            sqe = 0
            actev = [0, 0, 0, 0]
            for fq in range(4):
                hoff = fq * 512
                for mc in range(32):
                    par = mc % 2
                    par4 = mc % 4
                    w = [(wsem[par], w1l[par]), ("sv", qprep)]
                    if actev[par4]:
                        w.append(("sv", actev[par4]))
                    if w2done:
                        w.append(("st", w2done))
                    hm = 0
                    for kc in range(8):
                        hm = P.mm(ps_h[par4][:],
                                  w1_sb[par][:, kc * 128:(kc + 1) * 128],
                                  xn_bf[:, kc * T + hoff:kc * T + hoff + 512],
                                  start=(kc == 0), stop=(kc == 7), wait=w,
                                  inc="st" if kc == 7 else None)
                    idx = fq * 32 + mc
                    if idx + 2 < 128:
                        nb = (idx + 2) % 32
                        w1l[par] = P.loadc(sb3(w1_sb[par], 8),
                                           col_block_ap(w1, nb * 128,
                                                        (nb + 1) * 128),
                                           wait=[("st", P.now("st"))],
                                           sem=wsem[par])
                    P.op("scalar",
                         lambda s, p=ps_h[par4], m=mc: s.activation(
                             s01[:], p[:], AF.Sin, bias=b1s[:, m:m + 1],
                             scale=0.1),
                         wait=[("st", hm)])
                    se = P.op("scalar",
                              lambda s, p=ps_h[par4], m=mc: s.activation(
                                  c01[:], p[:], AF.Sin, bias=b1c[:, m:m + 1],
                                  scale=0.1),
                              inc="ss")
                    P.op("vector",
                         lambda v, p=ps_h[par4], m=mc: v.scalar_tensor_tensor(
                             tqs[:], c01[:], av_[:, m:m + 1], p[:], ALU.mult,
                             ALU.add),
                         wait=[("ss", se)])
                    P.op("vector",
                         lambda v, m=mc: v.scalar_tensor_tensor(
                             tqs[:], s01[:], bv_[:, m:m + 1], tqs[:], ALU.mult,
                             ALU.add))
                    actev[par4] = P.op(
                        "vector",
                        lambda v, m=mc, d=act[:, mc * 512:(mc + 1) * 512]:
                        v.tensor_scalar(d, tqs[:], b1v[:, m:m + 1], 0.0, ALU.add,
                                        ALU.max),
                        inc="sv")
                if stage == "act0" and fq == 0:
                    return dump(act[:], 32 * 512, [("sv", P.now("sv"))])
                actall = P.now("sv")
                w2ld = [P.loadc(sb3(w2_sb[i], 32),
                                col_block_ap(w2, i * 128, (i + 1) * 128),
                                wait=[("st", P.now("st"))], sem=wsem[i])
                        for i in range(2)]
                toks = slice(hoff, hoff + 512)
                pend_lnstat = None
                stprev = [(0, 0), (0, 0)]
                for oc in range(8):
                    par = oc % 2
                    w = [(wsem[par], w2ld[par]), ("sv", actall)]
                    if psfree[par]:
                        w.append(("ss", psfree[par]))
                    om = 0
                    for vc in range(32):
                        om = P.mm(ps_o[par][:],
                                  w2_sb[par][:, vc * 128:(vc + 1) * 128],
                                  act[:, vc * 512:(vc + 1) * 512],
                                  start=(vc == 0), stop=(vc == 31), wait=w,
                                  inc="st" if vc == 31 else None)
                    if pend_lnstat is not None:
                        pend_lnstat()
                        pend_lnstat = None
                    if oc + 2 < 8:
                        w2ld[par] = P.loadc(
                            sb3(w2_sb[par], 32),
                            col_block_ap(w2, (oc + 2) * 128, (oc + 3) * 128),
                            wait=[("st", P.now("st"))], sem=wsem[par])
                    xw = [("sv", xifree[par])] if xifree[par] else []
                    xl = P.load(xi_scr[par][:],
                                x_in.ap()[oc * 128:(oc + 1) * 128, toks],
                                wait=xw, sem=xsem[par])
                    ge = psfree[par] = P.op(
                        "scalar",
                        lambda s, p=ps_o[par], o=oc: s.activation(
                            g_scr[:], p[:], AF.Silu, bias=b2_sb[:, o:o + 1]),
                        wait=[("st", om), ("dl", qload)], inc="ss")
                    w2_ = [("ss", ge)]
                    if x2free[par]:
                        w2_.append((ssem_[par], x2free[par]))
                    P.op("vector",
                         lambda v, p=par, o=oc, ts=toks: v.tensor_mul(
                             x2_scr[p][:], g_scr[:],
                             sacc[:, o * T + ts.start:o * T + ts.stop]),
                         wait=w2_)
                    xifree[par] = P.op(
                        "vector",
                        lambda v, p=par: v.tensor_add(
                            x2_scr[p][:], x2_scr[p][:], xi_scr[p][:]),
                        wait=[(xsem[par], xl)], inc="sv")
                    x2vw = []
                    if stprev[par][0]:
                        x2vw.append(("st", stprev[par][0]))
                    if stprev[par][1]:
                        x2vw.append(("ss", stprev[par][1]))
                    x2v = P.op("vector",
                               lambda v, p=par: v.tensor_copy(x2b_scr[p][:],
                                                              x2_scr[p][:]),
                               wait=x2vw, inc="sv")
                    x2free[par] = P.load(
                        x1_dram.ap()[oc * 128:(oc + 1) * 128, toks],
                        x2_scr[par][:], wait=[("sv", xifree[par])],
                        sem=ssem_[par])

                    def mk_lnstat(oc_=oc, par_=par, x2v_=x2v):
                        def go():
                            nonlocal stm, sqe
                            P.mm(ps_st[0:1, 0:512], ones_bf[:],
                                 x2b_scr[par_][:], start=(oc_ == 0),
                                 stop=(oc_ == 7), wait=[("sv", x2v_)])
                            sqe = P.op("scalar",
                                       lambda s, p=par_: s.activation(
                                           sq3[:], x2b_scr[p][:], AF.Square),
                                       wait=[("sv", x2v_),
                                             ("st", stprev[1 - par_][0])],
                                       inc="ss")
                            stm = P.mm(ps_st[0:1, 512:1024], ones_bf[:],
                                       sq3[:], start=(oc_ == 0),
                                       stop=(oc_ == 7), wait=[("ss", sqe)],
                                       inc="st" if oc_ == 7 else None)
                            stprev[par_] = (stm, sqe)
                        return go
                    pend_lnstat = mk_lnstat()
                pend_lnstat()
                pend_lnstat = None
                P.op("vector",
                     lambda v, d=lnst[0:1, hoff:hoff + 512],
                     s=ps_st[0:1, 0:512]: v.tensor_copy(d, s),
                     wait=[("st", stm)])
                lnev = P.op("vector",
                            lambda v, d=lnst[0:1, T + hoff:T + hoff + 512],
                            s=ps_st[0:1, 512:1024]: v.tensor_copy(d, s),
                            inc="sv")
                w2done = P.now("st")
        P.end()

    if stage == "x1":
        with ExitStack() as sd:
            bb_ = sd.enter_context(nc.sbuf_tensor("bb_", [128, 8 * T], F32))
            P.begin()
            dv = 0
            for c in range(NCHUNK):
                dv = P.load(bb_[:, c * T:(c + 1) * T],
                            x1_dram.ap()[c * 128:(c + 1) * 128, :],
                            wait=[("dsa", P.cnt["dsa"]), ("dsb", P.cnt["dsb"])])
            P.end()
            return dump(bb_[:], 8 * T, [("dl", dv)])

    with ExitStack() as s5:
        x1_scr = [s5.enter_context(nc.sbuf_tensor(f"x1l{i}", [128, T], F32))
                  for i in range(2)]
        t_scr2 = s5.enter_context(nc.sbuf_tensor("t_scr2", [128, T], F32))
        lns2 = s5.enter_context(nc.sbuf_tensor("lns2", [1, 4 * T], F32))
        P.begin()
        mu = lns2[0:1, 0:T]
        va = lns2[0:1, T:2 * T]
        bb2 = lns2[0:1, 2 * T:3 * T]
        tmp = lns2[0:1, 3 * T:4 * T]
        P.op("vector", lambda v: v.tensor_scalar(mu, lnst[0:1, 0:T], 1.0 / D,
                                                 None, ALU.mult),
             wait=[("sv", P.now("sv"))])
        P.op("vector", lambda v: v.tensor_scalar(va, lnst[0:1, T:2 * T], 1.0 / D,
                                                 1e-5, ALU.mult, ALU.add))
        P.op("vector", lambda v: v.scalar_tensor_tensor(tmp, mu, -1.0, mu,
                                                        ALU.mult, ALU.mult))
        d1 = P.op("vector", lambda v: v.tensor_add(va, va, tmp), inc="sv")
        P.op("scalar", lambda s: s.activation(va, va, AF.Ln), wait=[("sv", d1)])
        d2 = P.op("scalar", lambda s: s.activation(va, va, AF.Exp, scale=-0.5),
                  inc="ss")
        d3 = P.op("vector", lambda v: v.scalar_tensor_tensor(bb2, mu, -1.0, va,
                                                             ALU.mult, ALU.mult),
                  wait=[("ss", d2)], inc="sv")
        with ExitStack() as sp:
            ps_a = sp.enter_context(nc.psum_tensor("ps_a2", [128, T], F32))
            ps_b = sp.enter_context(nc.psum_tensor("ps_b2", [128, T], F32))
            bmm = 0
            for qc in range(4):
                sl = slice(qc * 512, (qc + 1) * 512)
                P.mm(ps_a[:, sl], ones_row_f[:], va[:, sl], True, True,
                     wait=[("sv", d3)])
                bmm = P.mm(ps_b[:, sl], ones_row_f[:], bb2[:, sl], True, True,
                           inc="st")
            apl = [0, 0]
            xs = ["dxa", "dxb"]
            for c in range(NCHUNK):
                par = c % 2
                w = [("dsa", P.cnt["dsa"]), ("dsb", P.cnt["dsb"])]
                if apl[par]:
                    w.append(("sv", apl[par]))
                lw = P.load(x1_scr[par][:],
                            x1_dram.ap()[c * 128:(c + 1) * 128, :], wait=w,
                            sem=xs[par])
                P.op("vector",
                     lambda v, p=par: v.tensor_mul(t_scr2[:], x1_scr[p][:],
                                                   ps_a[:]),
                     wait=[(xs[par], lw), ("st", bmm)])
                apl[par] = P.op("vector",
                                lambda v, d=xn_bf[:, c * T:(c + 1) * T]:
                                v.tensor_add(d, t_scr2[:], ps_b[:]), inc="sv")
        P.end()

    if stage == "xn1":
        return dump(xn_bf[:], 8 * T, [("sv", P.now("sv"))])
    sln.close()

    sat = ExitStack()
    attn_bf = sat.enter_context(nc.sbuf_tensor("attn_bf", [128, NCHUNK * T], BF16))
    with ExitStack() as s6:
        wq_sb = s6.enter_context(nc.sbuf_tensor("l1wq", [128, 8 * 128], BF16))
        wk_sb = s6.enter_context(nc.sbuf_tensor("l1wk", [128, 8 * 128], BF16))
        wv_sb = s6.enter_context(nc.sbuf_tensor("l1wv", [128, 8 * 128], BF16))
        bv_sb = s6.enter_context(nc.sbuf_tensor("l1bv", [1, 128], BF16))
        qp_sb = s6.enter_context(nc.sbuf_tensor("qp_sb", [128, T], BF16))
        kp_sb = s6.enter_context(nc.sbuf_tensor("kp_sb", [128, T], BF16))
        v_aug = s6.enter_context(nc.sbuf_tensor("v_aug", [128, 16 * 130], BF16))
        pno = [s6.enter_context(nc.sbuf_tensor(f"pno{i}", [128, 1024], BF16))
               for i in range(4)]
        att_scr = s6.enter_context(nc.sbuf_tensor("att_scr", [128, T], F32))
        rsa = s6.enter_context(nc.sbuf_tensor("rsa", [1, T], F32))
        rsb = s6.enter_context(nc.sbuf_tensor("rsb", [1, T], F32))
        rs_inv = s6.enter_context(nc.sbuf_tensor("rs_inv", [2, T], F32))

        P.begin()
        vv = v_aug[:].rearrange("p (t g) -> p t g", g=130)
        P.op("vector", lambda v: v.memset(vv[:, :, 64:65], 1.0))
        vinit = P.op("vector", lambda v: v.memset(vv[:, :, 129:130], 1.0),
                     inc="sv")
        P.end()

        rsmove = 0
        bmm_prev = 0
        norm_sv = 0
        P.begin()
        prev_sv = 0
        prev_ss = 0
        qkv_end_prev = 0
        pend_norm = None
        for pr in range(8):
            P.loadc(sb3(wq_sb, 8), col_block_ap(wqkv1, pr * 128, (pr + 1) * 128),
                    wait=[("st", qkv_end_prev)])
            P.loadc(sb3(wk_sb, 8),
                    col_block_ap(wqkv1, D + pr * 128, D + (pr + 1) * 128))
            P.loadc(sb3(wv_sb, 8),
                    col_block_ap(wqkv1, 2 * D + pr * 128, 2 * D + (pr + 1) * 128))
            P.loadc(bv_sb[:], row1(bqkv1, 2 * D + pr * 128, 2 * D + (pr + 1) * 128))
            ldg = P.now("dg")
            with ExitStack() as sp:
                ps_qk = sp.enter_context(nc.psum_tensor(f"l1ps_qk{pr}", [128, T],
                                                        F32))
                ps_v = [sp.enter_context(nc.psum_tensor(f"l1ps_v{pr}_{i}",
                                                        [128, 128], F32))
                        for i in range(2)]
                ps_bc = sp.enter_context(nc.psum_tensor(f"l1ps_bc{pr}",
                                                        [128, 1024], F32))
                vev = [vinit, vinit]

                def vtile(tt):
                    par = tt % 2
                    w = [("dg", ldg), ("sv", vev[par])]
                    for kc in range(8):
                        P.mm(ps_v[par][:],
                             xn_bf[:, kc * T + tt * 128:kc * T + (tt + 1) * 128],
                             wv_sb[:, kc * 128:(kc + 1) * 128],
                             start=(kc == 0), stop=False, wait=w)
                    vmm = P.mm(ps_v[par][:], ones_row_bf[:], bv_sb[:],
                               start=False, stop=True, inc="st")
                    vev[par] = P.op(
                        "vector",
                        lambda v, t_=tt, p=ps_v[par]: v.tensor_copy(
                            v_aug[:, t_ * 130:(t_ + 1) * 130]
                            .rearrange("p (g c) -> p g c", c=65)[:, :, 0:64],
                            p[:].rearrange("p (g c) -> p g c", c=64)),
                        wait=[("st", vmm)], inc="sv")

                qmm = 0
                for qn in range(4):
                    for kc in range(8):
                        qmm = P.mm(ps_qk[:, qn * 512:(qn + 1) * 512],
                                   wq_sb[:, kc * 128:(kc + 1) * 128],
                                   xn_bf[:, kc * T + qn * 512:kc * T + (qn + 1) * 512],
                                   start=(kc == 0), stop=(kc == 7),
                                   wait=[("dg", ldg), ("sv", prev_sv),
                                         ("ss", prev_ss)],
                                   inc="st" if kc == 7 else None)
                if pend_norm is not None:
                    pend_norm(ps_bc)
                    pend_norm = None
                eq = P.op("scalar",
                          lambda s, b=bqkv1_sb[:, pr:pr + 1], p=ps_qk:
                          s.activation(qp_sb[:], p[:], AF.Identity, bias=b),
                          wait=[("st", qmm)], inc="ss")
                for tt in range(0, 4):
                    vtile(tt)
                kmm = 0
                for qn in range(4):
                    for kc in range(8):
                        kmm = P.mm(ps_qk[:, qn * 512:(qn + 1) * 512],
                                   wk_sb[:, kc * 128:(kc + 1) * 128],
                                   xn_bf[:, kc * T + qn * 512:kc * T + (qn + 1) * 512],
                                   start=(kc == 0), stop=(kc == 7),
                                   wait=[("ss", eq)],
                                   inc="st" if kc == 7 else None)
                ek = P.op("scalar",
                          lambda s, b=bqkv1_sb[:, 8 + pr:9 + pr], p=ps_qk:
                          s.activation(kp_sb[:], p[:], AF.Identity, bias=b),
                          wait=[("st", kmm)], inc="ss")
                for tt in range(4, 16):
                    vtile(tt)
                qkv_end_prev = P.now("st")
            with ExitStack() as sp:
                ps_s = [sp.enter_context(nc.psum_tensor(f"l1ps_s{pr}_{i}",
                                                        [128, 1024], F32))
                        for i in range(2)]
                ps_av = [sp.enter_context(nc.psum_tensor(f"l1ps_av{pr}_{i}",
                                                         [65, 1024], F32))
                         for i in range(2)]
                vall = P.now("sv")
                ekk = P.now("ss")
                exd = [[0, 0], [0, 0]]
                avm = [[0, 0], [0, 0]]
                evdone = 0
                for qh in range(2):
                    q0h = qh * 1024
                    pend = []
                    for tk in range(16):
                        par = tk % 2
                        for hb in range(2):
                            hb0 = hb * 64
                            w = [("ss", ekk), ("sv", vall)]
                            if exd[hb][1 - par]:
                                w.append(("ss", exd[hb][1 - par]))
                            smm = 0
                            for qn2 in range(2):
                                smm = P.mm(
                                    ps_s[hb][:, qn2 * 512:(qn2 + 1) * 512],
                                    kp_sb[hb0:hb0 + 64,
                                          tk * 128:(tk + 1) * 128],
                                    qp_sb[hb0:hb0 + 64,
                                          q0h + qn2 * 512:
                                          q0h + (qn2 + 1) * 512],
                                    True, True, wait=w,
                                    inc="st" if qn2 == 1 else None)
                            we = [("st", smm)]
                            if avm[hb][par]:
                                we.append(("st", avm[hb][par]))
                            exd[hb][par] = P.op(
                                "scalar",
                                lambda s, d=pno[2 * hb + par], p=ps_s[hb]:
                                s.activation(d[:], p[:], AF.Exp, scale=0.125,
                                             bias=neg10_sb[:, 0:1]),
                                wait=we, inc="ss")
                        for (hb_, tk_) in pend:
                            par_ = tk_ % 2
                            wa = [("ss", exd[hb_][par_])]
                            if tk_ == 0 and evdone:
                                wa.append(("sv", evdone))
                            for qn2 in range(2):
                                avm[hb_][par_] = P.mm(
                                    ps_av[hb_][:, qn2 * 512:(qn2 + 1) * 512],
                                    v_aug[:, tk_ * 130 + hb_ * 65:
                                          tk_ * 130 + (hb_ + 1) * 65],
                                    pno[2 * hb_ + par_][:, qn2 * 512:
                                                        (qn2 + 1) * 512],
                                    start=(tk_ == 0), stop=(tk_ == 15),
                                    wait=wa,
                                    inc="st" if qn2 == 1 else None)
                        pend = [(0, tk), (1, tk)]
                    for (hb_, tk_) in pend:
                        par_ = tk_ % 2
                        for qn2 in range(2):
                            avm[hb_][par_] = P.mm(
                                ps_av[hb_][:, qn2 * 512:(qn2 + 1) * 512],
                                v_aug[:, tk_ * 130 + hb_ * 65:
                                      tk_ * 130 + (hb_ + 1) * 65],
                                pno[2 * hb_ + par_][:, qn2 * 512:
                                                    (qn2 + 1) * 512],
                                start=(tk_ == 0), stop=(tk_ == 15),
                                wait=[("ss", exd[hb_][par_])],
                                inc="st" if qn2 == 1 else None)
                    for hb in range(2):
                        hb0 = hb * 64
                        aw = [("st", avm[hb][1]), ("dg", rsmove)]
                        if norm_sv:
                            aw.append(("sv", norm_sv))
                        ae = P.op("vector",
                                  lambda v, d=att_scr[hb0:hb0 + 64,
                                                      q0h:q0h + 1024],
                                  s=ps_av[hb][0:64, :]: v.tensor_copy(d, s),
                                  wait=aw)
                        dst_rs = (rsa if hb == 0 else rsb)[0:1, q0h:q0h + 1024]
                        evdone = P.op("vector",
                                      lambda v, d=dst_rs, s=ps_av[hb][64:65, :]:
                                      v.reciprocal(d, s), inc="sv")
                m1w = [("sv", evdone)]
                if bmm_prev:
                    m1w.append(("st", bmm_prev))
                P.op("gpsimd", lambda e: e.dma_start(rs_inv[0:1, :], rsa[0:1, :]),
                     wait=m1w, inc="dg", inc_n=16)
                rsmove = P.op("gpsimd",
                              lambda e: e.dma_start(rs_inv[1:2, :], rsb[0:1, :]),
                              inc="dg", inc_n=16)
                prev_sv = evdone
                prev_ss = P.now("ss")

            def mk_norm(pr_=pr, rsm=rsmove):
                def go(ps_bc):
                    nonlocal bmm_prev, norm_sv
                    bmm = 0
                    for half2 in range(2):
                        w = [("dg", rsm)]
                        if norm_sv:
                            w.append(("sv", norm_sv))
                        for qn in range(2):
                            c0 = half2 * 1024 + qn * 512
                            bmm = P.mm(ps_bc[:, qn * 512:(qn + 1) * 512],
                                       selab_sb[:], rs_inv[:, c0:c0 + 512],
                                       True, True, wait=w, inc="st")
                        norm_sv = P.op(
                            "vector",
                            lambda v,
                            d=attn_bf[:, pr_ * T + half2 * 1024:
                                      pr_ * T + (half2 + 1) * 1024],
                            s=att_scr[:, half2 * 1024:(half2 + 1) * 1024],
                            p=ps_bc: v.tensor_mul(d, s, p[:, 0:1024]),
                            wait=[("st", bmm)], inc="sv")
                    bmm_prev = bmm
                return go
            pend_norm = mk_norm()
        with ExitStack() as spf:
            ps_bcf = spf.enter_context(nc.psum_tensor("l1ps_bcf", [128, 1024],
                                                      F32))
            pend_norm(ps_bcf)
        P.end()

    if stage == "attn1":
        return dump(attn_bf[:], 8 * T, [("sv", P.now("sv"))])

    with ExitStack() as s7:
        wo_sb = [s7.enter_context(nc.sbuf_tensor(f"wo{i}", [128, 8 * 128], BF16))
                 for i in range(2)]
        bo_sb = s7.enter_context(nc.sbuf_tensor("bo_sb", [128, 8], F32))
        P.begin()
        P.load(bo_sb[:], bout1_pg.ap())
        bow = P.now("dl")
        attdone = P.now("sv")
        with ExitStack() as sp:
            ps_wo = [sp.enter_context(nc.psum_tensor(f"ps_wo{i}", [128, 512],
                                                     F32)) for i in range(2)]
            wsem = ["dwa", "dwb"]
            wl = [P.loadc(sb3(wo_sb[i], 8), col_block_ap(wout1, i * 128,
                                                         (i + 1) * 128),
                          wait=[("st", P.now("st"))], sem=wsem[i]) for i in range(2)]
            oev = [0, 0]
            for oc in range(8):
                bpar = oc % 2
                for qc in range(4):
                    par = (oc * 4 + qc) % 2
                    w = [(wsem[bpar], wl[bpar]), ("sv", attdone)]
                    if oev[par]:
                        w.append(("ss", oev[par]))
                    pm = 0
                    for kc in range(8):
                        pm = P.mm(ps_wo[par][:],
                                  wo_sb[bpar][:, kc * 128:(kc + 1) * 128],
                                  attn_bf[:, kc * T + qc * 512:
                                          kc * T + (qc + 1) * 512],
                                  start=(kc == 0), stop=(kc == 7), wait=w,
                                  inc="st" if kc == 7 else None)
                    oev[par] = P.op(
                        "scalar",
                        lambda s, d=sacc[:, oc * T + qc * 512:
                                         oc * T + (qc + 1) * 512],
                        p=ps_wo[par], o=oc: s.activation(
                            d, p[:], AF.Identity, bias=bo_sb[:, o:o + 1]),
                        wait=[("st", pm), ("dl", bow)], inc="ss")
                if oc + 2 < 8:
                    wl[bpar] = P.loadc(sb3(wo_sb[bpar], 8),
                                       col_block_ap(wout1, (oc + 2) * 128,
                                                    (oc + 3) * 128),
                                       wait=[("st", P.now("st"))], sem=wsem[bpar])
        P.end()

    if stage == "s1":
        return dump(sacc[:], 8 * T, [("ss", P.now("ss"))])
    sat.close()

    with ExitStack() as s8:
        cw1 = s8.enter_context(nc.sbuf_tensor("cw1", [128, 8 * 1024], BF16))
        cw2 = s8.enter_context(nc.sbuf_tensor("cw2", [128, 8 * 512], BF16))
        cw3 = s8.enter_context(nc.sbuf_tensor("cw3", [128, 4 * 256], BF16))
        cw4 = s8.enter_context(nc.sbuf_tensor("cw4", [128, 2], BF16))
        cb1 = s8.enter_context(nc.sbuf_tensor("cb1", [128, 8], F32))
        cb2 = s8.enter_context(nc.sbuf_tensor("cb2", [128, 4], F32))
        cb3 = s8.enter_context(nc.sbuf_tensor("cb3", [128, 2], F32))
        sc_sb = s8.enter_context(nc.sbuf_tensor("sc_sb", [1, 16], F32))
        gcv = s8.enter_context(nc.sbuf_tensor("gcv", [128, 16], BF16))
        gcs = s8.enter_context(nc.sbuf_tensor("gcs", [128, 8], F32))
        out_q = s8.enter_context(nc.sbuf_tensor("out_q", [128, 8 * 512], F32))
        outb_q = s8.enter_context(nc.sbuf_tensor("outb_q", [128, 8 * 512], BF16))
        h1b = s8.enter_context(nc.sbuf_tensor("h1b", [128, 8 * 512], BF16))
        h2b = s8.enter_context(nc.sbuf_tensor("h2b", [128, 4 * 512], BF16))
        h3b = s8.enter_context(nc.sbuf_tensor("h3b", [128, 2 * 512], BF16))
        th4 = s8.enter_context(nc.sbuf_tensor("th4", [1, 512], F32))
        cpv = s8.enter_context(nc.sbuf_tensor("cpv", [1, 512], F32))
        tmpc = s8.enter_context(nc.sbuf_tensor("tmpc", [128, 512], F32))
        tmc2 = s8.enter_context(nc.sbuf_tensor("tmc2", [128, 512], F32))
        gf_scr = [s8.enter_context(nc.sbuf_tensor(f"gfs{i}", [128, 512], F32))
                  for i in range(2)]
        fm_scr = s8.enter_context(nc.sbuf_tensor("fm_scr", [128, 512], F32))
        xf_all = s8.enter_context(nc.sbuf_tensor("xf_all", [128, 8 * 512], F32))
        s8g = ExitStack()
        gw1 = s8g.enter_context(nc.sbuf_tensor("gw1", [128, 8 * 512], BF16))
        gw2 = s8g.enter_context(nc.sbuf_tensor("gw2", [128, 4 * 256], BF16))
        gw3 = s8g.enter_context(nc.sbuf_tensor("gw3", [128, 2], BF16))
        gb1 = s8g.enter_context(nc.sbuf_tensor("gb1", [128, 4], F32))
        gb2 = s8g.enter_context(nc.sbuf_tensor("gb2", [128, 2], F32))

        P.begin()
        P.loadc(sb3(cw1, 8), cnw[0].ap().rearrange("(kc p) m -> p kc m", p=128))
        P.loadc(sb3(cw2, 8), cnw[1].ap().rearrange("(kc p) m -> p kc m", p=128))
        P.loadc(sb3(cw3, 4), cnw[2].ap().rearrange("(kc p) m -> p kc m", p=128))
        P.loadc(cw4[:], cnw4_pg.ap())
        P.loadc(sb3(gw1, 8), gcw[0].ap().rearrange("(kc p) m -> p kc m", p=128))
        P.loadc(sb3(gw2, 4), gcw[1].ap().rearrange("(kc p) m -> p kc m", p=128))
        P.loadc(gw3[:], gcw3_pg.ap())
        wload = P.now("dg")
        P.load(cb1[:], cnb_pg[0].ap())
        P.load(cb2[:], cnb_pg[1].ap())
        P.load(cb3[:], cnb_pg[2].ap())
        P.load(sc_sb[0:1, 6:7], row1(cnb[3], 0, 1))
        P.load(gb1[:], gcb_pg[0].ap())
        P.load(gb2[:], gcb_pg[1].ap())
        P.load(sc_sb[0:1, 7:8], row1(gcb[2], 0, 1))
        P.load(sc_sb[0:1, 2:3], row1(arange_in, 0, 1))
        bload = P.now("dl")
        P.vseq(lambda v: v.tensor_scalar(sc_sb[0:1, 0:1], sc_sb[0:1, 6:7],
                                         0.5, None, ALU.mult),
               wait=[("dl", bload)])
        sc1 = P.vseq(lambda v: v.tensor_scalar(sc_sb[0:1, 1:2],
                                               sc_sb[0:1, 7:8], 0.5, None,
                                               ALU.mult))
        for c in range(NCHUNK):
            P.op("vector",
                 lambda v, d=gcs[:, c:c + 1], s=xn_bf[:, c * T:(c + 1) * T]:
                 v.reduce_sum(d, s, axis=AX), inc="sv")
        gsum = P.vseq(lambda v: v.tensor_copy(gcv[:, 0:8], gcs[:]))
        with ExitStack() as sp:
            ps_gn = sp.enter_context(nc.psum_tensor("ps_gfn", [128, 8], F32))
            ps_gf = sp.enter_context(nc.psum_tensor("ps_gf", [1, 2], F32))
            gmm = 0
            for m in range(4):
                for kc in range(8):
                    gmm = P.mm(ps_gn[:, m:m + 1],
                               gw1[:, kc * 512 + m * 128:kc * 512 + (m + 1) * 128],
                               gcv[:, kc:kc + 1], start=(kc == 0), stop=(kc == 7),
                               wait=[("dg", wload), ("sv", gsum)],
                               inc="st" if kc == 7 else None)
            g1e = 0
            for m in range(4):
                g1e = P.op("scalar",
                           lambda s, m_=m: s.activation(
                               gcv[:, 8 + m_:9 + m_], ps_gn[:, m_:m_ + 1],
                               AF.Gelu, bias=gb1[:, m_:m_ + 1], scale=1.0 / T),
                           wait=[("st", gmm), ("dl", bload)], inc="ss")
            gm2 = 0
            for m in range(2):
                for kc in range(4):
                    gm2 = P.mm(ps_gn[:, 4 + m:5 + m],
                               gw2[:, kc * 256 + m * 128:kc * 256 + (m + 1) * 128],
                               gcv[:, 8 + kc:9 + kc], start=(kc == 0),
                               stop=(kc == 3),
                               wait=[("ss", g1e)],
                               inc="st" if kc == 3 else None)
            g2e = 0
            for m in range(2):
                g2e = P.op("scalar",
                           lambda s, m_=m: s.activation(
                               gcv[:, 12 + m_:13 + m_], ps_gn[:, 4 + m_:5 + m_],
                               AF.Gelu, bias=gb2[:, m_:m_ + 1]),
                           wait=[("st", gm2)], inc="ss")
            gm3 = 0
            for kc in range(2):
                gm3 = P.mm(ps_gf[0:1, 0:1], gw3[:, kc:kc + 1],
                           gcv[:, 12 + kc:13 + kc], start=(kc == 0),
                           stop=(kc == 1),
                           wait=[("ss", g2e)], inc="st" if kc == 1 else None)
            sgf = P.op("vector", lambda v: v.tensor_copy(sc_sb[0:1, 8:9],
                                                         ps_gf[0:1, 0:1]),
                       wait=[("st", gm3)], inc="sv")
            tgf = P.op("scalar", lambda s: s.activation(sc_sb[0:1, 3:4],
                                                        sc_sb[0:1, 8:9],
                                                        AF.Tanh, scale=0.5,
                                                        bias=sc_sb[0:1, 1:2]),
                       wait=[("sv", sgf), ("sv", sc1)], inc="ss")
        P.vseq(lambda v: v.tensor_scalar(sc_sb[0:1, 3:4], sc_sb[0:1, 3:4],
                                         0.5, 0.5, ALU.mult, ALU.add),
               wait=[("ss", tgf)])
        P.vseq(lambda v: v.tensor_scalar(sc_sb[0:1, 8:9], sc_sb[0:1, 3:4],
                                         1.4, None, ALU.mult))
        P.vseq(lambda v: v.tensor_mul(sc_sb[0:1, 8:9], sc_sb[0:1, 8:9],
                                      sc_sb[0:1, 2:3]))
        P.vseq(lambda v: v.tensor_scalar(sc_sb[0:1, 9:10], sc_sb[0:1, 2:3],
                                         -0.7, 0.1, ALU.mult, ALU.add))
        P.vseq(lambda v: v.tensor_add(sc_sb[0:1, 4:5], sc_sb[0:1, 8:9],
                                      sc_sb[0:1, 9:10]))
        cprep = P.vseq(lambda v: v.tensor_scalar(sc_sb[0:1, 5:6],
                                                 sc_sb[0:1, 2:3], 0.3, None,
                                                 ALU.mult))
        P.end()

        if stage == "gfs":
            return dump(sc_sb[0:1, :], 16, [("sv", cprep)])

        s8g.close()
        tw_sgn = [1, 1, -1, -1, -1, -1, 1, 1]
        prev_q_done = 0
        osem = ["dsa", "dsb"]
        ostore = [P.cnt["dsa"], P.cnt["dsb"]]
        xf_w = [0, 0, 0, 0]
        prev_fin = 0
        P.begin()
        for Q in range(4):
            qoff = Q * 512
            for c in range(NCHUNK):
                xf_w[Q] = P.load(xf_all[:, c * 512:(c + 1) * 512],
                                 x1_dram.ap()[c * 128:(c + 1) * 128,
                                              qoff:qoff + 512],
                                 wait=[("ss", prev_q_done), ("sv", prev_fin)],
                                 sem="dxa" if Q % 2 == 0 else "dxb")
            inits = 0
            for c in range(NCHUNK):
                w = [("sv", cprep)]
                if prev_q_done:
                    w.append(("ss", prev_q_done))
                inits = P.op("vector",
                             lambda v, d=out_q[:, c * 512:(c + 1) * 512],
                             s=xn_bf[:, c * T + qoff:c * T + qoff + 512]:
                             v.tensor_copy(d, s), wait=w, inc="sv")
            with ExitStack() as sp:
                ps_c = [sp.enter_context(nc.psum_tensor(f"ps_c{Q}{i}", [128, 512],
                                                        F32)) for i in range(2)]
                ps_c4 = sp.enter_context(nc.psum_tensor(f"ps_c4{Q}", [1, 512],
                                                        F32))
                ps_cp = sp.enter_context(nc.psum_tensor(f"ps_cp{Q}", [128, 512],
                                                        F32))
                tws = inits
                psc_free = [0, 0]
                for cyc in range(3):
                    def rhs1(kc):
                        if cyc == 0:
                            return xn_bf[:, kc * T + qoff:kc * T + qoff + 512]
                        return outb_q[:, kc * 512:(kc + 1) * 512]
                    h1e = 0
                    for mc in range(8):
                        par = mc % 2
                        w = [("dg", wload), ("dl", bload), ("sv", tws),
                             ("ss", psc_free[par])]
                        m1 = 0
                        for kc in range(8):
                            m1 = P.mm(ps_c[par][:],
                                      cw1[:, kc * 1024 + mc * 128:
                                          kc * 1024 + (mc + 1) * 128],
                                      rhs1(kc), start=(kc == 0), stop=(kc == 7),
                                      wait=w, inc="st" if kc == 7 else None)
                        h1e = psc_free[par] = P.op(
                            "scalar",
                            lambda s, d=h1b[:, mc * 512:(mc + 1) * 512],
                            p=ps_c[par], m=mc: s.activation(
                                d, p[:], AF.Gelu, bias=cb1[:, m:m + 1]),
                            wait=[("st", m1)], inc="ss")
                    h2e = 0
                    for mc in range(4):
                        par = mc % 2
                        m2 = 0
                        for kc in range(8):
                            m2 = P.mm(ps_c[par][:],
                                      cw2[:, kc * 512 + mc * 128:
                                          kc * 512 + (mc + 1) * 128],
                                      h1b[:, kc * 512:(kc + 1) * 512],
                                      start=(kc == 0), stop=(kc == 7),
                                      wait=[("ss", h1e),
                                            ("ss", psc_free[par])],
                                      inc="st" if kc == 7 else None)
                        h2e = psc_free[par] = P.op(
                            "scalar",
                            lambda s, d=h2b[:, mc * 512:(mc + 1) * 512],
                            p=ps_c[par], m=mc: s.activation(
                                d, p[:], AF.Gelu, bias=cb2[:, m:m + 1]),
                            wait=[("st", m2)], inc="ss")
                    h3e = 0
                    for mc in range(2):
                        par = mc % 2
                        m3 = 0
                        for kc in range(4):
                            m3 = P.mm(ps_c[par][:],
                                      cw3[:, kc * 256 + mc * 128:
                                          kc * 256 + (mc + 1) * 128],
                                      h2b[:, kc * 512:(kc + 1) * 512],
                                      start=(kc == 0), stop=(kc == 3),
                                      wait=[("ss", h2e),
                                            ("ss", psc_free[par])],
                                      inc="st" if kc == 3 else None)
                        h3e = psc_free[par] = P.op(
                            "scalar",
                            lambda s, d=h3b[:, mc * 512:(mc + 1) * 512],
                            p=ps_c[par], m=mc: s.activation(
                                d, p[:], AF.Gelu, bias=cb3[:, m:m + 1]),
                            wait=[("st", m3)], inc="ss")
                    m4 = 0
                    for kc in range(2):
                        m4 = P.mm(ps_c4[0:1, :], cw4[:, kc:kc + 1],
                                  h3b[:, kc * 512:(kc + 1) * 512],
                                  start=(kc == 0), stop=(kc == 1),
                                  wait=[("ss", h3e)],
                                  inc="st" if kc == 1 else None)
                    t4e = P.op("scalar",
                               lambda s: s.activation(th4[:], ps_c4[:], AF.Tanh,
                                                      scale=0.5,
                                                      bias=sc_sb[0:1, 0:1]),
                               wait=[("st", m4)], inc="ss")
                    cpe = P.op("vector",
                               lambda v: v.tensor_scalar(cpv[:], th4[:],
                                                         sc_sb[0:1, 5:6],
                                                         sc_sb[0:1, 4:5],
                                                         ALU.mult, ALU.add),
                               wait=[("ss", t4e)], inc="sv")
                    cbm = P.mm(ps_cp[:], ones_row_f[:], cpv[:], True, True,
                               wait=[("sv", cpe)], inc="st")
                    for c in range(4):
                        c2 = c + 4
                        oc1 = out_q[:, c * 512:(c + 1) * 512]
                        oc2 = out_q[:, c2 * 512:(c2 + 1) * 512]
                        P.op("vector", lambda v, a=oc1: v.tensor_copy(tmpc[:], a),
                             wait=[("st", cbm)])
                        P.op("vector", lambda v, a=oc2: v.tensor_mul(tmc2[:],
                                                                     ps_cp[:],
                                                                     a))
                        if tw_sgn[c] > 0:
                            P.op("vector",
                                 lambda v, a=oc1: v.tensor_add(a, a, tmc2[:]))
                        else:
                            P.op("vector",
                                 lambda v, a=oc1: v.tensor_sub(a, a, tmc2[:]))
                        P.op("vector", lambda v: v.tensor_mul(tmc2[:], ps_cp[:],
                                                              tmpc[:]))
                        if tw_sgn[c2] > 0:
                            tws = P.op("vector",
                                       lambda v, a=oc2: v.tensor_add(a, a,
                                                                     tmc2[:]),
                                       inc="sv")
                        else:
                            tws = P.op("vector",
                                       lambda v, a=oc2: v.tensor_sub(a, a,
                                                                     tmc2[:]),
                                       inc="sv")
                    if cyc < 2:
                        for c in range(NCHUNK):
                            tws = P.op(
                                "vector",
                                lambda v, d=outb_q[:, c * 512:(c + 1) * 512],
                                s=out_q[:, c * 512:(c + 1) * 512]:
                                v.tensor_copy(d, s),
                                wait=[("sv", tws)], inc="sv")
                if stage == "cyc0" and Q == 0:
                    return dump(out_q[:], 8 * 512, [("sv", tws)])
                xsem = "dxa" if Q % 2 == 0 else "dxb"
                for c in range(NCHUNK):
                    par = c % 2
                    w = [("sv", tws), (xsem, xf_w[Q])]
                    if ostore[par]:
                        w.append((osem[par], ostore[par]))
                    sge = prev_q_done = P.op(
                        "scalar",
                        lambda s, p=par, s_=out_q[:, c * 512:(c + 1) * 512]:
                        s.activation(gf_scr[p][:], s_, AF.Silu),
                        wait=w, inc="ss")
                    P.op("vector",
                         lambda v, p=par, c_=c, qo=qoff: v.tensor_mul(
                             fm_scr[:], gf_scr[p][:],
                             sacc[:, c_ * T + qo:c_ * T + qo + 512]),
                         wait=[("ss", sge)])
                    fin = P.op("vector",
                               lambda v, p=par, c_=c: v.tensor_add(
                                   gf_scr[p][:], fm_scr[:],
                                   xf_all[:, c_ * 512:(c_ + 1) * 512]),
                               inc="sv")
                    prev_fin = fin
                    ostore[par] = P.load(
                        out_d.ap()[c * 128:(c + 1) * 128, qoff:qoff + 512],
                        gf_scr[par][:], wait=[("sv", fin)], sem=osem[par])
        P.end()

    P.begin()
    P.op("sync", wait=[("dsa", P.cnt["dsa"]), ("dsb", P.cnt["dsb"]),
                       ("dl", P.now("dl")), ("dg", P.now("dg"))])
    P.end()
    P.stack.close()
    return nc


def _in_maps(inputs):
    C = _host_constants()
    x = np.ascontiguousarray(np.asarray(inputs["L0_attn_hidden_states"],
                                        dtype=np.float32))
    wqkv1p = np.ascontiguousarray(
        np.asarray(inputs["L1_attn_Wqkv"], dtype=np.float32)[:, C["l1perm"]])
    bqkv1p = np.ascontiguousarray(
        np.asarray(inputs["L1_attn_bqkv"], dtype=np.float32)[C["l1perm"]])
    common = {}
    for k in ("L0_attn_w_qkv", "L0_attn_b_qkv", "L0_attn_w_g", "L0_attn_b_g",
              "L0_attn_w_p", "L0_attn_b_p", "L0_ffn_W1", "L0_ffn_b1",
              "L0_ffn_quantum_phases", "L0_ffn_quantum_weights_real",
              "L0_ffn_quantum_weights_imag", "L0_ffn_W2", "L0_ffn_b2",
              "L1_attn_Wout", "L1_attn_bout",
              "L1_ffn_cn_w1", "L1_ffn_cn_b1", "L1_ffn_cn_w2", "L1_ffn_cn_b2",
              "L1_ffn_cn_w3", "L1_ffn_cn_b3", "L1_ffn_cn_w4", "L1_ffn_cn_b4",
              "L1_ffn_gc_w1", "L1_ffn_gc_b1", "L1_ffn_gc_w2", "L1_ffn_gc_b2",
              "L1_ffn_gc_w3", "L1_ffn_gc_b3", "L1_ffn_adaptive_range"):
        v = np.asarray(inputs[k], dtype=np.float32)
        common[k] = np.ascontiguousarray(v.reshape(v.shape if v.ndim > 0 else (1,)))
    common["L1_ffn_cn_w4"] = common["L1_ffn_cn_w4"].reshape(256, 1)
    common["L1_ffn_gc_w3"] = common["L1_ffn_gc_w3"].reshape(256, 1)
    common["L1_ffn_cn_b4"] = common["L1_ffn_cn_b4"].reshape(1)
    common["L1_ffn_gc_b3"] = common["L1_ffn_gc_b3"].reshape(1)
    common["L1_ffn_adaptive_range"] = common["L1_ffn_adaptive_range"].reshape(1)
    common["l1_wqkv_perm"] = wqkv1p
    common["l1_bqkv_perm"] = bqkv1p

    def pg(v, g):
        return np.ascontiguousarray(np.asarray(v, dtype=np.float32)
                                    .reshape(g, 128).T)
    common["b_qkv0_pg"] = pg(inputs["L0_attn_b_qkv"], 32)
    common["bg0_pg"] = pg(inputs["L0_attn_b_g"], 16)
    common["bp0_pg"] = pg(inputs["L0_attn_b_p"], 8)
    common["b1_pg"] = pg(inputs["L0_ffn_b1"], 32)
    common["b2_pg"] = pg(inputs["L0_ffn_b2"], 8)
    common["qph_pg"] = pg(inputs["L0_ffn_quantum_phases"], 32)
    common["qwr_pg"] = pg(inputs["L0_ffn_quantum_weights_real"], 32)
    common["qwi_pg"] = pg(inputs["L0_ffn_quantum_weights_imag"], 32)
    common["b_qkv1_pg"] = pg(bqkv1p, 24)
    common["bout1_pg"] = pg(inputs["L1_attn_bout"], 8)
    common["cnb1_pg"] = pg(inputs["L1_ffn_cn_b1"], 8)
    common["cnb2_pg"] = pg(inputs["L1_ffn_cn_b2"], 4)
    common["cnb3_pg"] = pg(inputs["L1_ffn_cn_b3"], 2)
    common["gcb1_pg"] = pg(inputs["L1_ffn_gc_b1"], 4)
    common["gcb2_pg"] = pg(inputs["L1_ffn_gc_b2"], 2)
    common["cnw4_pg"] = pg(inputs["L1_ffn_cn_w4"], 2)
    common["gcw3_pg"] = pg(inputs["L1_ffn_gc_w3"], 2)
    for k in ("tab_aq", "tab_bq", "tab_ak", "tab_bk", "permT", "permT_f32",
              "trimask", "selab", "sel16"):
        common[k] = C[k]
    maps = []
    for i in range(8):
        m = dict(common)
        m["x_fmaj"] = np.ascontiguousarray(x[i].T)
        maps.append(m)
    return maps


def kernel(**inputs):
    nc = build_nc("full")
    maps = _in_maps(inputs)
    res = run_bass_kernel_spmd(nc, maps, core_ids=list(range(8)))
    return np.stack([np.asarray(r["out_fmaj"], dtype=np.float32).T
                     for r in res.results], axis=0)


# version 11
# speedup vs baseline: 1.1740x; 1.0975x over previous
import math
import numpy as np
import ml_dtypes
from contextlib import ExitStack

import concourse.bass as bass
import concourse.mybir as mybir
from concourse.bass_utils import run_bass_kernel_spmd

F32 = mybir.dt.float32
BF16 = mybir.dt.bfloat16
AF = mybir.ActivationFunctionType
ALU = mybir.AluOpType
AX = mybir.AxisListType.X

T = 2048
D = 1024
NCHUNK = 8
H0 = 8
DK0, DV0 = 128, 256
DFF = 4096
PI = math.pi
MAGIC = 12582912.0
ENGINES = ("sync", "gpsimd", "tensor", "scalar", "vector")
SEMS = ("dl", "dg", "dwa", "dwb", "dxa", "dxb", "dsa", "dsb", "st", "ss", "sv")
DMA_SEMS = {"dl", "dg", "dwa", "dwb", "dxa", "dxb", "dsa", "dsb"}

_CONSTS = None


def _host_constants():
    global _CONSTS
    if _CONSTS is not None:
        return _CONSTS
    half = D // 2
    c = np.arange(half, dtype=np.float64)
    scale_vec = (2 * c + 0.4 * D) / (1.4 * D)
    pos = np.arange(T, dtype=np.float64)
    scale = scale_vec[None, :] ** (pos[:, None] / 512.0)
    inv_freq = 1.0 / 10000.0 ** (c / half)
    ang = pos[:, None] * inv_freq[None, :]
    sin_q = np.repeat(np.sin(ang) * scale, 2, axis=-1).T
    cos_q = np.repeat(np.cos(ang) * scale, 2, axis=-1).T
    sin_k = np.repeat(np.sin(ang) / scale, 2, axis=-1).T
    cos_k = np.repeat(np.cos(ang) / scale, 2, axis=-1).T
    gamma = 1.0 - np.exp(np.linspace(np.log(1 / 32.0), np.log(1 / 512.0), H0))
    qfac = np.zeros((H0, T))
    kfac = np.zeros((H0, T))
    for h in range(H0):
        g = gamma[h]
        rnorm = np.sqrt((1.0 - g ** (pos + 1)) / (1.0 - g))
        qfac[h] = g ** pos / rnorm * DK0 ** -0.5
        kfac[h] = g ** (-pos)
    fh = np.arange(D) // DK0
    bf = ml_dtypes.bfloat16
    f3 = np.float32
    tab_aq = np.ascontiguousarray((cos_q * qfac[fh, :]).astype(f3))
    tab_bq = np.ascontiguousarray((sin_q * qfac[fh, :]).astype(f3))
    tab_ak = np.ascontiguousarray((cos_k * kfac[fh, :]).astype(f3))
    tab_bk = np.ascontiguousarray((sin_k * kfac[fh, :]).astype(f3))
    P = np.zeros((128, 128), dtype=np.float64)
    for i in range(64):
        P[2 * i, 2 * i + 1] = -1.0
        P[2 * i + 1, 2 * i] = 1.0
    permT = np.ascontiguousarray(P.T.astype(bf))
    permT_f32 = np.ascontiguousarray(P.T.astype(np.float32))
    tri = (np.arange(128)[None, :] >= np.arange(128)[:, None])
    trimask = np.ascontiguousarray(tri.astype(np.float32))
    selab = np.zeros((2, 128), dtype=np.float32)
    selab[0, :64] = 1.0
    selab[1, 64:] = 1.0
    sel16 = np.zeros((16, 2048), dtype=ml_dtypes.bfloat16)
    for j in range(16):
        sel16[j, j * 128:(j + 1) * 128] = 1.0
    perm = []
    for off in (0, 64, 128):
        for h in range(16):
            perm.extend(range(h * 192 + off, h * 192 + off + 64))
    perm = np.asarray(perm, dtype=np.int64)
    _CONSTS = dict(tab_aq=tab_aq, tab_bq=tab_bq, tab_ak=tab_ak, tab_bk=tab_bk,
                   permT=permT, permT_f32=permT_f32, trimask=trimask,
                   selab=selab, sel16=sel16, l1perm=perm)
    return _CONSTS


class Prog:

    def __init__(self):
        self.nc = bass.Bass("TRN2", target_bir_lowering=False)
        self.stack = ExitStack()
        self.sems = {n: self.stack.enter_context(self.nc.semaphore(n))
                     for n in SEMS}
        self.cnt = {n: 0 for n in SEMS}
        self.ops = None
        self.last_wait = None

    def begin(self):
        assert self.ops is None
        self.ops = {e: [] for e in ENGINES}
        self.last_wait = {e: {} for e in ENGINES}

    def end(self):
        ops, self.ops = self.ops, None
        nonempty = [e for e in ENGINES if ops[e]]
        if not nonempty:
            return
        with self.nc.Block() as b:
            for e in nonempty:
                def runner(eng, _lst=ops[e]):
                    for g in _lst:
                        g(eng)
                getattr(b, e)(runner)

    def op(self, engine, fn=None, wait=(), inc=None, inc_n=1):
        sems = self.sems
        waits = []
        for s, v in wait:
            if not v:
                continue
            assert v <= self.cnt[s], (s, v, self.cnt[s])
            if self.last_wait[engine].get(s, -1) >= v:
                continue
            self.last_wait[engine][s] = v
            waits.append((s, v))

        def g(eng):
            for s, v in waits:
                eng.wait_ge(sems[s], v)
            if fn is not None:
                instr = fn(eng)
                if inc is not None:
                    instr.then_inc(sems[inc], inc_n)
        self.ops[engine].append(g)
        if fn is not None and inc is not None:
            self.cnt[inc] += inc_n
        return self.cnt[inc] if inc is not None else None

    def now(self, sem):
        return self.cnt[sem]

    def vseq(self, fn, wait=()):
        w = list(wait) + [("sv", self.cnt["sv"])]
        return self.op("vector", fn, wait=w, inc="sv")

    def sseq(self, fn, wait=()):
        w = list(wait) + [("ss", self.cnt["ss"])]
        return self.op("scalar", fn, wait=w, inc="ss")

    def at(self, sem):
        return (sem, self.cnt[sem])

    def load(self, dst, src, wait=(), sem="dl", queue=None):
        q = queue or ("gpsimd" if sem in ("dg",) else "sync")
        return self.op(q, lambda e: e.dma_start(dst, src), wait=wait,
                       inc=sem, inc_n=16)

    def loadc(self, dst, src, wait=(), sem="dg"):
        return self.op("gpsimd", lambda e: e.dma_start(dst, src), wait=wait,
                       inc=sem, inc_n=16)

    def mm(self, dst, lhsT, rhs, start, stop, wait=(), inc=None):
        return self.op("tensor",
                       lambda t: t.matmul(dst, lhsT, rhs, start=start, stop=stop),
                       wait=wait, inc=inc)


def col_block_ap(w, c0, c1):
    return w.ap()[:, c0:c1].rearrange("(kc p) m -> p kc m", p=128)


def sb3(t, kchunks):
    return t[:].rearrange("p (kc m) -> p kc m", kc=kchunks)


def vec_pg(v):
    return v.ap().rearrange("(g p) -> p g", p=128)


def vec_pm1(v):
    return v.ap().rearrange("(m p) -> p m one", p=128, one=1)


def row1(v, a, b):
    return v.ap()[a:b].rearrange("(one n) -> one n", one=1)


def colv(v, a, b):
    return v.ap()[a:b].rearrange("(p one) -> p one", one=1)


def build_nc(stage="full"):
    C = _host_constants()
    P = Prog()
    nc = P.nc
    st = P.stack

    def din(name, shape, dtype=F32):
        return nc.dram_tensor(name, shape, dtype, kind="ExternalInput")

    x_in = din("x_fmaj", [D, T])
    wqkv0 = din("L0_attn_w_qkv", [D, 4 * D])
    bqkv0 = din("L0_attn_b_qkv", [4 * D])
    wg0 = din("L0_attn_w_g", [D, 2 * D])
    bg0 = din("L0_attn_b_g", [2 * D])
    wp0 = din("L0_attn_w_p", [2 * D, D])
    bp0 = din("L0_attn_b_p", [D])
    w1 = din("L0_ffn_W1", [D, DFF])
    b1 = din("L0_ffn_b1", [DFF])
    qph = din("L0_ffn_quantum_phases", [DFF])
    qwr = din("L0_ffn_quantum_weights_real", [DFF])
    qwi = din("L0_ffn_quantum_weights_imag", [DFF])
    w2 = din("L0_ffn_W2", [DFF, D])
    b2 = din("L0_ffn_b2", [D])
    wqkv1 = din("l1_wqkv_perm", [D, 3 * D])
    bqkv1 = din("l1_bqkv_perm", [3 * D])
    wout1 = din("L1_attn_Wout", [D, D])
    bout1 = din("L1_attn_bout", [D])
    cnw = [din(f"L1_ffn_cn_w{i}", s) for i, s in
           zip(range(1, 5), ([D, D], [D, D // 2], [D // 2, D // 4], [D // 4, 1]))]
    cnb = [din(f"L1_ffn_cn_b{i}", [s]) for i, s in
           zip(range(1, 5), (D, D // 2, D // 4, 1))]
    gcw = [din(f"L1_ffn_gc_w{i}", s) for i, s in
           zip(range(1, 4), ([D, D // 2], [D // 2, D // 4], [D // 4, 1]))]
    gcb = [din(f"L1_ffn_gc_b{i}", [s]) for i, s in
           zip(range(1, 4), (D // 2, D // 4, 1))]
    arange_in = din("L1_ffn_adaptive_range", [1])
    bqkv0_pg = din("b_qkv0_pg", [128, 32])
    bg0_pg = din("bg0_pg", [128, 16])
    bp0_pg = din("bp0_pg", [128, 8])
    b1_pg = din("b1_pg", [128, 32])
    b2_pg = din("b2_pg", [128, 8])
    qph_pg = din("qph_pg", [128, 32])
    qwr_pg = din("qwr_pg", [128, 32])
    qwi_pg = din("qwi_pg", [128, 32])
    bqkv1_pg = din("b_qkv1_pg", [128, 24])
    bout1_pg = din("bout1_pg", [128, 8])
    cnb_pg = [din(f"cnb{i}_pg", [128, g]) for i, g in ((1, 8), (2, 4), (3, 2))]
    cnw4_pg = din("cnw4_pg", [128, 2])
    gcw3_pg = din("gcw3_pg", [128, 2])
    gcb_pg = [din(f"gcb{i}_pg", [128, g]) for i, g in ((1, 4), (2, 2))]
    tab_d = {n: din(n, [D, T], F32)
             for n in ("tab_aq", "tab_bq", "tab_ak", "tab_bk")}
    permT_d = din("permT", [128, 128], BF16)
    permTf_d = din("permT_f32", [128, 128], F32)
    trimask_d = din("trimask", [128, 128], F32)
    selab_d = din("selab", [2, 128], F32)
    sel16_d = din("sel16", [16, 2048], BF16)

    out_d = nc.dram_tensor("out_fmaj", [D, T], F32, kind="ExternalOutput")
    x1_dram = nc.dram_tensor("x1_scratch", [D, T], F32)

    dbg_d = None
    if stage != "full":
        dbg_d = nc.dram_tensor("dbg", [128, 16 * T], F32, kind="ExternalOutput")

    def dump(buf, cols, wait):
        if P.ops is not None:
            P.end()
        P.begin()
        nparts = buf.shape[0]
        dw = P.op("gpsimd",
                  lambda e: e.dma_start(dbg_d.ap()[0:nparts, 0:cols],
                                        buf[:, 0:cols]),
                  wait=wait, inc="dg", inc_n=16)
        P.op("gpsimd", wait=[("dg", dw)])
        P.end()
        return nc

    xn_bf = st.enter_context(nc.sbuf_tensor("xn_bf", [128, NCHUNK * T], BF16))
    ones_bf = st.enter_context(nc.sbuf_tensor("ones_bf", [128, 1], BF16))
    ones_row_bf = st.enter_context(nc.sbuf_tensor("ones_row_bf", [1, 128], BF16))
    ones_row_f = st.enter_context(nc.sbuf_tensor("ones_row_f", [1, 128], F32))
    permT_sb = st.enter_context(nc.sbuf_tensor("permT_sb", [128, 128], BF16))
    permTf_sb = st.enter_context(nc.sbuf_tensor("permTf_sb", [128, 128], F32))
    xn0f = st.enter_context(nc.sbuf_tensor("xn0f", [128, 8], F32))
    trimask_sb = st.enter_context(nc.sbuf_tensor("trimask_sb", [128, 128], F32))
    selab_sb = st.enter_context(nc.sbuf_tensor("selab_sb", [2, 128], F32))
    bqkv0_sb = st.enter_context(nc.sbuf_tensor("bqkv0_sb", [128, 32], F32))
    neg10_sb = st.enter_context(nc.sbuf_tensor("neg10_sb", [128, 1], F32))
    bqkv1_sb = st.enter_context(nc.sbuf_tensor("bqkv1_sb", [128, 24], F32))

    with ExitStack() as sa:
        x_ph = sa.enter_context(nc.sbuf_tensor("x_ph", [128, NCHUNK * T], F32))
        sq_scr = sa.enter_context(nc.sbuf_tensor("sq_scr", [128, 2 * T], BF16))
        stv = sa.enter_context(nc.sbuf_tensor("stv", [1, 6 * T], F32))
        t_scr = sa.enter_context(nc.sbuf_tensor("t_scrA", [128, T], F32))

        P.begin()
        P.load(permT_sb[:], permT_d.ap())
        P.load(permTf_sb[:], permTf_d.ap())
        P.load(trimask_sb[:], trimask_d.ap())
        P.load(selab_sb[:], selab_d.ap())
        P.load(bqkv0_sb[:], bqkv0_pg.ap())
        P.load(bqkv1_sb[:], bqkv1_pg.ap())
        P.op("vector", lambda v: v.memset(ones_bf[:], 1.0))
        P.op("vector", lambda v: v.memset(ones_row_bf[:], 1.0))
        P.op("vector", lambda v: v.memset(ones_row_f[:], 1.0))
        P.op("vector", lambda v: v.memset(neg10_sb[:], -10.0))
        for c in range(NCHUNK):
            P.load(x_ph[:, c * T:(c + 1) * T], x_in.ap()[c * 128:(c + 1) * 128, :])
        xall = P.now("dl")
        with ExitStack() as sp:
            ps_sum = sp.enter_context(nc.psum_tensor("ps_sum", [1, T], F32))
            ps_sq = sp.enter_context(nc.psum_tensor("ps_sq", [1, T], F32))
            stmm = [0] * NCHUNK
            for c in range(NCHUNK):
                xc = xn_bf[:, c * T:(c + 1) * T]
                sqc = sq_scr[:, (c % 2) * T:(c % 2 + 1) * T]
                cv = P.op("vector",
                          lambda v, d=xc, s=x_ph[:, c * T:(c + 1) * T]:
                          v.tensor_copy(d, s),
                          wait=[("dl", xall)], inc="sv")
                w = [("sv", cv)]
                if c >= 2:
                    w.append(("st", stmm[c - 2]))
                cs = P.op("scalar",
                          lambda s, d=sqc, i=xc: s.activation(d, i, AF.Square),
                          wait=w, inc="ss")
                for qc in range(4):
                    P.mm(ps_sum[0:1, qc * 512:(qc + 1) * 512], ones_bf[:],
                         xc[:, qc * 512:(qc + 1) * 512],
                         start=(c == 0), stop=(c == NCHUNK - 1),
                         wait=[("sv", cv)])
                for qc in range(4):
                    stmm[c] = P.mm(ps_sq[0:1, qc * 512:(qc + 1) * 512], ones_bf[:],
                                   sqc[:, qc * 512:(qc + 1) * 512],
                                   start=(c == 0), stop=(c == NCHUNK - 1),
                                   wait=[("ss", cs)], inc="st")
            P.op("vector", lambda v: v.tensor_copy(stv[0:1, 0:T], ps_sum[:]),
                 wait=[("st", P.now("st"))])
            P.op("vector", lambda v: v.tensor_copy(stv[0:1, T:2 * T], ps_sq[:]),
                 inc="sv")
        mu = stv[0:1, 2 * T:3 * T]
        varp = stv[0:1, 3 * T:4 * T]
        aa = stv[0:1, 4 * T:5 * T]
        bb = stv[0:1, 5 * T:6 * T]
        P.op("vector", lambda v: v.tensor_scalar(mu, stv[0:1, 0:T], 1.0 / D, None,
                                                 ALU.mult))
        P.op("vector", lambda v: v.tensor_scalar(varp, stv[0:1, T:2 * T], 1.0 / D,
                                                 1e-5, ALU.mult, ALU.add))
        P.op("vector", lambda v: v.scalar_tensor_tensor(t_scr[0:1, 0:T], mu, -1.0,
                                                        mu, ALU.mult, ALU.mult))
        vst = P.op("vector", lambda v: v.tensor_add(varp, varp, t_scr[0:1, 0:T]),
                   inc="sv")
        P.op("scalar", lambda s: s.activation(varp, varp, AF.Ln),
             wait=[("sv", vst)])
        sst = P.op("scalar", lambda s: s.activation(aa, varp, AF.Exp, scale=-0.5),
                   inc="ss")
        vb = P.op("vector", lambda v: v.scalar_tensor_tensor(bb, mu, -1.0, aa,
                                                             ALU.mult, ALU.mult),
                  wait=[("ss", sst)], inc="sv")
        with ExitStack() as sp:
            ps_a = sp.enter_context(nc.psum_tensor("ps_a", [128, T], F32))
            ps_b = sp.enter_context(nc.psum_tensor("ps_b", [128, T], F32))
            bcw = 0
            for qc in range(4):
                sl = slice(qc * 512, (qc + 1) * 512)
                P.mm(ps_a[:, sl], ones_row_f[:], aa[:, sl], True, True,
                     wait=[("sv", vb)])
                bcw = P.mm(ps_b[:, sl], ones_row_f[:], bb[:, sl], True, True,
                           inc="st")
            for c in range(NCHUNK):
                P.op("vector",
                     lambda v, a=x_ph[:, c * T:(c + 1) * T]:
                     v.tensor_mul(t_scr[:], a, ps_a[:]),
                     wait=[("st", bcw)])
                P.op("vector",
                     lambda v, d=xn_bf[:, c * T:(c + 1) * T]:
                     v.tensor_add(d, t_scr[:], ps_b[:]),
                     inc="sv")
                P.vseq(lambda v, c_=c: v.tensor_add(xn0f[:, c_:c_ + 1],
                                                    t_scr[:, 0:1],
                                                    ps_b[:, 0:1]))
        P.end()

    if stage == "xn0":
        return dump(xn_bf[:], 8 * T, [("sv", P.now("sv"))])

    retg = None
    with ExitStack() as sl0:
        retg = sl0.enter_context(nc.sbuf_tensor("retg", [128, 16 * T], BF16))
        gnab = sl0.enter_context(nc.sbuf_tensor("gnab", [16, T], BF16))
        sl0h = ExitStack()
        wq_sb = sl0h.enter_context(nc.sbuf_tensor("wq_sb", [128, 8 * 128], BF16))
        wk_sb = sl0h.enter_context(nc.sbuf_tensor("wk_sb", [128, 8 * 128], BF16))
        wv_sb = sl0h.enter_context(nc.sbuf_tensor("wv_sb", [128, 8 * 256], BF16))
        bv_sb = sl0h.enter_context(nc.sbuf_tensor("bv_sb", [1, 256], BF16))
        tabA = sl0h.enter_context(nc.sbuf_tensor("tabA", [128, T], F32))
        tabB = sl0h.enter_context(nc.sbuf_tensor("tabB", [128, T], F32))
        qk_bf = sl0h.enter_context(nc.sbuf_tensor("qk_bf", [128, T], BF16))
        qt_sb = sl0h.enter_context(nc.sbuf_tensor("qt_sb", [128, T], BF16))
        kt_sb = sl0h.enter_context(nc.sbuf_tensor("kt_sb", [128, T], BF16))
        qtf = sl0h.enter_context(nc.sbuf_tensor("qtf", [128, T], F32))
        ktf = sl0h.enter_context(nc.sbuf_tensor("ktf", [128, T], F32))
        wqf_sb = sl0h.enter_context(nc.sbuf_tensor("wqf_sb", [128, 8 * 128], F32))
        q0s = sl0h.enter_context(nc.sbuf_tensor("q0s", [128, 4], F32))
        t1s = sl0h.enter_context(nc.sbuf_tensor("t1s", [128, T], F32))
        v_sb = sl0h.enter_context(nc.sbuf_tensor("v_sb", [128, 16 * 256], BF16))
        pnow = [sl0h.enter_context(nc.sbuf_tensor(f"pnow{i}", [128, 1024], BF16))
                for i in range(2)]
        pstore = sl0h.enter_context(nc.sbuf_tensor("pstore", [128, 8 * 1024], BF16))
        sq2 = sl0h.enter_context(nc.sbuf_tensor("sq2", [128, 1024], BF16))
        gn_scr = sl0h.enter_context(nc.sbuf_tensor("gn_scr", [1, 4 * 512], F32))

        ps_gn = sl0.enter_context(nc.psum_tensor("ps_gn_l0", [1, 512], F32))
        P.begin()
        st_state = {}
        gevp = 0
        gn_free = 0
        dma_free = 0

        stats_done_st = 0
        stats_queue = []

        def flush_stats(n=1):
            for _ in range(min(n, len(stats_queue))):
                stats_queue.pop(0)()

        def queue_gnstats(hh, gevs):
            for qh_ in range(2):
                for qc_ in range(2):
                    stats_queue.append(
                        lambda hh_=hh, g_=tuple(gevs), a_=qh_, b_=qc_:
                        emit_gnstat_unit(hh_, g_, a_, b_))

        def emit_gnstat_unit(hh, gevs, qh, qc):
            nonlocal gn_free, dma_free, stats_done_st
            vcb = 2 * hh
            if True:
                q0h = qh * 1024
                if True:
                    gmu = gn_scr[0:1, 0:512]
                    gva = gn_scr[0:1, 512:1024]
                    gtmp = gn_scr[0:1, 1024:1536]
                    gb_ = gn_scr[0:1, 1536:2048]
                    gmm = 0
                    sqw = 0
                    for dvc in range(2):
                        rsl = retg[:, (vcb + dvc) * T + q0h + qc * 512:
                                   (vcb + dvc) * T + q0h + (qc + 1) * 512]
                        wst = [("sv", gevs[qh])]
                        if gn_free:
                            wst.append(("sv", gn_free))
                        gmm = P.mm(ps_gn[0:1, 0:512], ones_bf[:], rsl,
                                   start=(dvc == 0), stop=(dvc == 1), wait=wst,
                                   inc="st" if dvc == 1 else None)
                        wsq = [("sv", gevs[qh])]
                        if stats_done_st:
                            wsq.append(("st", stats_done_st))
                        sqw = P.op("scalar",
                                   lambda s,
                                   d=sq2[:, dvc * 512:(dvc + 1) * 512],
                                   i=rsl: s.activation(d, i, AF.Square),
                                   wait=wsq, inc="ss")
                    dw = [("st", gmm)]
                    if dma_free:
                        dw.append(("dg", dma_free))
                    gmur = P.vseq(lambda v: v.tensor_scalar(gmu,
                                                            ps_gn[0:1, 0:512],
                                                            1.0 / DV0, None,
                                                            ALU.mult), wait=dw)
                    for dvc in range(2):
                        gmm = P.mm(ps_gn[0:1, 0:512], ones_bf[:],
                                   sq2[:, dvc * 512:(dvc + 1) * 512],
                                   start=(dvc == 0), stop=(dvc == 1),
                                   wait=[("ss", sqw), ("sv", gmur)],
                                   inc="st" if dvc == 1 else None)
                    stats_done_st = gmm
                    P.vseq(lambda v: v.tensor_scalar(gva, ps_gn[0:1, 0:512],
                                                     1.0 / DV0, 1e-5, ALU.mult,
                                                     ALU.add),
                           wait=[("st", gmm)])
                    P.vseq(lambda v: v.scalar_tensor_tensor(gtmp, gmu, -1.0,
                                                            gmu, ALU.mult,
                                                            ALU.mult))
                    gn_free = P.vseq(lambda v: v.tensor_add(gva, gva, gtmp))
                    P.sseq(lambda s: s.activation(gva, gva, AF.Ln),
                           wait=[("sv", gn_free)])
                    gns = P.sseq(lambda s: s.activation(gva, gva, AF.Exp,
                                                        scale=-0.5))
                    gnb = P.vseq(lambda v: v.scalar_tensor_tensor(
                        gb_, gmu, -1.0, gva, ALU.mult, ALU.mult),
                        wait=[("ss", gns)])
                    col = q0h + qc * 512
                    P.loadc(gnab[hh:hh + 1, col:col + 512], gva,
                            wait=[("sv", gnb)])
                    dma_free = P.loadc(gnab[8 + hh:9 + hh, col:col + 512], gb_)

        for h in range(H0):
            S = st_state
            P.load(tabA[:], tab_d["tab_aq"].ap()[h * 128:(h + 1) * 128, :],
                   wait=[("sv", S.get("vk0", 0))])
            P.load(tabB[:], tab_d["tab_bq"].ap()[h * 128:(h + 1) * 128, :])
            P.load(sb3(wqf_sb, 8), col_block_ap(wqkv0, h * 128, (h + 1) * 128),
                   wait=[("st", S.get("t0k_mm", 0))])
            ldl = P.now("dl")
            P.loadc(sb3(wq_sb, 8), col_block_ap(wqkv0, h * 128, (h + 1) * 128),
                    wait=[("st", S.get("qkv_end", 0))])
            P.loadc(sb3(wk_sb, 8),
                    col_block_ap(wqkv0, D + h * 128, D + (h + 1) * 128))
            P.loadc(sb3(wv_sb, 8),
                    col_block_ap(wqkv0, 2 * D + h * 256, 2 * D + (h + 1) * 256))
            P.loadc(bv_sb[:], row1(bqkv0, 2 * D + h * 256, 2 * D + (h + 1) * 256))
            ldg = P.now("dg")

            with ExitStack() as sp:
                ps_qk = sp.enter_context(nc.psum_tensor(f"ps_qk{h}", [128, T],
                                                        F32))
                ps_v = [sp.enter_context(nc.psum_tensor(f"ps_v{h}_{i}",
                                                        [128, 256], F32))
                        for i in range(2)]
                ps_t0 = sp.enter_context(nc.psum_tensor(f"ps_t0{h}", [128, 4],
                                                        F32))
                vev = [0, 0]

                def vtile(tt):
                    par = tt % 2
                    w = [("dg", ldg)]
                    if vev[par]:
                        w.append(("sv", vev[par]))
                    for kc in range(8):
                        P.mm(ps_v[par][:],
                             xn_bf[:, kc * T + tt * 128:kc * T + (tt + 1) * 128],
                             wv_sb[:, kc * 256:(kc + 1) * 256],
                             start=(kc == 0), stop=False, wait=w)
                    vmm = P.mm(ps_v[par][:], ones_row_bf[:], bv_sb[:],
                               start=False, stop=True, inc="st")
                    vev[par] = P.op(
                        "vector",
                        lambda v, d=v_sb[:, tt * 256:(tt + 1) * 256],
                        s=ps_v[par]: v.tensor_copy(d, s[:]),
                        wait=[("st", vmm)], inc="sv")

                qmm = 0
                for qn in range(4):
                    for kc in range(8):
                        qmm = P.mm(ps_qk[:, qn * 512:(qn + 1) * 512],
                                   wq_sb[:, kc * 128:(kc + 1) * 128],
                                   xn_bf[:, kc * T + qn * 512:kc * T + (qn + 1) * 512],
                                   start=(kc == 0), stop=(kc == 7),
                                   wait=[("dg", ldg), ("sv", gevp)],
                                   inc="st" if kc == 7 else None)
                flush_stats()
                eq = P.op("scalar",
                          lambda s, b=bqkv0_sb[:, h:h + 1], p=ps_qk:
                          s.activation(t1s[:], p[:], AF.Identity, bias=b),
                          wait=[("st", qmm), ("dl", ldl)], inc="ss")
                eqb = P.op("vector", lambda v: v.tensor_copy(qk_bf[:], t1s[:]),
                           wait=[("ss", eq)], inc="sv")
                vtile(0)
                vtile(1)
                rmm = 0
                for qn in range(4):
                    rmm = P.mm(ps_qk[:, qn * 512:(qn + 1) * 512], permT_sb[:],
                               qk_bf[:, qn * 512:(qn + 1) * 512], True, True,
                               wait=[("sv", eqb), ("ss", eq)],
                               inc="st" if qn == 3 else None)
                P.op("vector", lambda v: v.tensor_mul(t1s[:], t1s[:], tabA[:]),
                     wait=[("ss", eq), ("dl", ldl), ("sv", eqb)])
                P.op("vector",
                     lambda v, p=ps_qk: v.scalar_tensor_tensor(
                         qtf[:], p[:], 1.0, tabB[:], ALU.mult, ALU.mult),
                     wait=[("st", rmm)])
                P.op("vector", lambda v: v.tensor_add(qtf[:], t1s[:], qtf[:]))
                vq = P.op("vector", lambda v: v.tensor_copy(qt_sb[:], qtf[:]),
                          inc="sv")
                vtile(2)
                vtile(3)
                flush_stats()
                t0m = 0
                for kc in range(8):
                    t0m = P.mm(ps_t0[:, 0:1], wqf_sb[:, kc * 128:(kc + 1) * 128],
                               xn0f[:, kc:kc + 1], start=(kc == 0),
                               stop=(kc == 7), wait=[("dl", ldl)],
                               inc="st" if kc == 7 else None)
                t0e = P.sseq(lambda s, b=bqkv0_sb[:, h:h + 1], p=ps_t0:
                             s.activation(q0s[:, 0:1], p[:, 0:1],
                                          AF.Identity, bias=b),
                             wait=[("st", t0m)])
                t0r = P.mm(ps_t0[:, 1:2], permTf_sb[:], q0s[:, 0:1], True,
                           True, wait=[("ss", t0e)], inc="st")
                P.vseq(lambda v: v.tensor_mul(q0s[:, 1:2], q0s[:, 0:1],
                                              tabA[:, 0:1]),
                       wait=[("sv", vq), ("ss", t0e)])
                P.vseq(lambda v, p=ps_t0: v.scalar_tensor_tensor(
                    qtf[:, 0:1], p[:, 1:2], 1.0, tabB[:, 0:1], ALU.mult,
                    ALU.mult), wait=[("st", t0r)])
                vq0 = P.vseq(lambda v: v.tensor_add(qtf[:, 0:1], qtf[:, 0:1],
                                                    q0s[:, 1:2]))
                kmm = 0
                for qn in range(4):
                    for kc in range(8):
                        kmm = P.mm(ps_qk[:, qn * 512:(qn + 1) * 512],
                                   wk_sb[:, kc * 128:(kc + 1) * 128],
                                   xn_bf[:, kc * T + qn * 512:kc * T + (qn + 1) * 512],
                                   start=(kc == 0), stop=(kc == 7),
                                   wait=[("sv", vq)],
                                   inc="st" if kc == 7 else None)
                P.load(tabA[:], tab_d["tab_ak"].ap()[h * 128:(h + 1) * 128, :],
                       wait=[("sv", vq0)])
                P.load(tabB[:], tab_d["tab_bk"].ap()[h * 128:(h + 1) * 128, :])
                P.load(sb3(wqf_sb, 8),
                       col_block_ap(wqkv0, D + h * 128, D + (h + 1) * 128),
                       wait=[("ss", t0e)])
                ldk = P.now("dl")
                flush_stats()
                ek = P.op("scalar",
                          lambda s, b=bqkv0_sb[:, 8 + h:9 + h], p=ps_qk:
                          s.activation(t1s[:], p[:], AF.Identity, bias=b),
                          wait=[("st", kmm)], inc="ss")
                ekb = P.op("vector", lambda v: v.tensor_copy(qk_bf[:], t1s[:]),
                           wait=[("ss", ek)], inc="sv")
                vtile(4)
                vtile(5)
                rkm = 0
                for qn in range(4):
                    rkm = P.mm(ps_qk[:, qn * 512:(qn + 1) * 512], permT_sb[:],
                               qk_bf[:, qn * 512:(qn + 1) * 512], True, True,
                               wait=[("sv", ekb), ("ss", ek)],
                               inc="st" if qn == 3 else None)
                P.op("vector", lambda v: v.tensor_mul(t1s[:], t1s[:], tabA[:]),
                     wait=[("ss", ek), ("dl", ldk), ("sv", ekb)])
                P.op("vector",
                     lambda v, p=ps_qk: v.scalar_tensor_tensor(
                         ktf[:], p[:], 1.0, tabB[:], ALU.mult, ALU.mult),
                     wait=[("st", rkm)])
                P.op("vector", lambda v: v.tensor_add(ktf[:], t1s[:], ktf[:]))
                vk = P.op("vector", lambda v: v.tensor_copy(kt_sb[:], ktf[:]),
                          inc="sv")
                for tt in range(6, 12):
                    vtile(tt)
                flush_stats()
                t0m = 0
                for kc in range(8):
                    t0m = P.mm(ps_t0[:, 2:3], wqf_sb[:, kc * 128:(kc + 1) * 128],
                               xn0f[:, kc:kc + 1], start=(kc == 0),
                               stop=(kc == 7), wait=[("dl", ldk)],
                               inc="st" if kc == 7 else None)
                st_state = {"t0k_mm": t0m}
                t0e = P.sseq(lambda s, b=bqkv0_sb[:, 8 + h:9 + h], p=ps_t0:
                             s.activation(q0s[:, 2:3], p[:, 2:3],
                                          AF.Identity, bias=b),
                             wait=[("st", t0m)])
                t0r = P.mm(ps_t0[:, 3:4], permTf_sb[:], q0s[:, 2:3], True,
                           True, wait=[("ss", t0e)], inc="st")
                P.vseq(lambda v: v.tensor_mul(q0s[:, 3:4], q0s[:, 2:3],
                                              tabA[:, 0:1]),
                       wait=[("sv", vk), ("ss", t0e)])
                P.vseq(lambda v, p=ps_t0: v.scalar_tensor_tensor(
                    ktf[:, 0:1], p[:, 3:4], 1.0, tabB[:, 0:1], ALU.mult,
                    ALU.mult), wait=[("st", t0r)])
                vk0 = P.vseq(lambda v: v.tensor_add(ktf[:, 0:1], ktf[:, 0:1],
                                                    q0s[:, 3:4]))
                for tt in range(12, 16):
                    vtile(tt)
                st_state["vk0"] = vk0
                st_state["qkv_end"] = P.now("st")

            with ExitStack() as sp:
                ps_s = [sp.enter_context(nc.psum_tensor(f"ps_s{h}_{i}",
                                                        [128, 512], F32))
                        for i in range(2)]
                ps_av = sp.enter_context(nc.psum_tensor(f"ps_av{h}",
                                                        [128, 2048], F32))
                vready = P.now("sv")
                evv = [0, 0]
                evs = [0, 0]
                avdone = [0, 0]
                avprev = 0
                gevs = []
                for qh in range(2):
                    q0h = qh * 1024
                    lastk = 7 if qh == 0 else 15
                    for tk in range(lastk + 1):
                        par = tk % 2
                        k0 = tk * 128
                        if qh == 0 or tk >= 8:
                            spans = [(k0, k0 + 128, True)]
                            q = k0 + 128
                            while q < T:
                                qe = min(q + 512, T)
                                spans.append((q, qe, False))
                                q = qe
                            for (qa, qb2, isdiag) in spans:
                                w = [("sv", vready)]
                                if evv[par]:
                                    w.append(("sv", evv[par]))
                                if evs[par]:
                                    w.append(("ss", evs[par]))
                                if isdiag:
                                    if tk == 0:
                                        w.append(("sv", vk0))
                                    smm = P.mm(ps_s[par][:, 0:qb2 - qa],
                                               ktf[:, k0:k0 + 128],
                                               qtf[:, qa:qb2], True, True,
                                               wait=w, inc="st")
                                else:
                                    smm = P.mm(ps_s[par][:, 0:qb2 - qa],
                                               kt_sb[:, k0:k0 + 128],
                                               qt_sb[:, qa:qb2], True, True,
                                               wait=w, inc="st")
                                dsts = []
                                if qh == 0:
                                    if qa < 1024:
                                        bb2 = min(qb2, 1024)
                                        dsts.append((pnow[par][:, qa:bb2], qa,
                                                     bb2, True))
                                    if qb2 > 1024:
                                        aa2 = max(qa, 1024)
                                        dsts.append(
                                            (pstore[:, tk * 1024 + aa2 - 1024:
                                                     tk * 1024 + qb2 - 1024],
                                             aa2, qb2, False))
                                else:
                                    dsts.append(
                                        (pnow[par][:, qa - 1024:qb2 - 1024],
                                         qa, qb2, True))
                                for dst, a2, b2_, ispn in dsts:
                                    wv2 = [("st", smm)]
                                    if ispn and avdone[par]:
                                        wv2.append(("st", avdone[par]))
                                    if isdiag:
                                        evv[par] = P.op(
                                            "vector",
                                            lambda v, d=dst,
                                            s=ps_s[par][:, a2 - qa:b2_ - qa]:
                                            v.tensor_mul(d, s, trimask_sb[:]),
                                            wait=wv2, inc="sv")
                                    else:
                                        evs[par] = P.op(
                                            "scalar",
                                            lambda s_, d=dst,
                                            s=ps_s[par][:, a2 - qa:b2_ - qa]:
                                            s_.activation(d, s, AF.Identity),
                                            wait=wv2, inc="ss")
                        qa = max(k0, q0h)
                        qb2 = q0h + 1024
                        first = (tk == 0)
                        if qh == 0 or tk >= 8:
                            srcbuf, off = pnow[par], q0h
                        else:
                            srcbuf, off = pstore, 1024 - tk * 1024
                        w = [("sv", evv[par]), ("ss", evs[par])]
                        if first and gevp:
                            w.append(("sv", gevp))
                        q = qa
                        while q < qb2:
                            if q % 512:
                                qe = min(qb2, (q // 512 + 1) * 512)
                            else:
                                qe = min(q + 512, qb2)
                            for dvc in range(2):
                                avprev = P.mm(
                                    ps_av[:, dvc * 1024 + q - q0h:
                                          dvc * 1024 + qe - q0h],
                                    v_sb[:, tk * 256 + dvc * 128:
                                         tk * 256 + (dvc + 1) * 128],
                                    srcbuf[:, q - off:qe - off],
                                    start=first, stop=(tk == lastk), wait=w,
                                    inc="st")
                            q = qe
                        avdone[par] = avprev
                    vcb = 2 * h
                    gev = 0
                    for dvc in range(2):
                        gev = P.op(
                            "vector",
                            lambda v,
                            d=retg[:, (vcb + dvc) * T + q0h:
                                   (vcb + dvc) * T + q0h + 1024],
                            s=ps_av[:, dvc * 1024:(dvc + 1) * 1024]:
                            v.tensor_copy(d, s),
                            wait=[("st", avprev)], inc="sv")
                    gevp = gev
                    gevs.append(gev)
                queue_gnstats(h, gevs)
        flush_stats(99)
        P.end()
        sl0h.close()

        if stage == "ret0":
            return dump(retg[:], 16 * T, [("sv", P.now("sv"))])

        with ExitStack() as s2:
            wgf_sb = s2.enter_context(nc.sbuf_tensor("wgf_sb", [128, 8 * 2048],
                                                     BF16))
            bg_sb = s2.enter_context(nc.sbuf_tensor("bg_sb", [128, 16], F32))
            sel16_sb = s2.enter_context(nc.sbuf_tensor("sel16_sb", [16, 2048],
                                                       BF16))
            gate_scr = [s2.enter_context(nc.sbuf_tensor(f"gate{i}", [128, 512],
                                                        F32)) for i in range(2)]
            tp_scr = [s2.enter_context(nc.sbuf_tensor(f"tp{i}", [128, 512],
                                                      F32)) for i in range(2)]
            P.begin()
            P.loadc(sb3(wgf_sb, 8),
                    wg0.ap().rearrange("(kc p) m -> p kc m", p=128))
            gw = P.now("dg")
            P.load(bg_sb[:], bg0_pg.ap())
            P.load(sel16_sb[:], sel16_d.ap())
            bgw = P.now("dl")
            gnabw = P.now("dg")
            with ExitStack() as sp:
                ps_g = [sp.enter_context(nc.psum_tensor(f"ps_g{i}", [128, 512],
                                                        F32)) for i in range(2)]
                ps_bca = [sp.enter_context(nc.psum_tensor(f"ps_bca{i}",
                                                          [128, 512], F32))
                          for i in range(2)]
                ps_bcb = [sp.enter_context(nc.psum_tensor(f"ps_bcb{i}",
                                                          [128, 512], F32))
                          for i in range(2)]
                gev = [0, 0]
                i = 0
                for h in range(H0):
                    for dvc in range(2):
                        vc = 2 * h + dvc
                        for qn in range(4):
                            par = i % 2
                            i += 1
                            w = [("dg", gw), ("dg", gnabw), ("dl", bgw)]
                            if gev[par]:
                                w.append(("sv", gev[par]))
                            bca = P.mm(ps_bca[par][:],
                                       sel16_sb[:, h * 128:(h + 1) * 128],
                                       gnab[:, qn * 512:(qn + 1) * 512],
                                       True, True, wait=w, inc="st")
                            bcb = P.mm(ps_bcb[par][:],
                                       sel16_sb[:, (8 + h) * 128:(9 + h) * 128],
                                       gnab[:, qn * 512:(qn + 1) * 512],
                                       True, True, inc="st")
                            gm = 0
                            for kc in range(8):
                                gm = P.mm(ps_g[par][:],
                                          wgf_sb[:, kc * 2048 + vc * 128:
                                                 kc * 2048 + (vc + 1) * 128],
                                          xn_bf[:, kc * T + qn * 512:
                                                kc * T + (qn + 1) * 512],
                                          start=(kc == 0), stop=(kc == 7),
                                          inc="st" if kc == 7 else None)
                            sev = P.op("scalar",
                                       lambda s, d=gate_scr[par], p=ps_g[par],
                                       v_=vc: s.activation(
                                           d[:], p[:], AF.Silu,
                                           bias=bg_sb[:, v_:v_ + 1]),
                                       wait=[("st", gm)], inc="ss")
                            rsl = retg[:, vc * T + qn * 512:vc * T + (qn + 1) * 512]
                            P.op("vector",
                                 lambda v, r=rsl, p=par: v.tensor_mul(
                                     tp_scr[p][:], r, ps_bca[p][:]),
                                 wait=[("st", bcb)])
                            P.op("vector",
                                 lambda v, p=par: v.tensor_add(
                                     tp_scr[p][:], tp_scr[p][:], ps_bcb[p][:]))
                            gev[par] = P.op("vector",
                                            lambda v, r=rsl, p=par,
                                            g=gate_scr[par]:
                                            v.tensor_mul(r, tp_scr[p][:], g[:]),
                                            wait=[("ss", sev)], inc="sv")
            P.end()

        if stage == "gated":
            return dump(retg[:], 16 * T, [("sv", P.now("sv"))])

        sacc = st.enter_context(nc.sbuf_tensor("sacc", [128, NCHUNK * T], BF16,
                                               side="right"))
        with ExitStack() as s3:
            wp_sb = [s3.enter_context(nc.sbuf_tensor(f"wp_sb{i}", [128, 16 * 128],
                                                     BF16)) for i in range(2)]
            bp_sb = s3.enter_context(nc.sbuf_tensor("bp_sb", [128, 8], F32))
            P.begin()
            P.load(bp_sb[:], bp0_pg.ap())
            bpw = P.now("dl")
            gatedone = P.now("sv")
            with ExitStack() as sp:
                ps_wp = [sp.enter_context(nc.psum_tensor(f"ps_wp{i}", [128, 512],
                                                         F32)) for i in range(2)]
                wsem = ["dwa", "dwb"]
                wl = [P.loadc(sb3(wp_sb[i], 16),
                              col_block_ap(wp0, i * 128, (i + 1) * 128),
                              sem=wsem[i]) for i in range(2)]
                wpev = [0, 0]
                for oc in range(8):
                    bpar = oc % 2
                    for qc in range(4):
                        par = (oc * 4 + qc) % 2
                        w = [(wsem[bpar], wl[bpar]), ("sv", gatedone)]
                        if wpev[par]:
                            w.append(("ss", wpev[par]))
                        pm = 0
                        for vc in range(16):
                            pm = P.mm(ps_wp[par][:],
                                      wp_sb[bpar][:, vc * 128:(vc + 1) * 128],
                                      retg[:, vc * T + qc * 512:
                                           vc * T + (qc + 1) * 512],
                                      start=(vc == 0), stop=(vc == 15), wait=w,
                                      inc="st" if vc == 15 else None)
                        wpev[par] = P.op(
                            "scalar",
                            lambda s, d=sacc[:, oc * T + qc * 512:
                                             oc * T + (qc + 1) * 512],
                            p=ps_wp[par], o=oc: s.activation(
                                d, p[:], AF.Identity, bias=bp_sb[:, o:o + 1]),
                            wait=[("st", pm), ("dl", bpw)], inc="ss")
                    if oc + 2 < 8:
                        wl[bpar] = P.loadc(sb3(wp_sb[bpar], 16),
                                           col_block_ap(wp0, (oc + 2) * 128,
                                                        (oc + 3) * 128),
                                           wait=[("st", P.now("st"))], sem=wsem[bpar])
            P.end()

    if stage == "s0":
        return dump(sacc[:], 8 * T, [("ss", P.now("ss"))])

    sln = ExitStack()
    lnst = sln.enter_context(nc.sbuf_tensor("lnst", [1, 2 * T], BF16))
    with ExitStack() as s4:
        act = s4.enter_context(nc.sbuf_tensor("act", [128, 32 * 1024], BF16))
        w1_sb = [s4.enter_context(nc.sbuf_tensor(f"w1_sb{i}", [128, 8 * 128],
                                                 BF16)) for i in range(2)]
        w2_sb = [s4.enter_context(nc.sbuf_tensor(f"w2_sb{i}", [128, 32 * 128],
                                                 BF16)) for i in range(2)]
        qv = s4.enter_context(nc.sbuf_tensor("qv", [128, 8 * 32], F32))
        b2_sb = s4.enter_context(nc.sbuf_tensor("b2_sb", [128, 8], F32))
        s01 = s4.enter_context(nc.sbuf_tensor("s01", [128, 512], F32))
        c01 = s4.enter_context(nc.sbuf_tensor("c01", [128, 512], F32))
        tqs = s4.enter_context(nc.sbuf_tensor("tqs", [128, 512], F32))
        g_scr = s4.enter_context(nc.sbuf_tensor("g_scr", [128, 512], F32))
        xi_scr = [s4.enter_context(nc.sbuf_tensor(f"xi{i}", [128, 512], F32))
                  for i in range(2)]
        x2_scr = [s4.enter_context(nc.sbuf_tensor(f"x2s{i}", [128, 512], F32))
                  for i in range(2)]
        x2b_scr = [s4.enter_context(nc.sbuf_tensor(f"x2b{i}", [128, 512], BF16))
                   for i in range(2)]
        sq3 = s4.enter_context(nc.sbuf_tensor("sq3", [128, 512], BF16))

        P.begin()
        P.load(qv[:, 5 * 32:6 * 32], qph_pg.ap())
        P.load(qv[:, 6 * 32:7 * 32], qwr_pg.ap())
        P.load(qv[:, 7 * 32:8 * 32], qwi_pg.ap())
        P.load(qv[:, 0:32], b1_pg.ap())
        P.load(b2_sb[:], b2_pg.ap())
        qload = P.now("dl")

        def qrow(g):
            return qv[:, g * 32:(g + 1) * 32]
        b1v, av_, bv_, b1s, b1c = qrow(0), qrow(1), qrow(2), qrow(3), qrow(4)
        th, wr, wi = qrow(5), qrow(6), qrow(7)
        scr1 = s01[:, 0:32]
        scr2 = s01[:, 32:64]
        sinth = s01[:, 64:96]
        costh = s01[:, 96:128]

        def rngred(dst, src, extra, wait=()):
            P.vseq(lambda v: v.tensor_scalar(scr1, src, 1.0 / (2 * PI),
                                             extra / (2 * PI), ALU.mult,
                                             ALU.add), wait=wait)
            P.vseq(lambda v: v.tensor_scalar(scr2, scr1, MAGIC, None, ALU.add))
            P.vseq(lambda v: v.tensor_scalar(scr2, scr2, -MAGIC, None, ALU.add))
            P.vseq(lambda v: v.tensor_sub(scr1, scr1, scr2))
            return P.vseq(lambda v: v.tensor_scalar(dst, scr1, 2 * PI, None,
                                                    ALU.mult))
        r1 = rngred(scr2, th, 0.0, wait=[("dl", qload)])
        s1_ = P.op("scalar", lambda s: s.activation(sinth, scr2, AF.Sin),
                   wait=[("sv", r1)], inc="ss")
        r2 = rngred(scr2, th, PI / 2, wait=[("ss", s1_)])
        s2_ = P.op("scalar", lambda s: s.activation(costh, scr2, AF.Sin),
                   wait=[("sv", r2)], inc="ss")
        P.vseq(lambda v: v.tensor_mul(scr1, costh, wr), wait=[("ss", s2_)])
        P.vseq(lambda v: v.tensor_mul(scr2, sinth, wi))
        P.vseq(lambda v: v.tensor_add(scr1, scr1, scr2))
        P.vseq(lambda v: v.tensor_mul(scr2, scr1, wi))
        P.vseq(lambda v: v.tensor_scalar(scr2, scr2, 0.1, None, ALU.mult))
        P.vseq(lambda v: v.tensor_mul(av_, scr2, sinth))
        P.vseq(lambda v: v.tensor_mul(bv_, scr2, costh))
        P.vseq(lambda v: v.tensor_scalar(b1s, b1v, 0.1, None, ALU.mult))
        qprep = P.vseq(lambda v: v.tensor_scalar(b1c, b1v, 0.1, PI / 2,
                                                 ALU.mult, ALU.add))

        if stage == "qprep":
            return dump(qv[:], 8 * 32, [("sv", qprep)])
        with ExitStack() as sp:
            ps_h = [sp.enter_context(nc.psum_tensor(f"ps_h{i}", [128, 512], F32))
                    for i in range(4)]
            ps_o = [sp.enter_context(nc.psum_tensor(f"ps_o{i}", [128, 512], F32))
                    for i in range(2)]
            ps_st = sp.enter_context(nc.psum_tensor("ps_st", [1, 1024], F32))
            wsem = ["dwa", "dwb"]
            xsem = ["dxa", "dxb"]
            ssem_ = ["dsa", "dsb"]
            w1l = [P.loadc(sb3(w1_sb[i], 8), col_block_ap(w1, i * 128,
                                                          (i + 1) * 128),
                           sem=wsem[i]) for i in range(2)]
            w2ld = [0, 0]
            w2done = 0
            psfree = [0, 0]
            x2free = [0, 0]
            xifree = [0, 0]
            stm = 0
            sqe = 0
            actev = [0, 0, 0, 0]
            for fq in range(2):
                hoff = fq * 1024
                for unit in range(64):
                    mc, qn2 = unit // 2, unit % 2
                    par = mc % 2
                    par4 = unit % 4
                    w = [(wsem[par], w1l[par]), ("sv", qprep)]
                    if actev[par4]:
                        w.append(("sv", actev[par4]))
                    if w2done:
                        w.append(("st", w2done))
                    hm = 0
                    for kc in range(8):
                        hm = P.mm(ps_h[par4][:],
                                  w1_sb[par][:, kc * 128:(kc + 1) * 128],
                                  xn_bf[:, kc * T + hoff + qn2 * 512:
                                        kc * T + hoff + (qn2 + 1) * 512],
                                  start=(kc == 0), stop=(kc == 7), wait=w,
                                  inc="st" if kc == 7 else None)
                    if qn2 == 1:
                        idx = fq * 32 + mc
                        if idx + 2 < 64:
                            nb = (idx + 2) % 32
                            w1l[par] = P.loadc(sb3(w1_sb[par], 8),
                                               col_block_ap(w1, nb * 128,
                                                            (nb + 1) * 128),
                                               wait=[("st", P.now("st"))],
                                               sem=wsem[par])
                    P.op("scalar",
                         lambda s, p=ps_h[par4], m=mc: s.activation(
                             s01[:], p[:], AF.Sin, bias=b1s[:, m:m + 1],
                             scale=0.1),
                         wait=[("st", hm)])
                    se = P.op("scalar",
                              lambda s, p=ps_h[par4], m=mc: s.activation(
                                  c01[:], p[:], AF.Sin, bias=b1c[:, m:m + 1],
                                  scale=0.1),
                              inc="ss")
                    P.op("vector",
                         lambda v, p=ps_h[par4], m=mc: v.scalar_tensor_tensor(
                             tqs[:], c01[:], av_[:, m:m + 1], p[:], ALU.mult,
                             ALU.add),
                         wait=[("ss", se)])
                    P.op("vector",
                         lambda v, m=mc: v.scalar_tensor_tensor(
                             tqs[:], s01[:], bv_[:, m:m + 1], tqs[:], ALU.mult,
                             ALU.add))
                    actev[par4] = P.op(
                        "vector",
                        lambda v, m=mc,
                        d=act[:, mc * 1024 + qn2 * 512:
                              mc * 1024 + (qn2 + 1) * 512]:
                        v.tensor_scalar(d, tqs[:], b1v[:, m:m + 1], 0.0, ALU.add,
                                        ALU.max),
                        inc="sv")
                if stage == "act0" and fq == 0:
                    return dump(act[:], 32 * 512, [("sv", P.now("sv"))])
                actall = P.now("sv")
                w2ld = [P.loadc(sb3(w2_sb[i], 32),
                                col_block_ap(w2, i * 128, (i + 1) * 128),
                                wait=[("st", P.now("st"))], sem=wsem[i])
                        for i in range(2)]
                stprev = [(0, 0), (0, 0)]
                for qc in range(2):
                    toks = slice(hoff + qc * 512, hoff + (qc + 1) * 512)
                    pend_lnstat = None
                    for oc in range(8):
                        par = oc % 2
                        w = [(wsem[par], w2ld[par]), ("sv", actall)]
                        if psfree[par]:
                            w.append(("ss", psfree[par]))
                        om = 0
                        for vc in range(32):
                            om = P.mm(ps_o[par][:],
                                      w2_sb[par][:, vc * 128:(vc + 1) * 128],
                                      act[:, vc * 1024 + qc * 512:
                                          vc * 1024 + (qc + 1) * 512],
                                      start=(vc == 0), stop=(vc == 31), wait=w,
                                      inc="st" if vc == 31 else None)
                        if pend_lnstat is not None:
                            pend_lnstat()
                            pend_lnstat = None
                        j = qc * 8 + oc
                        if j + 2 < 16:
                            nb2 = (j + 2) % 8
                            w2ld[par] = P.loadc(
                                sb3(w2_sb[par], 32),
                                col_block_ap(w2, nb2 * 128, (nb2 + 1) * 128),
                                wait=[("st", P.now("st"))], sem=wsem[par])
                        xw = [("sv", xifree[par])] if xifree[par] else []
                        xl = P.load(xi_scr[par][:],
                                    x_in.ap()[oc * 128:(oc + 1) * 128, toks],
                                    wait=xw, sem=xsem[par])
                        ge = psfree[par] = P.op(
                            "scalar",
                            lambda s, p=ps_o[par], o=oc: s.activation(
                                g_scr[:], p[:], AF.Silu, bias=b2_sb[:, o:o + 1]),
                            wait=[("st", om), ("dl", qload)], inc="ss")
                        w2_ = [("ss", ge)]
                        if x2free[par]:
                            w2_.append((ssem_[par], x2free[par]))
                        P.op("vector",
                             lambda v, p=par, o=oc, ts=toks: v.tensor_mul(
                                 x2_scr[p][:], g_scr[:],
                                 sacc[:, o * T + ts.start:o * T + ts.stop]),
                             wait=w2_)
                        xifree[par] = P.op(
                            "vector",
                            lambda v, p=par: v.tensor_add(
                                x2_scr[p][:], x2_scr[p][:], xi_scr[p][:]),
                            wait=[(xsem[par], xl)], inc="sv")
                        x2vw = []
                        if stprev[par][0]:
                            x2vw.append(("st", stprev[par][0]))
                        if stprev[par][1]:
                            x2vw.append(("ss", stprev[par][1]))
                        x2v = P.op("vector",
                                   lambda v, p=par: v.tensor_copy(x2b_scr[p][:],
                                                                  x2_scr[p][:]),
                                   wait=x2vw, inc="sv")
                        x2free[par] = P.load(
                            x1_dram.ap()[oc * 128:(oc + 1) * 128, toks],
                            x2_scr[par][:], wait=[("sv", xifree[par])],
                            sem=ssem_[par])

                        def mk_lnstat(oc_=oc, par_=par, x2v_=x2v):
                            def go():
                                nonlocal stm, sqe
                                P.mm(ps_st[0:1, 0:512], ones_bf[:],
                                     x2b_scr[par_][:], start=(oc_ == 0),
                                     stop=(oc_ == 7), wait=[("sv", x2v_)])
                                sqe = P.op("scalar",
                                           lambda s, p=par_: s.activation(
                                               sq3[:], x2b_scr[p][:], AF.Square),
                                           wait=[("sv", x2v_),
                                                 ("st", stprev[1 - par_][0])],
                                           inc="ss")
                                stm = P.mm(ps_st[0:1, 512:1024], ones_bf[:],
                                           sq3[:], start=(oc_ == 0),
                                           stop=(oc_ == 7), wait=[("ss", sqe)],
                                           inc="st" if oc_ == 7 else None)
                                stprev[par_] = (stm, sqe)
                            return go
                        pend_lnstat = mk_lnstat()
                    pend_lnstat()
                    pend_lnstat = None
                    tk0 = hoff + qc * 512
                    P.op("vector",
                         lambda v, d=lnst[0:1, tk0:tk0 + 512],
                         s=ps_st[0:1, 0:512]: v.tensor_copy(d, s),
                         wait=[("st", stm)])
                    lnev = P.op("vector",
                                lambda v, d=lnst[0:1, T + tk0:T + tk0 + 512],
                                s=ps_st[0:1, 512:1024]: v.tensor_copy(d, s),
                                inc="sv")
                w2done = P.now("st")
        P.end()

    if stage == "x1":
        with ExitStack() as sd:
            bb_ = sd.enter_context(nc.sbuf_tensor("bb_", [128, 8 * T], F32))
            P.begin()
            dv = 0
            for c in range(NCHUNK):
                dv = P.load(bb_[:, c * T:(c + 1) * T],
                            x1_dram.ap()[c * 128:(c + 1) * 128, :],
                            wait=[("dsa", P.cnt["dsa"]), ("dsb", P.cnt["dsb"])])
            P.end()
            return dump(bb_[:], 8 * T, [("dl", dv)])

    with ExitStack() as s5:
        x1_scr = [s5.enter_context(nc.sbuf_tensor(f"x1l{i}", [128, T], F32))
                  for i in range(2)]
        t_scr2 = s5.enter_context(nc.sbuf_tensor("t_scr2", [128, T], F32))
        lns2 = s5.enter_context(nc.sbuf_tensor("lns2", [1, 4 * T], F32))
        P.begin()
        mu = lns2[0:1, 0:T]
        va = lns2[0:1, T:2 * T]
        bb2 = lns2[0:1, 2 * T:3 * T]
        tmp = lns2[0:1, 3 * T:4 * T]
        P.op("vector", lambda v: v.tensor_scalar(mu, lnst[0:1, 0:T], 1.0 / D,
                                                 None, ALU.mult),
             wait=[("sv", P.now("sv"))])
        P.op("vector", lambda v: v.tensor_scalar(va, lnst[0:1, T:2 * T], 1.0 / D,
                                                 1e-5, ALU.mult, ALU.add))
        P.op("vector", lambda v: v.scalar_tensor_tensor(tmp, mu, -1.0, mu,
                                                        ALU.mult, ALU.mult))
        d1 = P.op("vector", lambda v: v.tensor_add(va, va, tmp), inc="sv")
        P.op("scalar", lambda s: s.activation(va, va, AF.Ln), wait=[("sv", d1)])
        d2 = P.op("scalar", lambda s: s.activation(va, va, AF.Exp, scale=-0.5),
                  inc="ss")
        d3 = P.op("vector", lambda v: v.scalar_tensor_tensor(bb2, mu, -1.0, va,
                                                             ALU.mult, ALU.mult),
                  wait=[("ss", d2)], inc="sv")
        with ExitStack() as sp:
            ps_a = sp.enter_context(nc.psum_tensor("ps_a2", [128, T], F32))
            ps_b = sp.enter_context(nc.psum_tensor("ps_b2", [128, T], F32))
            bmm = 0
            for qc in range(4):
                sl = slice(qc * 512, (qc + 1) * 512)
                P.mm(ps_a[:, sl], ones_row_f[:], va[:, sl], True, True,
                     wait=[("sv", d3)])
                bmm = P.mm(ps_b[:, sl], ones_row_f[:], bb2[:, sl], True, True,
                           inc="st")
            apl = [0, 0]
            xs = ["dxa", "dxb"]
            for c in range(NCHUNK):
                par = c % 2
                w = [("dsa", P.cnt["dsa"]), ("dsb", P.cnt["dsb"])]
                if apl[par]:
                    w.append(("sv", apl[par]))
                lw = P.load(x1_scr[par][:],
                            x1_dram.ap()[c * 128:(c + 1) * 128, :], wait=w,
                            sem=xs[par])
                P.op("vector",
                     lambda v, p=par: v.tensor_mul(t_scr2[:], x1_scr[p][:],
                                                   ps_a[:]),
                     wait=[(xs[par], lw), ("st", bmm)])
                apl[par] = P.op("vector",
                                lambda v, d=xn_bf[:, c * T:(c + 1) * T]:
                                v.tensor_add(d, t_scr2[:], ps_b[:]), inc="sv")
        P.end()

    if stage == "xn1":
        return dump(xn_bf[:], 8 * T, [("sv", P.now("sv"))])
    sln.close()

    sat = ExitStack()
    attn_bf = sat.enter_context(nc.sbuf_tensor("attn_bf", [128, NCHUNK * T], BF16))
    with ExitStack() as s6:
        wq_sb = s6.enter_context(nc.sbuf_tensor("l1wq", [128, 8 * 128], BF16))
        wk_sb = s6.enter_context(nc.sbuf_tensor("l1wk", [128, 8 * 128], BF16))
        wv_sb = s6.enter_context(nc.sbuf_tensor("l1wv", [128, 8 * 128], BF16))
        bv_sb = s6.enter_context(nc.sbuf_tensor("l1bv", [1, 128], BF16))
        qp_sb = s6.enter_context(nc.sbuf_tensor("qp_sb", [128, T], BF16))
        kp_sb = s6.enter_context(nc.sbuf_tensor("kp_sb", [128, T], BF16))
        v_aug = s6.enter_context(nc.sbuf_tensor("v_aug", [128, 16 * 130], BF16))
        pno = [s6.enter_context(nc.sbuf_tensor(f"pno{i}", [128, 1024], BF16))
               for i in range(4)]
        att_scr = s6.enter_context(nc.sbuf_tensor("att_scr", [128, T], F32))
        rsa = s6.enter_context(nc.sbuf_tensor("rsa", [1, T], F32))
        rsb = s6.enter_context(nc.sbuf_tensor("rsb", [1, T], F32))
        rs_inv = s6.enter_context(nc.sbuf_tensor("rs_inv", [2, T], F32))

        P.begin()
        vv = v_aug[:].rearrange("p (t g) -> p t g", g=130)
        P.op("vector", lambda v: v.memset(vv[:, :, 64:65], 1.0))
        vinit = P.op("vector", lambda v: v.memset(vv[:, :, 129:130], 1.0),
                     inc="sv")
        P.end()

        rsmove = 0
        bmm_prev = 0
        norm_sv = 0
        P.begin()
        prev_sv = 0
        prev_ss = 0
        qkv_end_prev = 0
        pend_norm = None
        for pr in range(8):
            P.loadc(sb3(wq_sb, 8), col_block_ap(wqkv1, pr * 128, (pr + 1) * 128),
                    wait=[("st", qkv_end_prev)])
            P.loadc(sb3(wk_sb, 8),
                    col_block_ap(wqkv1, D + pr * 128, D + (pr + 1) * 128))
            P.loadc(sb3(wv_sb, 8),
                    col_block_ap(wqkv1, 2 * D + pr * 128, 2 * D + (pr + 1) * 128))
            P.loadc(bv_sb[:], row1(bqkv1, 2 * D + pr * 128, 2 * D + (pr + 1) * 128))
            ldg = P.now("dg")
            with ExitStack() as sp:
                ps_qk = sp.enter_context(nc.psum_tensor(f"l1ps_qk{pr}", [128, T],
                                                        F32))
                ps_v = [sp.enter_context(nc.psum_tensor(f"l1ps_v{pr}_{i}",
                                                        [128, 128], F32))
                        for i in range(2)]
                ps_bc = sp.enter_context(nc.psum_tensor(f"l1ps_bc{pr}",
                                                        [128, 1024], F32))
                vev = [vinit, vinit]

                def vtile(tt):
                    par = tt % 2
                    w = [("dg", ldg), ("sv", vev[par])]
                    for kc in range(8):
                        P.mm(ps_v[par][:],
                             xn_bf[:, kc * T + tt * 128:kc * T + (tt + 1) * 128],
                             wv_sb[:, kc * 128:(kc + 1) * 128],
                             start=(kc == 0), stop=False, wait=w)
                    vmm = P.mm(ps_v[par][:], ones_row_bf[:], bv_sb[:],
                               start=False, stop=True, inc="st")
                    vev[par] = P.op(
                        "vector",
                        lambda v, t_=tt, p=ps_v[par]: v.tensor_copy(
                            v_aug[:, t_ * 130:(t_ + 1) * 130]
                            .rearrange("p (g c) -> p g c", c=65)[:, :, 0:64],
                            p[:].rearrange("p (g c) -> p g c", c=64)),
                        wait=[("st", vmm)], inc="sv")

                qmm = 0
                for qn in range(4):
                    for kc in range(8):
                        qmm = P.mm(ps_qk[:, qn * 512:(qn + 1) * 512],
                                   wq_sb[:, kc * 128:(kc + 1) * 128],
                                   xn_bf[:, kc * T + qn * 512:kc * T + (qn + 1) * 512],
                                   start=(kc == 0), stop=(kc == 7),
                                   wait=[("dg", ldg), ("sv", prev_sv),
                                         ("ss", prev_ss)],
                                   inc="st" if kc == 7 else None)
                if pend_norm is not None:
                    pend_norm(ps_bc)
                    pend_norm = None
                eq = P.op("scalar",
                          lambda s, b=bqkv1_sb[:, pr:pr + 1], p=ps_qk:
                          s.activation(qp_sb[:], p[:], AF.Identity, bias=b),
                          wait=[("st", qmm)], inc="ss")
                for tt in range(0, 4):
                    vtile(tt)
                kmm = 0
                for qn in range(4):
                    for kc in range(8):
                        kmm = P.mm(ps_qk[:, qn * 512:(qn + 1) * 512],
                                   wk_sb[:, kc * 128:(kc + 1) * 128],
                                   xn_bf[:, kc * T + qn * 512:kc * T + (qn + 1) * 512],
                                   start=(kc == 0), stop=(kc == 7),
                                   wait=[("ss", eq)],
                                   inc="st" if kc == 7 else None)
                ek = P.op("scalar",
                          lambda s, b=bqkv1_sb[:, 8 + pr:9 + pr], p=ps_qk:
                          s.activation(kp_sb[:], p[:], AF.Identity, bias=b),
                          wait=[("st", kmm)], inc="ss")
                for tt in range(4, 16):
                    vtile(tt)
                qkv_end_prev = P.now("st")
            with ExitStack() as sp:
                ps_s = [sp.enter_context(nc.psum_tensor(f"l1ps_s{pr}_{i}",
                                                        [128, 1024], F32))
                        for i in range(2)]
                ps_av = [sp.enter_context(nc.psum_tensor(f"l1ps_av{pr}_{i}",
                                                         [65, 1024], F32))
                         for i in range(2)]
                vall = P.now("sv")
                ekk = P.now("ss")
                exd = [[0, 0], [0, 0]]
                avm = [[0, 0], [0, 0]]
                evdone = 0
                for qh in range(2):
                    q0h = qh * 1024
                    pend = []
                    for tk in range(16):
                        par = tk % 2
                        for hb in range(2):
                            hb0 = hb * 64
                            w = [("ss", ekk), ("sv", vall)]
                            if exd[hb][1 - par]:
                                w.append(("ss", exd[hb][1 - par]))
                            smm = 0
                            for qn2 in range(2):
                                smm = P.mm(
                                    ps_s[hb][:, qn2 * 512:(qn2 + 1) * 512],
                                    kp_sb[hb0:hb0 + 64,
                                          tk * 128:(tk + 1) * 128],
                                    qp_sb[hb0:hb0 + 64,
                                          q0h + qn2 * 512:
                                          q0h + (qn2 + 1) * 512],
                                    True, True, wait=w,
                                    inc="st" if qn2 == 1 else None)
                            we = [("st", smm)]
                            if avm[hb][par]:
                                we.append(("st", avm[hb][par]))
                            exd[hb][par] = P.op(
                                "scalar",
                                lambda s, d=pno[2 * hb + par], p=ps_s[hb]:
                                s.activation(d[:], p[:], AF.Exp, scale=0.125,
                                             bias=neg10_sb[:, 0:1]),
                                wait=we, inc="ss")
                        for (hb_, tk_) in pend:
                            par_ = tk_ % 2
                            wa = [("ss", exd[hb_][par_])]
                            if tk_ == 0 and evdone:
                                wa.append(("sv", evdone))
                            for qn2 in range(2):
                                avm[hb_][par_] = P.mm(
                                    ps_av[hb_][:, qn2 * 512:(qn2 + 1) * 512],
                                    v_aug[:, tk_ * 130 + hb_ * 65:
                                          tk_ * 130 + (hb_ + 1) * 65],
                                    pno[2 * hb_ + par_][:, qn2 * 512:
                                                        (qn2 + 1) * 512],
                                    start=(tk_ == 0), stop=(tk_ == 15),
                                    wait=wa,
                                    inc="st" if qn2 == 1 else None)
                        pend = [(0, tk), (1, tk)]
                    for (hb_, tk_) in pend:
                        par_ = tk_ % 2
                        for qn2 in range(2):
                            avm[hb_][par_] = P.mm(
                                ps_av[hb_][:, qn2 * 512:(qn2 + 1) * 512],
                                v_aug[:, tk_ * 130 + hb_ * 65:
                                      tk_ * 130 + (hb_ + 1) * 65],
                                pno[2 * hb_ + par_][:, qn2 * 512:
                                                    (qn2 + 1) * 512],
                                start=(tk_ == 0), stop=(tk_ == 15),
                                wait=[("ss", exd[hb_][par_])],
                                inc="st" if qn2 == 1 else None)
                    for hb in range(2):
                        hb0 = hb * 64
                        aw = [("st", avm[hb][1]), ("dg", rsmove)]
                        if norm_sv:
                            aw.append(("sv", norm_sv))
                        ae = P.op("vector",
                                  lambda v, d=att_scr[hb0:hb0 + 64,
                                                      q0h:q0h + 1024],
                                  s=ps_av[hb][0:64, :]: v.tensor_copy(d, s),
                                  wait=aw)
                        dst_rs = (rsa if hb == 0 else rsb)[0:1, q0h:q0h + 1024]
                        evdone = P.op("vector",
                                      lambda v, d=dst_rs, s=ps_av[hb][64:65, :]:
                                      v.tensor_copy(d, s), inc="sv")
                        P.op("scalar",
                             lambda s, d=dst_rs: s.activation(d, d, AF.Ln),
                             wait=[("sv", evdone)])
                        rcpd = P.op("scalar",
                                    lambda s, d=dst_rs: s.activation(
                                        d, d, AF.Exp, scale=-1.0),
                                    inc="ss")
                m1w = [("sv", evdone), ("ss", rcpd)]
                if bmm_prev:
                    m1w.append(("st", bmm_prev))
                P.op("gpsimd", lambda e: e.dma_start(rs_inv[0:1, :], rsa[0:1, :]),
                     wait=m1w, inc="dg", inc_n=16)
                rsmove = P.op("gpsimd",
                              lambda e: e.dma_start(rs_inv[1:2, :], rsb[0:1, :]),
                              inc="dg", inc_n=16)
                prev_sv = evdone
                prev_ss = P.now("ss")

            def mk_norm(pr_=pr, rsm=rsmove):
                def go(ps_bc):
                    nonlocal bmm_prev, norm_sv
                    bmm = 0
                    for half2 in range(2):
                        w = [("dg", rsm)]
                        if norm_sv:
                            w.append(("sv", norm_sv))
                        for qn in range(2):
                            c0 = half2 * 1024 + qn * 512
                            bmm = P.mm(ps_bc[:, qn * 512:(qn + 1) * 512],
                                       selab_sb[:], rs_inv[:, c0:c0 + 512],
                                       True, True, wait=w, inc="st")
                        norm_sv = P.op(
                            "vector",
                            lambda v,
                            d=attn_bf[:, pr_ * T + half2 * 1024:
                                      pr_ * T + (half2 + 1) * 1024],
                            s=att_scr[:, half2 * 1024:(half2 + 1) * 1024],
                            p=ps_bc: v.tensor_mul(d, s, p[:, 0:1024]),
                            wait=[("st", bmm)], inc="sv")
                    bmm_prev = bmm
                return go
            pend_norm = mk_norm()
        with ExitStack() as spf:
            ps_bcf = spf.enter_context(nc.psum_tensor("l1ps_bcf", [128, 1024],
                                                      F32))
            pend_norm(ps_bcf)
        P.end()

    if stage == "attn1":
        return dump(attn_bf[:], 8 * T, [("sv", P.now("sv"))])

    with ExitStack() as s7:
        wo_sb = [s7.enter_context(nc.sbuf_tensor(f"wo{i}", [128, 8 * 128], BF16))
                 for i in range(2)]
        bo_sb = s7.enter_context(nc.sbuf_tensor("bo_sb", [128, 8], F32))
        P.begin()
        P.load(bo_sb[:], bout1_pg.ap())
        bow = P.now("dl")
        attdone = P.now("sv")
        with ExitStack() as sp:
            ps_wo = [sp.enter_context(nc.psum_tensor(f"ps_wo{i}", [128, 512],
                                                     F32)) for i in range(2)]
            wsem = ["dwa", "dwb"]
            wl = [P.loadc(sb3(wo_sb[i], 8), col_block_ap(wout1, i * 128,
                                                         (i + 1) * 128),
                          wait=[("st", P.now("st"))], sem=wsem[i]) for i in range(2)]
            oev = [0, 0]
            for oc in range(8):
                bpar = oc % 2
                for qc in range(4):
                    par = (oc * 4 + qc) % 2
                    w = [(wsem[bpar], wl[bpar]), ("sv", attdone)]
                    if oev[par]:
                        w.append(("ss", oev[par]))
                    pm = 0
                    for kc in range(8):
                        pm = P.mm(ps_wo[par][:],
                                  wo_sb[bpar][:, kc * 128:(kc + 1) * 128],
                                  attn_bf[:, kc * T + qc * 512:
                                          kc * T + (qc + 1) * 512],
                                  start=(kc == 0), stop=(kc == 7), wait=w,
                                  inc="st" if kc == 7 else None)
                    oev[par] = P.op(
                        "scalar",
                        lambda s, d=sacc[:, oc * T + qc * 512:
                                         oc * T + (qc + 1) * 512],
                        p=ps_wo[par], o=oc: s.activation(
                            d, p[:], AF.Identity, bias=bo_sb[:, o:o + 1]),
                        wait=[("st", pm), ("dl", bow)], inc="ss")
                if oc + 2 < 8:
                    wl[bpar] = P.loadc(sb3(wo_sb[bpar], 8),
                                       col_block_ap(wout1, (oc + 2) * 128,
                                                    (oc + 3) * 128),
                                       wait=[("st", P.now("st"))], sem=wsem[bpar])
        P.end()

    if stage == "s1":
        return dump(sacc[:], 8 * T, [("ss", P.now("ss"))])
    sat.close()

    with ExitStack() as s8:
        cw1 = s8.enter_context(nc.sbuf_tensor("cw1", [128, 8 * 1024], BF16))
        cw2 = s8.enter_context(nc.sbuf_tensor("cw2", [128, 8 * 512], BF16))
        cw3 = s8.enter_context(nc.sbuf_tensor("cw3", [128, 4 * 256], BF16))
        cw4 = s8.enter_context(nc.sbuf_tensor("cw4", [128, 2], BF16))
        cb1 = s8.enter_context(nc.sbuf_tensor("cb1", [128, 8], F32))
        cb2 = s8.enter_context(nc.sbuf_tensor("cb2", [128, 4], F32))
        cb3 = s8.enter_context(nc.sbuf_tensor("cb3", [128, 2], F32))
        sc_sb = s8.enter_context(nc.sbuf_tensor("sc_sb", [1, 16], F32))
        gcv = s8.enter_context(nc.sbuf_tensor("gcv", [128, 16], BF16))
        gcs = s8.enter_context(nc.sbuf_tensor("gcs", [128, 8], F32))
        out_q = s8.enter_context(nc.sbuf_tensor("out_q", [128, 8 * 512], F32))
        outb_q = s8.enter_context(nc.sbuf_tensor("outb_q", [128, 8 * 512], BF16))
        h1b = s8.enter_context(nc.sbuf_tensor("h1b", [128, 8 * 512], BF16))
        h2b = s8.enter_context(nc.sbuf_tensor("h2b", [128, 4 * 512], BF16))
        h3b = s8.enter_context(nc.sbuf_tensor("h3b", [128, 2 * 512], BF16))
        th4 = s8.enter_context(nc.sbuf_tensor("th4", [1, 512], F32))
        cpv = s8.enter_context(nc.sbuf_tensor("cpv", [1, 512], F32))
        tmpc = s8.enter_context(nc.sbuf_tensor("tmpc", [128, 512], F32))
        tmc2 = s8.enter_context(nc.sbuf_tensor("tmc2", [128, 512], F32))
        gf_scr = [s8.enter_context(nc.sbuf_tensor(f"gfs{i}", [128, 512], F32))
                  for i in range(2)]
        fm_scr = s8.enter_context(nc.sbuf_tensor("fm_scr", [128, 512], F32))
        xf_all = s8.enter_context(nc.sbuf_tensor("xf_all", [128, 8 * 512], F32))
        s8g = ExitStack()
        gw1 = s8g.enter_context(nc.sbuf_tensor("gw1", [128, 8 * 512], BF16))
        gw2 = s8g.enter_context(nc.sbuf_tensor("gw2", [128, 4 * 256], BF16))
        gw3 = s8g.enter_context(nc.sbuf_tensor("gw3", [128, 2], BF16))
        gb1 = s8g.enter_context(nc.sbuf_tensor("gb1", [128, 4], F32))
        gb2 = s8g.enter_context(nc.sbuf_tensor("gb2", [128, 2], F32))

        P.begin()
        P.loadc(sb3(cw1, 8), cnw[0].ap().rearrange("(kc p) m -> p kc m", p=128))
        P.loadc(sb3(cw2, 8), cnw[1].ap().rearrange("(kc p) m -> p kc m", p=128))
        P.loadc(sb3(cw3, 4), cnw[2].ap().rearrange("(kc p) m -> p kc m", p=128))
        P.loadc(cw4[:], cnw4_pg.ap())
        P.loadc(sb3(gw1, 8), gcw[0].ap().rearrange("(kc p) m -> p kc m", p=128))
        P.loadc(sb3(gw2, 4), gcw[1].ap().rearrange("(kc p) m -> p kc m", p=128))
        P.loadc(gw3[:], gcw3_pg.ap())
        wload = P.now("dg")
        P.load(cb1[:], cnb_pg[0].ap())
        P.load(cb2[:], cnb_pg[1].ap())
        P.load(cb3[:], cnb_pg[2].ap())
        P.load(sc_sb[0:1, 6:7], row1(cnb[3], 0, 1))
        P.load(gb1[:], gcb_pg[0].ap())
        P.load(gb2[:], gcb_pg[1].ap())
        P.load(sc_sb[0:1, 7:8], row1(gcb[2], 0, 1))
        P.load(sc_sb[0:1, 2:3], row1(arange_in, 0, 1))
        bload = P.now("dl")
        P.vseq(lambda v: v.tensor_scalar(sc_sb[0:1, 0:1], sc_sb[0:1, 6:7],
                                         0.5, None, ALU.mult),
               wait=[("dl", bload)])
        sc1 = P.vseq(lambda v: v.tensor_scalar(sc_sb[0:1, 1:2],
                                               sc_sb[0:1, 7:8], 0.5, None,
                                               ALU.mult))
        for c in range(NCHUNK):
            P.op("vector",
                 lambda v, d=gcs[:, c:c + 1], s=xn_bf[:, c * T:(c + 1) * T]:
                 v.reduce_sum(d, s, axis=AX), inc="sv")
        gsum = P.vseq(lambda v: v.tensor_copy(gcv[:, 0:8], gcs[:]))
        with ExitStack() as sp:
            ps_gn = sp.enter_context(nc.psum_tensor("ps_gfn", [128, 8], F32))
            ps_gf = sp.enter_context(nc.psum_tensor("ps_gf", [1, 2], F32))
            gmm = 0
            for m in range(4):
                for kc in range(8):
                    gmm = P.mm(ps_gn[:, m:m + 1],
                               gw1[:, kc * 512 + m * 128:kc * 512 + (m + 1) * 128],
                               gcv[:, kc:kc + 1], start=(kc == 0), stop=(kc == 7),
                               wait=[("dg", wload), ("sv", gsum)],
                               inc="st" if kc == 7 else None)
            g1e = 0
            for m in range(4):
                g1e = P.op("scalar",
                           lambda s, m_=m: s.activation(
                               gcv[:, 8 + m_:9 + m_], ps_gn[:, m_:m_ + 1],
                               AF.Gelu, bias=gb1[:, m_:m_ + 1], scale=1.0 / T),
                           wait=[("st", gmm), ("dl", bload)], inc="ss")
            gm2 = 0
            for m in range(2):
                for kc in range(4):
                    gm2 = P.mm(ps_gn[:, 4 + m:5 + m],
                               gw2[:, kc * 256 + m * 128:kc * 256 + (m + 1) * 128],
                               gcv[:, 8 + kc:9 + kc], start=(kc == 0),
                               stop=(kc == 3),
                               wait=[("ss", g1e)],
                               inc="st" if kc == 3 else None)
            g2e = 0
            for m in range(2):
                g2e = P.op("scalar",
                           lambda s, m_=m: s.activation(
                               gcv[:, 12 + m_:13 + m_], ps_gn[:, 4 + m_:5 + m_],
                               AF.Gelu, bias=gb2[:, m_:m_ + 1]),
                           wait=[("st", gm2)], inc="ss")
            gm3 = 0
            for kc in range(2):
                gm3 = P.mm(ps_gf[0:1, 0:1], gw3[:, kc:kc + 1],
                           gcv[:, 12 + kc:13 + kc], start=(kc == 0),
                           stop=(kc == 1),
                           wait=[("ss", g2e)], inc="st" if kc == 1 else None)
            sgf = P.op("vector", lambda v: v.tensor_copy(sc_sb[0:1, 8:9],
                                                         ps_gf[0:1, 0:1]),
                       wait=[("st", gm3)], inc="sv")
            tgf = P.op("scalar", lambda s: s.activation(sc_sb[0:1, 3:4],
                                                        sc_sb[0:1, 8:9],
                                                        AF.Tanh, scale=0.5,
                                                        bias=sc_sb[0:1, 1:2]),
                       wait=[("sv", sgf), ("sv", sc1)], inc="ss")
        P.vseq(lambda v: v.tensor_scalar(sc_sb[0:1, 3:4], sc_sb[0:1, 3:4],
                                         0.5, 0.5, ALU.mult, ALU.add),
               wait=[("ss", tgf)])
        P.vseq(lambda v: v.tensor_scalar(sc_sb[0:1, 8:9], sc_sb[0:1, 3:4],
                                         1.4, None, ALU.mult))
        P.vseq(lambda v: v.tensor_mul(sc_sb[0:1, 8:9], sc_sb[0:1, 8:9],
                                      sc_sb[0:1, 2:3]))
        P.vseq(lambda v: v.tensor_scalar(sc_sb[0:1, 9:10], sc_sb[0:1, 2:3],
                                         -0.7, 0.1, ALU.mult, ALU.add))
        P.vseq(lambda v: v.tensor_add(sc_sb[0:1, 4:5], sc_sb[0:1, 8:9],
                                      sc_sb[0:1, 9:10]))
        cprep = P.vseq(lambda v: v.tensor_scalar(sc_sb[0:1, 5:6],
                                                 sc_sb[0:1, 2:3], 0.3, None,
                                                 ALU.mult))
        P.end()

        if stage == "gfs":
            return dump(sc_sb[0:1, :], 16, [("sv", cprep)])

        s8g.close()
        tw_sgn = [1, 1, -1, -1, -1, -1, 1, 1]
        prev_q_done = 0
        osem = ["dsa", "dsb"]
        ostore = [P.cnt["dsa"], P.cnt["dsb"]]
        xf_w = [0, 0, 0, 0]
        prev_fin = 0
        P.begin()
        for Q in range(4):
            qoff = Q * 512
            for c in range(NCHUNK):
                xf_w[Q] = P.load(xf_all[:, c * 512:(c + 1) * 512],
                                 x1_dram.ap()[c * 128:(c + 1) * 128,
                                              qoff:qoff + 512],
                                 wait=[("ss", prev_q_done), ("sv", prev_fin)],
                                 sem="dxa" if Q % 2 == 0 else "dxb")
            inits = 0
            for c in range(NCHUNK):
                w = [("sv", cprep)]
                if prev_q_done:
                    w.append(("ss", prev_q_done))
                inits = P.op("vector",
                             lambda v, d=out_q[:, c * 512:(c + 1) * 512],
                             s=xn_bf[:, c * T + qoff:c * T + qoff + 512]:
                             v.tensor_copy(d, s), wait=w, inc="sv")
            with ExitStack() as sp:
                ps_c = [sp.enter_context(nc.psum_tensor(f"ps_c{Q}{i}", [128, 512],
                                                        F32)) for i in range(2)]
                ps_c4 = sp.enter_context(nc.psum_tensor(f"ps_c4{Q}", [1, 512],
                                                        F32))
                ps_cp = sp.enter_context(nc.psum_tensor(f"ps_cp{Q}", [128, 512],
                                                        F32))
                tws = inits
                psc_free = [0, 0]
                for cyc in range(3):
                    def rhs1(kc):
                        if cyc == 0:
                            return xn_bf[:, kc * T + qoff:kc * T + qoff + 512]
                        return outb_q[:, kc * 512:(kc + 1) * 512]
                    h1e = 0
                    for mc in range(8):
                        par = mc % 2
                        w = [("dg", wload), ("dl", bload), ("sv", tws),
                             ("ss", psc_free[par])]
                        m1 = 0
                        for kc in range(8):
                            m1 = P.mm(ps_c[par][:],
                                      cw1[:, kc * 1024 + mc * 128:
                                          kc * 1024 + (mc + 1) * 128],
                                      rhs1(kc), start=(kc == 0), stop=(kc == 7),
                                      wait=w, inc="st" if kc == 7 else None)
                        h1e = psc_free[par] = P.op(
                            "scalar",
                            lambda s, d=h1b[:, mc * 512:(mc + 1) * 512],
                            p=ps_c[par], m=mc: s.activation(
                                d, p[:], AF.Gelu, bias=cb1[:, m:m + 1]),
                            wait=[("st", m1)], inc="ss")
                    h2e = 0
                    for mc in range(4):
                        par = mc % 2
                        m2 = 0
                        for kc in range(8):
                            m2 = P.mm(ps_c[par][:],
                                      cw2[:, kc * 512 + mc * 128:
                                          kc * 512 + (mc + 1) * 128],
                                      h1b[:, kc * 512:(kc + 1) * 512],
                                      start=(kc == 0), stop=(kc == 7),
                                      wait=[("ss", h1e),
                                            ("ss", psc_free[par])],
                                      inc="st" if kc == 7 else None)
                        h2e = psc_free[par] = P.op(
                            "scalar",
                            lambda s, d=h2b[:, mc * 512:(mc + 1) * 512],
                            p=ps_c[par], m=mc: s.activation(
                                d, p[:], AF.Gelu, bias=cb2[:, m:m + 1]),
                            wait=[("st", m2)], inc="ss")
                    h3e = 0
                    for mc in range(2):
                        par = mc % 2
                        m3 = 0
                        for kc in range(4):
                            m3 = P.mm(ps_c[par][:],
                                      cw3[:, kc * 256 + mc * 128:
                                          kc * 256 + (mc + 1) * 128],
                                      h2b[:, kc * 512:(kc + 1) * 512],
                                      start=(kc == 0), stop=(kc == 3),
                                      wait=[("ss", h2e),
                                            ("ss", psc_free[par])],
                                      inc="st" if kc == 3 else None)
                        h3e = psc_free[par] = P.op(
                            "scalar",
                            lambda s, d=h3b[:, mc * 512:(mc + 1) * 512],
                            p=ps_c[par], m=mc: s.activation(
                                d, p[:], AF.Gelu, bias=cb3[:, m:m + 1]),
                            wait=[("st", m3)], inc="ss")
                    m4 = 0
                    for kc in range(2):
                        m4 = P.mm(ps_c4[0:1, :], cw4[:, kc:kc + 1],
                                  h3b[:, kc * 512:(kc + 1) * 512],
                                  start=(kc == 0), stop=(kc == 1),
                                  wait=[("ss", h3e)],
                                  inc="st" if kc == 1 else None)
                    t4e = P.op("scalar",
                               lambda s: s.activation(th4[:], ps_c4[:], AF.Tanh,
                                                      scale=0.5,
                                                      bias=sc_sb[0:1, 0:1]),
                               wait=[("st", m4)], inc="ss")
                    cpe = P.op("vector",
                               lambda v: v.tensor_scalar(cpv[:], th4[:],
                                                         sc_sb[0:1, 5:6],
                                                         sc_sb[0:1, 4:5],
                                                         ALU.mult, ALU.add),
                               wait=[("ss", t4e)], inc="sv")
                    cbm = P.mm(ps_cp[:], ones_row_f[:], cpv[:], True, True,
                               wait=[("sv", cpe)], inc="st")
                    for c in range(4):
                        c2 = c + 4
                        oc1 = out_q[:, c * 512:(c + 1) * 512]
                        oc2 = out_q[:, c2 * 512:(c2 + 1) * 512]
                        P.op("vector", lambda v, a=oc1: v.tensor_copy(tmpc[:], a),
                             wait=[("st", cbm)])
                        P.op("vector", lambda v, a=oc2: v.tensor_mul(tmc2[:],
                                                                     ps_cp[:],
                                                                     a))
                        if tw_sgn[c] > 0:
                            P.op("vector",
                                 lambda v, a=oc1: v.tensor_add(a, a, tmc2[:]))
                        else:
                            P.op("vector",
                                 lambda v, a=oc1: v.tensor_sub(a, a, tmc2[:]))
                        P.op("vector", lambda v: v.tensor_mul(tmc2[:], ps_cp[:],
                                                              tmpc[:]))
                        if tw_sgn[c2] > 0:
                            tws = P.op("vector",
                                       lambda v, a=oc2: v.tensor_add(a, a,
                                                                     tmc2[:]),
                                       inc="sv")
                        else:
                            tws = P.op("vector",
                                       lambda v, a=oc2: v.tensor_sub(a, a,
                                                                     tmc2[:]),
                                       inc="sv")
                    if cyc < 2:
                        for c in range(NCHUNK):
                            tws = P.op(
                                "vector",
                                lambda v, d=outb_q[:, c * 512:(c + 1) * 512],
                                s=out_q[:, c * 512:(c + 1) * 512]:
                                v.tensor_copy(d, s),
                                wait=[("sv", tws)], inc="sv")
                if stage == "cyc0" and Q == 0:
                    return dump(out_q[:], 8 * 512, [("sv", tws)])
                xsem = "dxa" if Q % 2 == 0 else "dxb"
                for c in range(NCHUNK):
                    par = c % 2
                    w = [("sv", tws), (xsem, xf_w[Q])]
                    if ostore[par]:
                        w.append((osem[par], ostore[par]))
                    sge = prev_q_done = P.op(
                        "scalar",
                        lambda s, p=par, s_=out_q[:, c * 512:(c + 1) * 512]:
                        s.activation(gf_scr[p][:], s_, AF.Silu),
                        wait=w, inc="ss")
                    P.op("vector",
                         lambda v, p=par, c_=c, qo=qoff: v.tensor_mul(
                             fm_scr[:], gf_scr[p][:],
                             sacc[:, c_ * T + qo:c_ * T + qo + 512]),
                         wait=[("ss", sge)])
                    fin = P.op("vector",
                               lambda v, p=par, c_=c: v.tensor_add(
                                   gf_scr[p][:], fm_scr[:],
                                   xf_all[:, c_ * 512:(c_ + 1) * 512]),
                               inc="sv")
                    prev_fin = fin
                    ostore[par] = P.load(
                        out_d.ap()[c * 128:(c + 1) * 128, qoff:qoff + 512],
                        gf_scr[par][:], wait=[("sv", fin)], sem=osem[par])
        P.end()

    P.begin()
    P.op("sync", wait=[("dsa", P.cnt["dsa"]), ("dsb", P.cnt["dsb"]),
                       ("dl", P.now("dl")), ("dg", P.now("dg"))])
    P.end()
    P.stack.close()
    return nc


def _in_maps(inputs):
    C = _host_constants()
    x = np.ascontiguousarray(np.asarray(inputs["L0_attn_hidden_states"],
                                        dtype=np.float32))
    wqkv1p = np.ascontiguousarray(
        np.asarray(inputs["L1_attn_Wqkv"], dtype=np.float32)[:, C["l1perm"]])
    bqkv1p = np.ascontiguousarray(
        np.asarray(inputs["L1_attn_bqkv"], dtype=np.float32)[C["l1perm"]])
    common = {}
    for k in ("L0_attn_w_qkv", "L0_attn_b_qkv", "L0_attn_w_g", "L0_attn_b_g",
              "L0_attn_w_p", "L0_attn_b_p", "L0_ffn_W1", "L0_ffn_b1",
              "L0_ffn_quantum_phases", "L0_ffn_quantum_weights_real",
              "L0_ffn_quantum_weights_imag", "L0_ffn_W2", "L0_ffn_b2",
              "L1_attn_Wout", "L1_attn_bout",
              "L1_ffn_cn_w1", "L1_ffn_cn_b1", "L1_ffn_cn_w2", "L1_ffn_cn_b2",
              "L1_ffn_cn_w3", "L1_ffn_cn_b3", "L1_ffn_cn_w4", "L1_ffn_cn_b4",
              "L1_ffn_gc_w1", "L1_ffn_gc_b1", "L1_ffn_gc_w2", "L1_ffn_gc_b2",
              "L1_ffn_gc_w3", "L1_ffn_gc_b3", "L1_ffn_adaptive_range"):
        v = np.asarray(inputs[k], dtype=np.float32)
        common[k] = np.ascontiguousarray(v.reshape(v.shape if v.ndim > 0 else (1,)))
    common["L1_ffn_cn_w4"] = common["L1_ffn_cn_w4"].reshape(256, 1)
    common["L1_ffn_gc_w3"] = common["L1_ffn_gc_w3"].reshape(256, 1)
    common["L1_ffn_cn_b4"] = common["L1_ffn_cn_b4"].reshape(1)
    common["L1_ffn_gc_b3"] = common["L1_ffn_gc_b3"].reshape(1)
    common["L1_ffn_adaptive_range"] = common["L1_ffn_adaptive_range"].reshape(1)
    common["l1_wqkv_perm"] = wqkv1p
    common["l1_bqkv_perm"] = bqkv1p

    def pg(v, g):
        return np.ascontiguousarray(np.asarray(v, dtype=np.float32)
                                    .reshape(g, 128).T)
    common["b_qkv0_pg"] = pg(inputs["L0_attn_b_qkv"], 32)
    common["bg0_pg"] = pg(inputs["L0_attn_b_g"], 16)
    common["bp0_pg"] = pg(inputs["L0_attn_b_p"], 8)
    common["b1_pg"] = pg(inputs["L0_ffn_b1"], 32)
    common["b2_pg"] = pg(inputs["L0_ffn_b2"], 8)
    common["qph_pg"] = pg(inputs["L0_ffn_quantum_phases"], 32)
    common["qwr_pg"] = pg(inputs["L0_ffn_quantum_weights_real"], 32)
    common["qwi_pg"] = pg(inputs["L0_ffn_quantum_weights_imag"], 32)
    common["b_qkv1_pg"] = pg(bqkv1p, 24)
    common["bout1_pg"] = pg(inputs["L1_attn_bout"], 8)
    common["cnb1_pg"] = pg(inputs["L1_ffn_cn_b1"], 8)
    common["cnb2_pg"] = pg(inputs["L1_ffn_cn_b2"], 4)
    common["cnb3_pg"] = pg(inputs["L1_ffn_cn_b3"], 2)
    common["gcb1_pg"] = pg(inputs["L1_ffn_gc_b1"], 4)
    common["gcb2_pg"] = pg(inputs["L1_ffn_gc_b2"], 2)
    common["cnw4_pg"] = pg(inputs["L1_ffn_cn_w4"], 2)
    common["gcw3_pg"] = pg(inputs["L1_ffn_gc_w3"], 2)
    for k in ("tab_aq", "tab_bq", "tab_ak", "tab_bk", "permT", "permT_f32",
              "trimask", "selab", "sel16"):
        common[k] = C[k]
    maps = []
    for i in range(8):
        m = dict(common)
        m["x_fmaj"] = np.ascontiguousarray(x[i].T)
        maps.append(m)
    return maps


def kernel(**inputs):
    nc = build_nc("full")
    maps = _in_maps(inputs)
    res = run_bass_kernel_spmd(nc, maps, core_ids=list(range(8)))
    return np.stack([np.asarray(r["out_fmaj"], dtype=np.float32).T
                     for r in res.results], axis=0)
